# version 11
# baseline (speedup 1.0000x reference)
import numpy as np
import concourse.bass as bass
import concourse.mybir as mybir
from concourse.bass_utils import run_bass_kernel_spmd

F32 = mybir.dt.float32
BF16 = mybir.dt.bfloat16
AF = mybir.ActivationFunctionType
ALU = mybir.AluOpType

NEG_BIG = -1.0e30
EXP_SHIFT = -12.0
LN_EPS = 1e-5
RG4 = [[0, 1, 2, 3], [4, 5, 6, 7]]
QNAMES = ("tensor", "vector", "scalar", "sync", "gpsimd")


def build_nc(D, S, HPC, upto=None, check=True):
    IT = D // 128
    NQB = S // 512
    TPC = S // 4
    NQT = TPC // 128
    NKT = S // 128
    FPC = HPC * 128
    NDT = FPC // 128
    NCH = (FPC + 511) // 512
    NOC = D // 512
    MYQB = TPC // 512
    SCALE = float(1.0 / np.sqrt(128.0))
    assert NQB <= 8 and 1 <= MYQB <= 8

    nc = bass.Bass("TRN2", target_bir_lowering=False, detect_race_conditions=False)

    xT = nc.dram_tensor("xT", [D, S], F32, kind="ExternalInput").ap()
    xTmy = nc.dram_tensor("xTmy", [D, TPC], F32, kind="ExternalInput").ap()
    x_tok = nc.dram_tensor("x_tok", [TPC, D], F32, kind="ExternalInput").ap()
    wqT = nc.dram_tensor("wqT", [D, FPC], F32, kind="ExternalInput").ap()
    wkT = nc.dram_tensor("wkT", [D, FPC], F32, kind="ExternalInput").ap()
    wvT = nc.dram_tensor("wvT", [D, FPC], F32, kind="ExternalInput").ap()
    woTmy = nc.dram_tensor("woTmy", [FPC, D], F32, kind="ExternalInput").ap()
    wfT = nc.dram_tensor("wfT", [D, D], F32, kind="ExternalInput").ap()
    gam_in = nc.dram_tensor("gam_in", [1, D], F32, kind="ExternalInput").ap()
    bet_in = nc.dram_tensor("bet_in", [1, D], F32, kind="ExternalInput").ap()
    ones128 = nc.dram_tensor("ones128", [1, 128], F32, kind="ExternalInput").ap()
    onesc = nc.dram_tensor("onesc", [1, 1], F32, kind="ExternalInput").ap()
    onesb = nc.dram_tensor("onesb", [128, 1], F32, kind="ExternalInput").ap()
    expb = nc.dram_tensor("expb", [128, 1], F32, kind="ExternalInput").ap()
    qsel = nc.dram_tensor("qsel", [128, 64], F32, kind="ExternalInput").ap()
    lsel = nc.dram_tensor("lsel", [128, HPC * 4], F32, kind="ExternalInput").ap()
    maskc = nc.dram_tensor("maskc", [4 * 128, 512], F32, kind="ExternalInput").ap()
    out = nc.dram_tensor("out", [TPC, D], F32, kind="ExternalOutput").ap()

    dbg = (nc.dram_tensor("dbg", [128, max(3 * HPC * S, NQT * D)], F32,
                          kind="ExternalOutput").ap() if upto else None)
    dbg2 = (nc.dram_tensor("dbg2", [TPC, D], F32, kind="ExternalOutput").ap()
            if upto == "p3s" else None)
    o_dram = nc.dram_tensor("o_dram", [FPC, S], BF16)
    sp = [nc.dram_tensor(f"s_part{c}", [S, 512], BF16) for c in range(NOC)]
    sm = [nc.dram_tensor(f"s_my{c}", [TPC, 512], BF16) for c in range(NOC)]

    xT_t = xT.rearrange("(i p) s -> p i s", p=128)
    xTmy_t = xTmy.rearrange("(i p) s -> p i s", p=128)
    wqT_t = wqT.rearrange("(i p) f -> p i f", p=128)
    wkT_t = wkT.rearrange("(i p) f -> p i f", p=128)
    wvT_t = wvT.rearrange("(i p) f -> p i f", p=128)
    woTmy_t = woTmy.rearrange("(i p) f -> p i f", p=128)
    wfT_t = wfT.rearrange("(i p) f -> p i f", p=128)
    maskc_t = maskc.rearrange("(t p) j -> p t j", p=128)
    o_dram_t = o_dram.ap().rearrange("(h p) s -> p h s", p=128)

    plan = {n: [] for n in QNAMES}
    COMPUTE_SEMS = ("mm", "vv", "ss", "cc")
    DMA_STREAMS = ("sc", "sgb", "so0", "so1", "sosh", "spe0", "spe1", "spe2", "spe3",
                   "sst0", "sst1", "sxt0", "sxt1", "sou0", "sou1",
                   "gc", "gx0", "gx1", "gw", "gwc0", "gwc1", "gwc2", "gwc3", "gf",
                   "gil", "gm", "gq", "gd")
    cnt = {k: 0 for k in COMPUTE_SEMS + DMA_STREAMS}
    SEM = {}
    guards = {}
    nodes = []
    sem_hist = {k: [] for k in cnt}

    def emit(q, fn, wait=None, inc=None, rd=(), wr=()):
        w = tuple((s, v) for (s, v) in (wait or ()) if v > 0)
        amt = 0
        if inc is not None:
            amt = 1 if inc in COMPUTE_SEMS else 16
            cnt[inc] += amt
        nid = len(nodes)
        nodes.append((nid, q, w, tuple(rd), tuple(wr)))
        if inc is not None:
            sem_hist[inc].append((cnt[inc], nid))

        def go(eng, fn=fn, w=w, inc=inc, amt=amt):
            for sname, val in w:
                eng.wait_ge(SEM[sname], val)
            r = fn(eng)
            if inc is not None:
                r.then_inc(SEM[inc], amt)
        plan[q].append(go)
        return cnt[inc] if inc is not None else None

    def check_races(max_report=40):
        qi = {n: i for i, n in enumerate(QNAMES)}

        def wait_node(s, v):
            lo, hi = 0, len(sem_hist[s])
            h = sem_hist[s]
            while lo < hi:
                mid = (lo + hi) // 2
                if h[mid][0] >= v:
                    hi = mid
                else:
                    lo = mid + 1
            return h[lo][1] if lo < len(h) else None

        pos = {}
        counters = {n: 0 for n in QNAMES}
        for nid, q, w, rd_, wr_ in nodes:
            pos[nid] = counters[q]
            counters[q] += 1
        NQ = len(QNAMES)
        vcs = [None] * len(nodes)
        last = {n: None for n in QNAMES}
        for nid, q, w, rd_, wr_ in nodes:
            vc = list(vcs[last[q]]) if last[q] is not None else [-1] * NQ
            for (s, v) in w:
                wn = wait_node(s, v)
                if wn is not None:
                    wv = vcs[wn]
                    for i in range(NQ):
                        if wv[i] > vc[i]:
                            vc[i] = wv[i]
                    wq = nodes[wn][1]
                    if pos[wn] > vc[qi[wq]]:
                        vc[qi[wq]] = pos[wn]
            vc[qi[q]] = pos[nid]
            vcs[nid] = vc
            last[q] = nid

        by_name = {}
        for nid, q, w, rd_, wr_ in nodes:
            for kind, accs in (("r", rd_), ("w", wr_)):
                for acc in accs:
                    by_name.setdefault(acc[0], []).append((nid, kind, acc[1], acc[2]))
        reports = []
        for name, lst in by_name.items():
            lst.sort()
            for i in range(len(lst)):
                ai, ak, a0, a1 = lst[i]
                for j in range(i + 1, len(lst)):
                    bi, bk, b0, b1 = lst[j]
                    if ak == "r" and bk == "r":
                        continue
                    if a1 <= b0 or b1 <= a0:
                        continue
                    qa, qb_ = nodes[ai][1], nodes[bi][1]
                    if qa == qb_:
                        continue
                    if vcs[bi][qi[qa]] >= pos[ai]:
                        continue
                    reports.append((ai, bi, name, (a0, a1, ak, qa), (b0, b1, bk, qb_)))
                    if len(reports) >= max_report:
                        return reports
        return reports

    with (
        nc.sbuf_tensor("r_tok", [128, NKT], F32) as r_tok,
        nc.sbuf_tensor("ones128_sb", [1, 128], F32) as ones128_sb,
        nc.sbuf_tensor("onesc_sb", [1, 1], F32) as onesc_sb,
        nc.sbuf_tensor("onesb_bf", [128, 1], BF16) as onesb_bf,
        nc.sbuf_tensor("expb_sb", [128, 1], F32) as expb_sb,
        nc.sbuf_tensor("qsel_bf", [128, 64], BF16) as qsel_bf,
        nc.sbuf_tensor("lsel_bf", [128, HPC * 4], BF16) as lsel_bf,
        nc.psum_tensor("ps", [128, 4096], F32) as ps,
    ):
        import contextlib
        _semstack = contextlib.ExitStack()
        for _k in COMPUTE_SEMS + DMA_STREAMS:
            SEM[_k] = _semstack.enter_context(nc.semaphore("sem_" + _k))

        def bank(i, w=1):
            return ps[:, i * 512:(i + w) * 512]

        def PS(i, w=1):
            return ("ps", i, i + w)

        spk_box = [None]

        def spr(k, rows=8):
            return spk_box[0][0:rows, k * 512:(k + 1) * 512]

        def SPR(k):
            return ("spk", k * 512, (k + 1) * 512)

        def emit_stats(loader, xq_sl, xq_res, sq_dst, sq_src, sq_sl, sq_res, nqb,
                       extra_wait_first):
            sq_guard = 0
            prevs = [(0, 0), (0, 0)]
            for qb in range(nqb):
                gp_q = loader(qb, prevs[qb % 2][0], prevs[qb % 2][1])
                for i in range(IT):
                    w0 = None
                    if i == 0:
                        w0 = [gp_q] + (extra_wait_first if qb == 0 else [])
                    emit("tensor", lambda e, i=i, q=qb: e.matmul(
                        ps[0:8, 2 * 512:3 * 512], qsel_bf[:, q * 8:q * 8 + 8],
                        xq_sl(q, i),
                        start=(q == 0 and i == 0), stop=(q == nqb - 1 and i == IT - 1),
                        skip_group_check=True),
                        wait=w0, inc="mm" if i == IT - 1 else None,
                        rd=[("qsel", 0, 64), xq_res(qb, i)], wr=[PS(2)])
                for c in range(IT // 4):
                    emit("scalar", lambda e, q=qb, c=c: e.activation(
                        sq_dst(q, c), sq_src(q, c), AF.Square),
                        wait=[gp_q, ("mm", sq_guard)], inc="ss",
                        rd=[xq_res(qb, 4 * c), xq_res(qb, 4 * c + 1),
                            xq_res(qb, 4 * c + 2), xq_res(qb, 4 * c + 3)],
                        wr=[sq_res(0, 4)])
                    ss_here = cnt["ss"]
                    for j in range(4):
                        ig = c * 4 + j
                        emit("tensor", lambda e, j=j, q=qb, ig=ig: e.matmul(
                            ps[0:8, 3 * 512:4 * 512], qsel_bf[:, q * 8:q * 8 + 8],
                            sq_sl(j),
                            start=(q == 0 and ig == 0), stop=(q == nqb - 1 and ig == IT - 1),
                            skip_group_check=True),
                            wait=[("ss", ss_here)] if j == 0 else None,
                            inc="mm" if j == 3 else None,
                            rd=[("qsel", 0, 64), sq_res(j, j + 1)], wr=[PS(3)])
                    sq_guard = cnt["mm"]
                prevs[qb % 2] = (cnt["mm"], cnt["ss"])
            stats_mm = cnt["mm"]
            emit("vector", lambda e: e.tensor_copy(spr(0, nqb), ps[0:nqb, 2 * 512:3 * 512]),
                 wait=[("mm", stats_mm)], rd=[PS(2)], wr=[SPR(0)])
            emit("vector", lambda e: e.tensor_copy(spr(1, nqb), ps[0:nqb, 3 * 512:4 * 512]),
                 inc="vv", rd=[PS(3)], wr=[SPR(1)])
            ps_free = cnt["vv"]
            emit("vector", lambda e: e.tensor_scalar(spr(2, nqb), spr(0, nqb), 1.0 / D, None, ALU.mult),
                 rd=[SPR(0)], wr=[SPR(2)])
            emit("vector", lambda e: e.tensor_tensor(spr(3, nqb), spr(2, nqb), spr(2, nqb), ALU.mult),
                 rd=[SPR(2)], wr=[SPR(3)])
            emit("vector", lambda e: e.tensor_scalar(spr(4, nqb), spr(1, nqb), 1.0 / D, None, ALU.mult),
                 rd=[SPR(1)], wr=[SPR(4)])
            emit("vector", lambda e: e.tensor_tensor(spr(4, nqb), spr(4, nqb), spr(3, nqb), ALU.subtract),
                 rd=[SPR(4), SPR(3)], wr=[SPR(4)])
            emit("vector", lambda e: e.tensor_scalar(spr(4, nqb), spr(4, nqb), LN_EPS, None, ALU.add),
                 rd=[SPR(4)], wr=[SPR(4)])
            emit("vector", lambda e: e.reciprocal(spr(3, nqb), spr(4, nqb)), inc="vv",
                 rd=[SPR(4)], wr=[SPR(3)])
            v_re = cnt["vv"]
            emit("scalar", lambda e: e.activation(spr(4, nqb), spr(3, nqb), AF.Sqrt),
                 wait=[("vv", v_re)], inc="ss", rd=[SPR(3)], wr=[SPR(4)])
            ss_r = cnt["ss"]
            emit("vector", lambda e: e.tensor_scalar(spr(3, nqb), spr(2, nqb), -1.0, None, ALU.mult),
                 wait=[("ss", ss_r)], inc="vv", rd=[SPR(2)], wr=[SPR(3)])
            v_nm = cnt["vv"]
            return stats_mm, ps_free, ss_r, v_nm

        emit("sync", lambda e: e.dma_start(out=ones128_sb[:, :], in_=ones128[:, :]),
             inc="sc", wr=[("ones128_sb", 0, 128)])
        emit("sync", lambda e: e.dma_start(out=onesc_sb[:, :], in_=onesc[:, :]),
             inc="sc", wr=[("onesc_sb", 0, 1)])
        emit("sync", lambda e: e.dma_start(out=expb_sb[:, :], in_=expb[:, :]),
             inc="sc", wr=[("expb_sb", 0, 1)])
        consts_st = ("sc", cnt["sc"])
        emit("gpsimd", lambda e: e.dma_start(out=onesb_bf[:, :], in_=onesb[:, :]),
             inc="gc", wr=[("onesb_bf", 0, 1)])
        emit("gpsimd", lambda e: e.dma_start(out=qsel_bf[:, :], in_=qsel[:, :]),
             inc="gc", wr=[("qsel", 0, 64)])
        emit("gpsimd", lambda e: e.dma_start(out=lsel_bf[:, :], in_=lsel[:, :]),
             inc="gc", wr=[("lsel", 0, HPC * 4)])
        consts_gp = ("gc", cnt["gc"])

        def phase1_merged(qT_bf, kT_bf, r_flat, negm_flat, rsum3):
            wqk_bf = guards["wqk_bf"]
            wqk_gp = guards["wqk_gp"]
            with (
                nc.sbuf_tensor("xq2_bf", [128, 2 * IT * 512], BF16) as xq2_bf,
                nc.sbuf_tensor("sq_bf", [128, 1024], BF16) as sq_bf,
                nc.sbuf_tensor("rb_sb", [128, 512], F32) as rb_sb,
                nc.sbuf_tensor("t_sum", [1, 512], F32) as t_sum,
                nc.sbuf_tensor("t_sq", [1, 512], F32) as t_sq,
                nc.sbuf_tensor("t_msq", [1, 512], F32) as t_msq,
                nc.sbuf_tensor("r_row", [1, 512], F32) as r_row,
            ):
                def wsl(k, i):
                    o = (k * IT + i) * FPC
                    return wqk_bf[:, o:o + FPC]

                def WSL(k, i):
                    o = (k * IT + i) * FPC
                    return ("wqk", o, o + FPC)

                def xq2(b, i):
                    return xq2_bf[:, (b * IT + i) * 512:(b * IT + i) * 512 + 512]

                def XQ2(b, i):
                    o = (b * IT + i) * 512
                    return ("xq2", o, o + 512)

                rs_free = 0
                for k in range(2):
                    for i in range(IT):
                        w0 = None
                        if i == 0:
                            w0 = [wqk_gp, ("vv", rs_free)]
                        emit("tensor", lambda e, k=k, i=i: e.matmul(
                            ps[0:1, 7 * 512:7 * 512 + FPC], onesb_bf[:, :], wsl(k, i),
                            start=(i == 0), stop=(i == IT - 1), skip_group_check=True),
                            wait=w0, inc="mm" if i == IT - 1 else None,
                            rd=[("onesb_bf", 0, 1), WSL(k, i)], wr=[PS(7)])
                    mm_rs = cnt["mm"]
                    emit("vector", lambda e, k=k: e.tensor_copy(
                        rsum3[0:1, k * FPC:(k + 1) * FPC], ps[0:1, 7 * 512:7 * 512 + FPC]),
                        wait=[("mm", mm_rs)], inc="vv",
                        rd=[PS(7)], wr=[("rsum3", k * FPC, (k + 1) * FPC)])
                    rs_free = cnt["vv"]
                bank7_free = rs_free

                QKB = [0, 1, 4, 5, 6]
                bank_free = {b: 0 for b in QKB}
                stats_free = 0
                xbuf = [(0, 0), (0, 0)]
                rtok_vv = 0
                for qb in range(NQB):
                    b = qb % 2
                    emit("gpsimd", lambda e, b=b, q=qb: e.dma_start(
                        out=xq2_bf.rearrange("p (b i f) -> p b i f", b=2, i=IT)[:, b],
                        in_=xT_t[:, :, q * 512:q * 512 + 512]),
                        wait=[("mm", xbuf[b][0]), ("ss", xbuf[b][1]), consts_gp],
                        inc="gx%d" % b,
                        rd=[("xT", 0, 1)], wr=[("xq2", b * IT * 512, (b + 1) * IT * 512)])
                    gp_here = ("gx%d" % b, cnt["gx%d" % b])
                    for i in range(IT):
                        w0 = None
                        if i == 0:
                            w0 = [gp_here, ("vv", stats_free), consts_gp]
                        emit("tensor", lambda e, i=i, b=b: e.matmul(
                            ps[0:1, 2 * 512:3 * 512], onesb_bf[:, :], xq2(b, i),
                            start=(i == 0), stop=(i == IT - 1), skip_group_check=True),
                            wait=w0, inc="mm" if i == IT - 1 else None,
                            rd=[("onesb_bf", 0, 1), XQ2(b, i)], wr=[PS(2)])
                    mm_sum = cnt["mm"]
                    sq_guard_local = 0
                    for c in range(IT // 2):
                        emit("scalar", lambda e, b=b, c=c: e.activation(
                            sq_bf[:, :],
                            xq2_bf[:, (b * IT) * 512 + c * 1024:(b * IT) * 512 + (c + 1) * 1024],
                            AF.Square),
                            wait=[gp_here, ("mm", sq_guard_local)], inc="ss",
                            rd=[XQ2(b, 2 * c), XQ2(b, 2 * c + 1)],
                            wr=[("sq1", 0, 1024)])
                        ss_here = cnt["ss"]
                        for j in range(2):
                            emit("tensor", lambda e, j=j, c=c: e.matmul(
                                ps[0:1, 3 * 512:4 * 512], onesb_bf[:, :],
                                sq_bf[:, j * 512:(j + 1) * 512],
                                start=(c == 0 and j == 0), stop=(c == IT // 2 - 1 and j == 1),
                                skip_group_check=True),
                                wait=[("ss", ss_here)] if j == 0 else None,
                                inc="mm" if j == 1 else None,
                                rd=[("onesb_bf", 0, 1), ("sq1", j * 512, (j + 1) * 512)],
                                wr=[PS(3)])
                        sq_guard_local = cnt["mm"]
                    mm_sq = cnt["mm"]
                    emit("vector", lambda e: e.tensor_copy(t_sum[0:1, :], ps[0:1, 2 * 512:3 * 512]),
                         wait=[("mm", mm_sum)], rd=[PS(2)], wr=[("t_sum", 0, 512)])
                    emit("vector", lambda e: e.tensor_copy(t_sq[0:1, :], ps[0:1, 3 * 512:4 * 512]),
                         wait=[("mm", mm_sq)], inc="vv", rd=[PS(3)], wr=[("t_sq", 0, 512)])
                    stats_free = cnt["vv"]
                    emit("vector", lambda e: e.tensor_scalar(t_sum[0:1, :], t_sum[0:1, :], 1.0 / D, None, ALU.mult),
                         rd=[("t_sum", 0, 512)], wr=[("t_sum", 0, 512)])
                    emit("vector", lambda e: e.tensor_tensor(t_msq[0:1, :], t_sum[0:1, :], t_sum[0:1, :], ALU.mult),
                         rd=[("t_sum", 0, 512)], wr=[("t_msq", 0, 512)])
                    emit("vector", lambda e: e.tensor_scalar(t_sq[0:1, :], t_sq[0:1, :], 1.0 / D, None, ALU.mult),
                         rd=[("t_sq", 0, 512)], wr=[("t_sq", 0, 512)])
                    emit("vector", lambda e: e.tensor_tensor(t_sq[0:1, :], t_sq[0:1, :], t_msq[0:1, :], ALU.subtract),
                         rd=[("t_sq", 0, 512), ("t_msq", 0, 512)], wr=[("t_sq", 0, 512)])
                    emit("vector", lambda e: e.tensor_scalar(t_sq[0:1, :], t_sq[0:1, :], LN_EPS, None, ALU.add),
                         rd=[("t_sq", 0, 512)], wr=[("t_sq", 0, 512)])
                    emit("vector", lambda e: e.reciprocal(t_sq[0:1, :], t_sq[0:1, :]), inc="vv",
                         rd=[("t_sq", 0, 512)], wr=[("t_sq", 0, 512)])
                    v_iv = cnt["vv"]
                    emit("scalar", lambda e: e.activation(r_row[0:1, :], t_sq[0:1, :], AF.Sqrt),
                         wait=[("vv", v_iv)], inc="ss",
                         rd=[("t_sq", 0, 512)], wr=[("r_row", 0, 512)])
                    ss_r = cnt["ss"]
                    emit("vector", lambda e, q=qb: e.tensor_scalar(
                        negm_flat[0:1, q * 512:q * 512 + 512], t_sum[0:1, :], -1.0, None, ALU.mult),
                        inc="vv",
                        rd=[("t_sum", 0, 512)],
                        wr=[("negm_flat", qb * 512, qb * 512 + 512)])
                    v_negm = cnt["vv"]
                    emit("tensor", lambda e: e.matmul(
                        bank(7), ones128_sb[:, :], r_row[0:1, :],
                        start=True, stop=True, skip_group_check=True),
                        wait=[("ss", ss_r), ("vv", bank7_free), consts_st], inc="mm",
                        rd=[("ones128_sb", 0, 128), ("r_row", 0, 512)], wr=[PS(7)])
                    mm_rb = cnt["mm"]
                    emit("vector", lambda e: e.tensor_copy(
                        rb_sb[:, :], bank(7)),
                        wait=[("mm", mm_rb)], inc="vv",
                        rd=[PS(7)], wr=[("rb_sb", 0, 512)])
                    v_rb = cnt["vv"]
                    fills = [(k, dt) for k in range(2) for dt in range(NDT)]
                    DEPTH = 4
                    evac_vv = []

                    def emit_raw(fi):
                        k, dt = fills[fi]
                        bk = QKB[fi % 5]
                        for i in range(IT):
                            w0 = None
                            if i == 0:
                                w0 = [gp_here, ("vv", bank_free[bk])]
                            emit("tensor", lambda e, k=k, i=i, dt=dt, b=b, bk=bk: e.matmul(
                                bank(bk), wsl(k, i)[:, dt * 128:dt * 128 + 128], xq2(b, i),
                                start=(i == 0), stop=False, skip_group_check=True),
                                wait=w0, rd=[WSL(k, i), XQ2(b, i)], wr=[PS(bk)])

                    def emit_r1_evac(fi):
                        k, dt = fills[fi]
                        bk = QKB[fi % 5]
                        dstT, dstN = (qT_bf, "qT") if k == 0 else (kT_bf, "kT")
                        emit("tensor", lambda e, k=k, dt=dt, q=qb, bk=bk: e.matmul(
                            bank(bk), rsum3[0:1, k * FPC + dt * 128:k * FPC + dt * 128 + 128],
                            negm_flat[0:1, q * 512:q * 512 + 512],
                            start=False, stop=True, skip_group_check=True),
                            wait=[("vv", v_negm)], inc="mm",
                            rd=[("rsum3", k * FPC + dt * 128, k * FPC + dt * 128 + 128),
                                ("negm_flat", qb * 512, qb * 512 + 512)],
                            wr=[PS(bk)])
                        mm_fill = cnt["mm"]
                        emit("vector", lambda e, dt=dt, q=qb, bk=bk, dstT=dstT: e.tensor_tensor(
                            dstT[:, dt * S + q * 512:dt * S + q * 512 + 512],
                            bank(bk), rb_sb[:, :], ALU.mult),
                            wait=[("mm", mm_fill), ("vv", v_rb)], inc="vv",
                            rd=[PS(bk), ("rb_sb", 0, 512)],
                            wr=[(dstN, dt * S + qb * 512, dt * S + qb * 512 + 512)])
                        bank_free[bk] = cnt["vv"]

                    for fi in range(len(fills)):
                        emit_raw(fi)
                        if fi >= DEPTH:
                            emit_r1_evac(fi - DEPTH)
                    for fi in range(len(fills) - DEPTH, len(fills)):
                        emit_r1_evac(fi)
                    for tt in range(4):
                        gtt = qb * 4 + tt
                        emit("tensor", lambda e, tt=tt, gtt=gtt: e.matmul(
                            ps[:, 7 * 512 + tt:7 * 512 + tt + 1],
                            r_row[0:1, tt * 128:tt * 128 + 128], onesc_sb[:, :],
                            start=True, stop=True, skip_group_check=True),
                            wait=[("vv", v_rb)] if tt == 0 else None,
                            inc="mm" if tt == 3 else None,
                            rd=[("r_row", tt * 128, tt * 128 + 128), ("onesc_sb", 0, 1)],
                            wr=[PS(7)])
                    mm_rt = cnt["mm"]
                    emit("vector", lambda e, q=qb: e.tensor_copy(
                        r_tok[:, q * 4:q * 4 + 4], ps[:, 7 * 512:7 * 512 + 4]),
                        wait=[("mm", mm_rt)], inc="vv",
                        rd=[PS(7)], wr=[("r_tok", qb * 4, qb * 4 + 4)])
                    bank7_free = cnt["vv"]
                    rtok_vv = cnt["vv"]
                    xbuf[b] = (cnt["mm"], cnt["ss"])
                guards["v_rtok"] = rtok_vv
                guards["qk_mm"] = cnt["mm"]
                guards["qk_vv"] = cnt["vv"]


        def phase1_v(vtok_bf, r_flat, negm_flat, rsum3):
            v_rtok = guards["v_rtok"]
            if True:
                with (
                    nc.sbuf_tensor("wv_bf", [128, IT * FPC], BF16) as wv_bf,
                    nc.sbuf_tensor("xq3_bf", [128, 2 * IT * 512], BF16) as xq3_bf,
                ):
                    def wvs(i):
                        return wv_bf[:, i * FPC:(i + 1) * FPC]

                    def WVS(i):
                        return ("wv", i * FPC, (i + 1) * FPC)

                    def xq3(b, i):
                        return xq3_bf[:, (b * IT + i) * 512:(b * IT + i) * 512 + 512]

                    def XQ3(b, i):
                        o = (b * IT + i) * 512
                        return ("xq3", o, o + 512)

                    emit("gpsimd", lambda e: e.dma_start(
                        out=wv_bf.rearrange("p (i f) -> p i f", i=IT), in_=wvT_t),
                        wait=[("mm", guards["qk_mm"]), ("vv", guards["qk_vv"])], inc="gw",
                        rd=[("wvT", 0, 1)], wr=[("wv", 0, IT * FPC)])
                    wv_gp = ("gw", cnt["gw"])
                    rs_free = 0
                    for ch in range(NCH):
                        w = min(512, FPC - ch * 512)
                        for i in range(IT):
                            w0 = None
                            if i == 0:
                                w0 = [wv_gp, ("vv", rs_free)]
                            emit("tensor", lambda e, i=i, ch=ch, w=w: e.matmul(
                                ps[0:1, 5 * 512:5 * 512 + w], onesb_bf[:, :],
                                wvs(i)[:, ch * 512:ch * 512 + w],
                                start=(i == 0), stop=(i == IT - 1), skip_group_check=True),
                                wait=w0, inc="mm" if i == IT - 1 else None,
                                rd=[("onesb_bf", 0, 1), WVS(i)], wr=[PS(5)])
                        mm_rs = cnt["mm"]
                        emit("vector", lambda e, ch=ch, w=w: e.tensor_copy(
                            rsum3[0:1, 2 * FPC + ch * 512:2 * FPC + ch * 512 + w],
                            ps[0:1, 5 * 512:5 * 512 + w]),
                            wait=[("mm", mm_rs)], inc="vv",
                            rd=[PS(5)], wr=[("rsum3", 2 * FPC, 3 * FPC)])
                        rs_free = cnt["vv"]
                    v_rsum_v = cnt["vv"]
                    bank_free = {2: 0, 3: 0}
                    xbuf_free = [guards["qk_mm"], guards["qk_mm"]]
                    for qb in range(NQB):
                        b = qb % 2
                        emit("gpsimd", lambda e, b=b, q=qb: e.dma_start(
                            out=xq3_bf.rearrange("p (b i f) -> p b i f", b=2, i=IT)[:, b],
                            in_=xT_t[:, :, q * 512:q * 512 + 512]),
                            wait=[("mm", xbuf_free[b]), ("vv", guards["qk_vv"])], inc="gx%d" % b,
                            rd=[("xT", 0, 1)], wr=[("xq3", b * IT * 512, (b + 1) * IT * 512)])
                        gp_here = ("gx%d" % b, cnt["gx%d" % b])
                        for tt in range(4):
                            gtt = qb * 4 + tt
                            bk = 2 + (tt % 2)
                            for ch in range(NCH):
                                w = min(512, FPC - ch * 512)
                                for i in range(IT):
                                    w0 = None
                                    if i == 0 and ch == 0:
                                        w0 = [gp_here,
                                              ("vv", max(bank_free[bk], v_rsum_v))]
                                    emit("tensor", lambda e, i=i, tt=tt, b=b, bk=bk, ch=ch, w=w: e.matmul(
                                        bank(bk)[:, ch * 512:ch * 512 + w],
                                        xq3(b, i)[:, tt * 128:tt * 128 + 128],
                                        wvs(i)[:, ch * 512:ch * 512 + w],
                                        start=(i == 0), stop=False, skip_group_check=True),
                                        wait=w0, rd=[XQ3(b, i), WVS(i)], wr=[PS(bk)])
                                emit("tensor", lambda e, gtt=gtt, bk=bk, ch=ch, w=w: e.matmul(
                                    bank(bk)[:, ch * 512:ch * 512 + w],
                                    negm_flat[0:1, gtt * 128:gtt * 128 + 128],
                                    rsum3[0:1, 2 * FPC + ch * 512:2 * FPC + ch * 512 + w],
                                    start=False, stop=True, skip_group_check=True),
                                    inc="mm" if ch == NCH - 1 else None,
                                    rd=[("negm_flat", gtt * 128, gtt * 128 + 128),
                                        ("rsum3", 2 * FPC, 3 * FPC)],
                                    wr=[PS(bk)])
                            mm_fill = cnt["mm"]
                            emit("vector", lambda e, gtt=gtt, bk=bk: e.tensor_scalar(
                                vtok_bf.rearrange("p (h s) -> p h s", h=HPC)[:, :, gtt * 128:gtt * 128 + 128],
                                bank(bk)[:, 0:FPC].rearrange("p (h d) -> p h d", h=HPC),
                                r_tok[:, gtt:gtt + 1], None, ALU.mult),
                                wait=[("mm", mm_fill), ("vv", v_rtok)], inc="vv",
                                rd=[PS(bk), ("r_tok", gtt, gtt + 1)],
                                wr=[("vtok", h * S + gtt * 128, h * S + gtt * 128 + 128)
                                    for h in range(HPC)])
                            bank_free[bk] = cnt["vv"]
                        xbuf_free[b] = cnt["mm"]
                    guards["proj_mm"] = cnt["mm"]
                    guards["proj_vv"] = cnt["vv"]

        def phase2(qT_bf, kT_bf, vtok_bf, l_pack):
            with (
                nc.sbuf_tensor("p_bf", [128, NKT * 512], BF16) as p_bf,
                nc.sbuf_tensor("mask_bf", [128, 4 * 512], BF16) as mask_bf,
                nc.sbuf_tensor("ostg2", [128, 2 * 512], BF16) as ostg2,
            ):
                emit("gpsimd", lambda e: e.dma_start(
                    out=mask_bf.rearrange("p (t j) -> p t j", t=4), in_=maskc_t),
                    wait=[("mm", guards["proj_mm"]), ("vv", guards["proj_vv"])], inc="gm",
                    rd=[("maskc", 0, 1)], wr=[("mask", 0, 4 * 512)])
                gp_mask = ("gm", cnt["gm"])
                o_free = [0]
                l_free = [0]
                stg_free = [0, 0]
                first = [True]
                groups = []
                for qb in range(NQB):
                    ng = (4 * qb + 4) // 2
                    for h in range(HPC):
                        for j in range(ng):
                            groups.append((qb, h, j, ng))
                gstate = {}

                def emit_qk_mask_exp(gi):
                    qb, h, j, ng = groups[gi]
                    pair = j % 2
                    for t2 in range(2):
                        kt = 2 * j + t2
                        w0 = None
                        if first[0]:
                            w0 = [("vv", guards["proj_vv"]), gp_mask, consts_st]
                            first[0] = False
                        emit("tensor", lambda e, h=h, qb=qb, kt=kt, pair=pair, t2=t2: e.matmul(
                            bank(pair * 2 + t2),
                            kT_bf[:, h * S + kt * 128:h * S + kt * 128 + 128],
                            qT_bf[:, h * S + qb * 512:h * S + qb * 512 + 512],
                            start=True, stop=True, skip_group_check=True),
                            wait=w0, inc="mm" if t2 == 1 else None,
                            rd=[("kT", h * S + kt * 128, h * S + kt * 128 + 128),
                                ("qT", h * S + qb * 512, h * S + qb * 512 + 512)],
                            wr=[PS(pair * 2 + t2)])
                    mm_qk = cnt["mm"]
                    if 2 * j >= 4 * qb:
                        for t2 in range(2):
                            tloc = 2 * j + t2 - 4 * qb
                            emit("vector", lambda e, pair=pair, t2=t2, tloc=tloc: e.tensor_tensor(
                                bank(pair * 2 + t2), bank(pair * 2 + t2),
                                mask_bf[:, tloc * 512:tloc * 512 + 512], ALU.add),
                                wait=[("mm", mm_qk)] if t2 == 0 else None,
                                inc="vv" if t2 == 1 else None,
                                rd=[PS(pair * 2 + t2), ("mask", tloc * 512, tloc * 512 + 512)],
                                wr=[PS(pair * 2 + t2)])
                        wexp = [("vv", cnt["vv"])]
                    else:
                        wexp = [("mm", mm_qk)]
                    emit("scalar", lambda e, pair=pair, j=j: e.activation(
                        p_bf[:, (2 * j) * 512:(2 * j) * 512 + 1024],
                        bank(pair * 2, 2), AF.Exp, bias=expb_sb[:, :], scale=SCALE),
                        wait=wexp, inc="ss",
                        rd=[PS(pair * 2, 2), ("expb_sb", 0, 1)],
                        wr=[("p_bf", 2 * j * 512, 2 * j * 512 + 1024)])
                    gstate[gi] = cnt["ss"]

                def emit_lpv(gi):
                    qb, h, j, ng = groups[gi]
                    ss_exp = gstate.pop(gi)
                    for t2 in range(2):
                        kt = 2 * j + t2
                        w0 = [("ss", ss_exp)] if t2 == 0 else []
                        if t2 == 0 and j == 0:
                            w0.append(("vv", o_free[0]))
                            if h == 0:
                                w0.append(("vv", l_free[0]))
                        emit("tensor", lambda e, h=h, kt=kt, nkt=2 * ng: e.matmul(
                            ps[0:4, 5 * 512:6 * 512], lsel_bf[:, h * 4:h * 4 + 4],
                            p_bf[:, kt * 512:kt * 512 + 512],
                            start=(h == 0 and kt == 0),
                            stop=(h == HPC - 1 and kt == nkt - 1),
                            skip_group_check=True), wait=w0 or None,
                            rd=[("lsel", h * 4, h * 4 + 4),
                                ("p_bf", kt * 512, kt * 512 + 512)],
                            wr=[PS(5)])
                        emit("tensor", lambda e, h=h, kt=kt, nkt=2 * ng: e.matmul(
                            bank(4), vtok_bf[:, h * S + kt * 128:h * S + kt * 128 + 128],
                            p_bf[:, kt * 512:kt * 512 + 512],
                            start=(kt == 0), stop=(kt == nkt - 1),
                            skip_group_check=True),
                            inc="mm" if kt == 2 * ng - 1 else None,
                            rd=[("vtok", h * S + kt * 128, h * S + kt * 128 + 128),
                                ("p_bf", kt * 512, kt * 512 + 512)],
                            wr=[PS(4)])
                    if j == ng - 1:
                        mm_pv = cnt["mm"]
                        sb = (qb * HPC + h) % 2
                        emit("vector", lambda e, sb=sb: e.tensor_copy(
                            ostg2[:, sb * 512:sb * 512 + 512], bank(4)),
                            wait=[("mm", mm_pv), ("so%d" % sb, stg_free[sb])], inc="vv",
                            rd=[PS(4)], wr=[("ostg2", sb * 512, sb * 512 + 512)])
                        o_free[0] = cnt["vv"]
                        v_stg = cnt["vv"]
                        emit("sync", lambda e, h=h, qb=qb, sb=sb: e.dma_start(
                            out=o_dram.ap()[h * 128:h * 128 + 128, qb * 512:qb * 512 + 512],
                            in_=ostg2[:, sb * 512:sb * 512 + 512]),
                            wait=[("vv", v_stg)], inc="so%d" % sb,
                            rd=[("ostg2", sb * 512, sb * 512 + 512)],
                            wr=[("o_dram", h * NQB + qb, h * NQB + qb + 1)])
                        stg_free[sb] = cnt["so%d" % sb]
                        if h == HPC - 1:
                            emit("vector", lambda e, qb=qb: e.tensor_copy(
                                l_pack[0:HPC, qb * 512:qb * 512 + 512],
                                ps[0:HPC, 5 * 512:6 * 512]),
                                wait=[("mm", cnt["mm"])], inc="vv",
                                rd=[PS(5)], wr=[("l_pack", qb * 512, qb * 512 + 512)])
                            l_free[0] = cnt["vv"]

                for gi in range(len(groups)):
                    emit_qk_mask_exp(gi)
                    if gi >= 1:
                        emit_lpv(gi - 1)
                emit_lpv(len(groups) - 1)
                emit("vector", lambda e: e.reciprocal(l_pack[0:HPC, :], l_pack[0:HPC, :]),
                     inc="vv", rd=[("l_pack", 0, S)], wr=[("l_pack", 0, S)])
                guards["v_il"] = cnt["vv"]
                guards["attn_mm"] = cnt["mm"]
                guards["attn_st0"] = cnt["so0"]
                guards["attn_st1"] = cnt["so1"]
                guards["attn_ss"] = cnt["ss"]

        def phase3a(l_pack):
            with (
                nc.sbuf_tensor("osh_l", [128, HPC * S], BF16) as osh_l,
                nc.sbuf_tensor("wo_bf", [128, HPC * D], BF16) as wo_bf,
                nc.sbuf_tensor("il_h", [1, S], F32) as il_h,
                nc.sbuf_tensor("ilb_sb", [128, 2 * 512], F32) as ilb_sb,
                nc.sbuf_tensor("ostg", [128, 2 * 512], BF16) as ostg,
            ):
                emit("gpsimd", lambda e: e.dma_start(
                    out=wo_bf.rearrange("p (i f) -> p i f", i=NDT), in_=woTmy_t),
                    wait=[("mm", guards["attn_mm"]), ("ss", guards["attn_ss"]),
                          ("vv", guards["v_il"])], inc="gw",
                    rd=[("woTmy", 0, 1)], wr=[("wo_bf", 0, HPC * D)])
                wo_gp = ("gw", cnt["gw"])
                emit("sync", lambda e: e.dma_start(
                    out=osh_l.rearrange("p (h s) -> p h s", h=HPC), in_=o_dram_t),
                    wait=[("so0", guards["attn_st0"]), ("so1", guards["attn_st1"]),
                          ("mm", guards["attn_mm"]),
                          ("ss", guards["attn_ss"]), ("vv", guards["v_il"])], inc="sosh",
                    rd=[("o_dram", 0, HPC * NQB)], wr=[("osh", 0, HPC * S)])
                st_osh = ("sosh", cnt["sosh"])
                ilb_free = [0, 0]
                ilh_mm = 0
                for h in range(HPC):
                    gp_il = ("gil", emit("gpsimd", lambda e, h=h: e.dma_start(
                        out=il_h[0:1, :], in_=l_pack[h:h + 1, :]),
                        wait=[("vv", guards["v_il"]), ("mm", ilh_mm)], inc="gil",
                        rd=[("l_pack", 0, S)], wr=[("il_h", 0, S)]))
                    for qb in range(NQB):
                        sb = (h * NQB + qb) % 2
                        bk = 6 + sb
                        emit("tensor", lambda e, h=h, qb=qb, bk=bk: e.matmul(
                            bank(bk), ones128_sb[:, :],
                            il_h[0:1, qb * 512:qb * 512 + 512],
                            start=True, stop=True, skip_group_check=True),
                            wait=[gp_il, ("vv", ilb_free[sb])], inc="mm",
                            rd=[("ones128_sb", 0, 128), ("il_h", qb * 512, qb * 512 + 512)],
                            wr=[PS(bk)])
                        mm_b = cnt["mm"]
                        emit("vector", lambda e, sb=sb, bk=bk: e.tensor_copy(
                            ilb_sb[:, sb * 512:sb * 512 + 512], bank(bk)),
                            wait=[("mm", mm_b)], inc="vv",
                            rd=[PS(bk)], wr=[("ilb", sb * 512, sb * 512 + 512)])
                        v_cp = cnt["vv"]
                        emit("vector", lambda e, h=h, qb=qb, sb=sb: e.tensor_tensor(
                            osh_l[:, h * S + qb * 512:h * S + qb * 512 + 512],
                            osh_l[:, h * S + qb * 512:h * S + qb * 512 + 512],
                            ilb_sb[:, sb * 512:sb * 512 + 512], ALU.mult),
                            wait=([st_osh] if (h == 0 and qb == 0) else None),
                            inc="vv",
                            rd=[("osh", h * S + qb * 512, h * S + qb * 512 + 512),
                                ("ilb", sb * 512, sb * 512 + 512)],
                            wr=[("osh", h * S + qb * 512, h * S + qb * 512 + 512)])
                        ilb_free[sb] = cnt["vv"]
                    ilh_mm = cnt["mm"]
                onorm_vv = cnt["vv"]

                rs_cc = []
                bank_free2 = {0: 0, 1: 0}
                stg_free = [("spe0", 0), ("spe0", 0)]
                for ec in range(NOC):
                    for qt in range(NKT):
                        bk = qt % 2
                        for h in range(NDT):
                            w0 = None
                            if h == 0:
                                w0 = [("vv", bank_free2[bk])]
                                if ec == 0 and qt == 0:
                                    w0 += [wo_gp, ("vv", onorm_vv)]
                            emit("tensor", lambda e, h=h, qt=qt, ec=ec, bk=bk: e.matmul(
                                bank(bk), osh_l[:, h * S + qt * 128:h * S + qt * 128 + 128],
                                wo_bf[:, h * D + ec * 512:h * D + ec * 512 + 512],
                                start=(h == 0), stop=(h == NDT - 1), skip_group_check=True),
                                wait=w0, inc="mm" if h == NDT - 1 else None,
                                rd=[("osh", h * S + qt * 128, h * S + qt * 128 + 128),
                                    ("wo_bf", h * D + ec * 512, h * D + ec * 512 + 512)],
                                wr=[PS(bk)])
                        mm_fill = cnt["mm"]
                        sb = qt % 2
                        emit("vector", lambda e, sb=sb, bk=bk: e.tensor_copy(
                            ostg[:, sb * 512:sb * 512 + 512], bank(bk)),
                            wait=[("mm", mm_fill), stg_free[sb]], inc="vv",
                            rd=[PS(bk)], wr=[("ostg", sb * 512, sb * 512 + 512)])
                        bank_free2[bk] = cnt["vv"]
                        v_stg = cnt["vv"]
                        strm = "spe%d" % ec
                        emit("sync", lambda e, ec=ec, qt=qt, sb=sb: e.dma_start(
                            out=sp[ec].ap()[qt * 128:qt * 128 + 128, :],
                            in_=ostg[:, sb * 512:sb * 512 + 512]),
                            wait=[("vv", v_stg)], inc=strm,
                            rd=[("ostg", sb * 512, sb * 512 + 512)],
                            wr=[(f"sp{ec}", qt * 128, qt * 128 + 128)])
                        stg_free[sb] = (strm, cnt[strm])
                guards["spe_totals"] = [("spe%d" % e2, cnt["spe%d" % e2])
                                        for e2 in range(NOC)]
                guards["rs_cc"] = rs_cc
                guards["oproj_mm"] = cnt["mm"]
                guards["oproj_vv"] = cnt["vv"]

        def phase3bc():
            with nc.sbuf_tensor("g_sb", [128, NQT * D], BF16) as g_sb:
                with (
                    nc.sbuf_tensor("spk", [8, 6 * 512], F32) as spk,
                    nc.sbuf_tensor("xqmy_bf", [128, IT * TPC], BF16) as xqmy_bf,
                    nc.sbuf_tensor("sqm_bf", [128, 2048], BF16) as sqm_bf,
                    nc.sbuf_tensor("wch_bf", [128, 4 * IT * 512], BF16) as wch_bf,
                    nc.sbuf_tensor("gb_sb", [128, D], F32) as gb_sb,
                    nc.sbuf_tensor("bb_sb", [128, D], F32) as bb_sb,
                    nc.sbuf_tensor("gb_row", [1, 2 * D], F32) as gb_row,
                    nc.sbuf_tensor("rsf_bf", [1, D], BF16) as rsf_bf,
                    nc.sbuf_tensor("rmy_flat", [1, TPC], F32) as rmy_flat,
                    nc.sbuf_tensor("nmy_flat", [1, TPC], BF16) as nmy_flat,
                    nc.sbuf_tensor("rtmy", [128, NQT], F32) as rtmy,
                    nc.sbuf_tensor("z_sb", [128, 2 * 512], F32) as z_sb,
                ):
                    spk_box[0] = spk
                    og = guards["oproj_mm"]
                    ogv = guards["oproj_vv"]
                    emit("gpsimd", lambda e: e.dma_start(
                        out=xqmy_bf.rearrange("p (i f) -> p i f", i=IT), in_=xTmy_t),
                        wait=[("mm", og), ("vv", ogv)], inc="gq",
                        rd=[("xTmy", 0, 1)], wr=[("xqmy", 0, IT * TPC)])
                    gp_xmy = ("gq", cnt["gq"])
                    emit("sync", lambda e: e.dma_start(out=gb_row[0:1, 0:D], in_=gam_in[:, :]),
                         wait=[("mm", og)], inc="sgb",
                         rd=[("gam_in", 0, 1)], wr=[("gb_row", 0, D)])
                    emit("sync", lambda e: e.dma_start(out=gb_row[0:1, D:2 * D], in_=bet_in[:, :]),
                         inc="sgb", rd=[("bet_in", 0, 1)], wr=[("gb_row", D, 2 * D)])
                    st_gb = ("sgb", cnt["sgb"])
                    wch_loaded = []
                    for oc in range(NOC):
                        strm = "gwc%d" % oc
                        emit("gpsimd", lambda e, oc=oc: e.dma_start(
                            out=wch_bf.rearrange("p (b i f) -> p b i f", b=4, i=IT)[:, oc],
                            in_=wfT_t[:, :, oc * 512:oc * 512 + 512]),
                            wait=[("mm", og), ("vv", ogv)] if oc == 0 else None,
                            inc=strm,
                            rd=[("wfT", 0, 1)],
                            wr=[("wch", oc * IT * 512, (oc + 1) * IT * 512)])
                        wch_loaded.append((strm, cnt[strm]))

                    def xqm(i):
                        return xqmy_bf[:, i * TPC:(i + 1) * TPC]

                    def wch(i, oc):
                        return wch_bf[:, (oc * IT + i) * 512:(oc * IT + i) * 512 + 512]

                    def WCH(i, oc):
                        o = (oc * IT + i) * 512
                        return ("wch", o, o + 512)

                    xqmy_v = xqmy_bf.rearrange("p (i f) -> p i f", i=IT)

                    r = emit_stats(
                        lambda qb, pm, pss: gp_xmy,
                        lambda q, i: xqm(i)[:, q * 512:q * 512 + 512],
                        lambda q, i: ("xqmy", i * TPC + q * 512, i * TPC + q * 512 + 512),
                        lambda q, c: sqm_bf.rearrange("p (i f) -> p i f", i=4),
                        lambda q, c: xqmy_v[:, 4 * c:4 * c + 4, q * 512:q * 512 + 512],
                        lambda j: sqm_bf[:, j * 512:(j + 1) * 512],
                        lambda j0, j1: ("sqm", j0 * 512, j1 * 512),
                        MYQB, [("vv", guards["qk_vv"]), ("mm", og)])
                    _, _, ss_r2, v_nm2 = r
                    emit("gpsimd", lambda e: e.dma_start(
                        out=rmy_flat[0:1, :].rearrange("a (q j) -> a q j", q=MYQB),
                        in_=spr(4, MYQB)),
                        wait=[("ss", ss_r2)], inc="gf",
                        rd=[SPR(4)], wr=[("rmy", 0, TPC)])
                    emit("gpsimd", lambda e: e.dma_start(
                        out=nmy_flat[0:1, :].rearrange("a (q j) -> a q j", q=MYQB),
                        in_=spr(3, MYQB)),
                        wait=[("vv", v_nm2)], inc="gf",
                        rd=[SPR(3)], wr=[("nmy", 0, TPC)])
                    gp_f2 = ("gf", cnt["gf"])

                    rs_cc = guards["rs_cc"]
                    if upto != "p3p":
                        for ec in range(NOC):
                            w0 = None
                            if ec == 0:
                                w0 = (guards["spe_totals"] +
                                      [gp_xmy, st_gb, gp_f2] + wch_loaded)
                            emit("gpsimd", lambda e, ec=ec: e.collective_compute(
                                "ReduceScatter", ALU.add, replica_groups=RG4,
                                ins=[sp[ec].ap().opt()], outs=[sm[ec].ap().opt()]),
                                wait=w0, inc="cc",
                                rd=[(f"sp{ec}", 0, S)], wr=[(f"sm{ec}", 0, TPC)])
                            rs_cc.append(cnt["cc"])
                    else:
                        rs_cc.extend([0] * NOC)
                    cc_all = rs_cc[-1]

                    for tt in range(NQT):
                        emit("tensor", lambda e, tt=tt: e.matmul(
                            ps[:, 4 * 512 + tt:4 * 512 + tt + 1],
                            rmy_flat[0:1, tt * 128:tt * 128 + 128], onesc_sb[:, :],
                            start=True, stop=True, skip_group_check=True),
                            wait=[gp_f2] if tt == 0 else None,
                            inc="mm" if tt == NQT - 1 else None,
                            rd=[("rmy", tt * 128, tt * 128 + 128), ("onesc_sb", 0, 1)],
                            wr=[PS(4)])
                    mm_rt2 = cnt["mm"]
                    emit("vector", lambda e: e.tensor_copy(rtmy[:, 0:NQT], ps[:, 4 * 512:4 * 512 + NQT]),
                         wait=[("mm", mm_rt2)], inc="vv",
                         rd=[PS(4)], wr=[("rtmy", 0, NQT)])
                    v_rt2 = cnt["vv"]

                    gbfree = [0, 0]
                    for kk, dst, dstN in ((0, gb_sb, "gb"), (1, bb_sb, "bb")):
                        for ocq in range(NOC):
                            sb6 = (kk * NOC + ocq) % 2
                            emit("tensor", lambda e, kk=kk, ocq=ocq, sb6=sb6: e.matmul(
                                bank(6 + sb6), ones128_sb[:, :],
                                gb_row[0:1, kk * D + ocq * 512:kk * D + ocq * 512 + 512],
                                start=True, stop=True, skip_group_check=True),
                                wait=[st_gb, ("vv", max(gbfree[sb6], ogv))], inc="mm",
                                rd=[("ones128_sb", 0, 128),
                                    ("gb_row", kk * D + ocq * 512, kk * D + ocq * 512 + 512)],
                                wr=[PS(6 + sb6)])
                            mm_g = cnt["mm"]
                            emit("vector", lambda e, ocq=ocq, sb6=sb6, dst=dst: e.tensor_copy(
                                dst[:, ocq * 512:ocq * 512 + 512], bank(6 + sb6)),
                                wait=[("mm", mm_g)], inc="vv",
                                rd=[PS(6 + sb6)],
                                wr=[(dstN, ocq * 512, ocq * 512 + 512)])
                            gbfree[sb6] = cnt["vv"]
                    v_gb = cnt["vv"]

                    bank_free3 = {0: 0, 1: 0}
                    z_free = [0, 0]
                    rsf_free = 0
                    for oc in range(NOC):
                        for i in range(IT):
                            w0 = None
                            if i == 0:
                                w0 = [wch_loaded[oc], ("vv", rsf_free)]
                            emit("tensor", lambda e, i=i, oc=oc: e.matmul(
                                ps[0:1, 5 * 512:6 * 512], onesb_bf[:, :], wch(i, oc),
                                start=(i == 0), stop=(i == IT - 1), skip_group_check=True),
                                wait=w0, inc="mm" if i == IT - 1 else None,
                                rd=[("onesb_bf", 0, 1), WCH(i, oc)], wr=[PS(5)])
                        mm_rsf = cnt["mm"]
                        emit("vector", lambda e, oc=oc: e.tensor_copy(
                            rsf_bf[0:1, oc * 512:oc * 512 + 512], ps[0:1, 5 * 512:6 * 512]),
                            wait=[("mm", mm_rsf)], inc="vv",
                            rd=[PS(5)], wr=[("rsf", oc * 512, oc * 512 + 512)])
                        v_rsf = cnt["vv"]
                        rsf_free = v_rsf
                        for qt in range(NQT):
                            bk = qt % 2
                            for i in range(IT):
                                w0 = [("vv", max(bank_free3[bk], v_rsf))] if i == 0 else None
                                emit("tensor", lambda e, i=i, qt=qt, oc=oc, bk=bk: e.matmul(
                                    bank(bk), xqm(i)[:, qt * 128:qt * 128 + 128], wch(i, oc),
                                    start=(i == 0), stop=False, skip_group_check=True),
                                    wait=w0,
                                    rd=[("xqmy", i * TPC + qt * 128, i * TPC + qt * 128 + 128),
                                        WCH(i, oc)],
                                    wr=[PS(bk)])
                            emit("tensor", lambda e, qt=qt, oc=oc, bk=bk: e.matmul(
                                bank(bk), nmy_flat[0:1, qt * 128:qt * 128 + 128],
                                rsf_bf[0:1, oc * 512:oc * 512 + 512],
                                start=False, stop=True, skip_group_check=True), inc="mm",
                                rd=[("nmy", qt * 128, qt * 128 + 128),
                                    ("rsf", oc * 512, oc * 512 + 512)],
                                wr=[PS(bk)])
                            mm_fill = cnt["mm"]
                            sbz = qt % 2
                            emit("vector", lambda e, qt=qt, oc=oc, bk=bk, sbz=sbz: e.scalar_tensor_tensor(
                                z_sb[:, sbz * 512:sbz * 512 + 512], bank(bk), rtmy[:, qt:qt + 1],
                                gb_sb[:, oc * 512:oc * 512 + 512], ALU.mult, ALU.mult),
                                wait=[("mm", mm_fill), ("vv", max(v_gb, v_rt2)),
                                      ("ss", z_free[sbz])], inc="vv",
                                rd=[PS(bk), ("rtmy", qt, qt + 1), ("gb", oc * 512, oc * 512 + 512)],
                                wr=[("z", sbz * 512, sbz * 512 + 512)])
                            bank_free3[bk] = cnt["vv"]
                            emit("vector", lambda e, oc=oc, sbz=sbz: e.tensor_tensor(
                                z_sb[:, sbz * 512:sbz * 512 + 512], z_sb[:, sbz * 512:sbz * 512 + 512],
                                bb_sb[:, oc * 512:oc * 512 + 512], ALU.add), inc="vv",
                                rd=[("z", sbz * 512, sbz * 512 + 512), ("bb", oc * 512, oc * 512 + 512)],
                                wr=[("z", sbz * 512, sbz * 512 + 512)])
                            v_z = cnt["vv"]
                            emit("scalar", lambda e, qt=qt, oc=oc, sbz=sbz: e.activation(
                                g_sb[:, qt * D + oc * 512:qt * D + oc * 512 + 512],
                                z_sb[:, sbz * 512:sbz * 512 + 512], AF.Silu),
                                wait=[("vv", v_z)], inc="ss",
                                rd=[("z", sbz * 512, sbz * 512 + 512)],
                                wr=[("g", qt * D + oc * 512, qt * D + oc * 512 + 512)])
                            z_free[sbz] = cnt["ss"]
                    guards["ffn_mm"] = cnt["mm"]
                    guards["ffn_ss"] = cnt["ss"]
                    guards["ffn_vv"] = cnt["vv"]
                    guards["ffn_gwc0"] = cnt["gwc0"]
                    guards["ffn_gwc1"] = cnt["gwc1"]
                    if guards.get("dump_g"):
                        dump([(g_sb[:, :], "g", 0)])

                with (
                    nc.sbuf_tensor("stile", [128, 2 * 512], BF16) as stile,
                    nc.sbuf_tensor("xtok_sb", [128, 2 * D], F32) as xtok_sb,
                    nc.sbuf_tensor("osb", [128, 2 * D], F32) as osb,
                ):
                    rs_cc = guards["rs_cc"]
                    s_free = [0, 0]
                    x_free = [0, 0]
                    o_store = [0, 0]
                    fg = guards["ffn_mm"]
                    for qt in range(NQT):
                        xb = qt % 2
                        w0 = [("vv", x_free[xb])]
                        if qt <= 1:
                            w0 += [("mm", fg), ("ss", guards["ffn_ss"]),
                                   ("vv", guards["ffn_vv"]),
                                   ("gwc0", guards["ffn_gwc0"]),
                                   ("gwc1", guards["ffn_gwc1"])]
                        emit("sync", lambda e, qt=qt, xb=xb: e.dma_start(
                            out=xtok_sb[:, xb * D:(xb + 1) * D],
                            in_=x_tok[qt * 128:qt * 128 + 128, :]),
                            wait=w0, inc="sxt%d" % xb,
                            rd=[("x_tok", 0, 1)], wr=[("xtok", xb * D, (xb + 1) * D)])
                        st_x = ("sxt%d" % xb, cnt["sxt%d" % xb])
                        for ec in range(NOC):
                            sb = (qt * NOC + ec) % 2
                            emit("sync", lambda e, qt=qt, ec=ec, sb=sb: e.dma_start(
                                out=stile[:, sb * 512:sb * 512 + 512],
                                in_=sm[ec].ap()[qt * 128:qt * 128 + 128, :]),
                                wait=[("cc", rs_cc[ec]), ("vv", s_free[sb])], inc="sst%d" % sb,
                                rd=[(f"sm{ec}", qt * 128, qt * 128 + 128)],
                                wr=[("stile", sb * 512, sb * 512 + 512)])
                            st_s = ("sst%d" % sb, cnt["sst%d" % sb])
                            emit("vector", lambda e, qt=qt, ec=ec, sb=sb, xb=xb: e.tensor_tensor(
                                osb[:, xb * D + ec * 512:xb * D + ec * 512 + 512],
                                stile[:, sb * 512:sb * 512 + 512],
                                g_sb[:, qt * D + ec * 512:qt * D + ec * 512 + 512], ALU.mult),
                                wait=[st_s,
                                      ("ss", guards["ffn_ss"] if (qt == 0 and ec == 0) else 0),
                                      ("sou%d" % xb, o_store[xb] if ec == 0 else 0)],
                                inc="vv",
                                rd=[("stile", sb * 512, sb * 512 + 512),
                                    ("g", qt * D + ec * 512, qt * D + ec * 512 + 512)],
                                wr=[("osb", xb * D + ec * 512, xb * D + ec * 512 + 512)])
                            s_free[sb] = cnt["vv"]
                            emit("vector", lambda e, qt=qt, ec=ec, xb=xb: e.tensor_tensor(
                                osb[:, xb * D + ec * 512:xb * D + ec * 512 + 512],
                                osb[:, xb * D + ec * 512:xb * D + ec * 512 + 512],
                                xtok_sb[:, xb * D + ec * 512:xb * D + ec * 512 + 512], ALU.add),
                                wait=[st_x] if ec == 0 else None, inc="vv",
                                rd=[("osb", xb * D + ec * 512, xb * D + ec * 512 + 512),
                                    ("xtok", xb * D + ec * 512, xb * D + ec * 512 + 512)],
                                wr=[("osb", xb * D + ec * 512, xb * D + ec * 512 + 512)])
                        v_row = cnt["vv"]
                        x_free[xb] = v_row
                        emit("sync", lambda e, qt=qt, xb=xb: e.dma_start(
                            out=out[qt * 128:qt * 128 + 128, :], in_=osb[:, xb * D:(xb + 1) * D]),
                            wait=[("vv", v_row)], inc="sou%d" % xb,
                            rd=[("osb", xb * D, (xb + 1) * D)],
                            wr=[("out", qt * 128, qt * 128 + 128)])
                        o_store[xb] = cnt["sou%d" % xb]
                    f0, f1 = cnt["sou0"], cnt["sou1"]
                    plan["sync"].append(lambda e, a=f0, b=f1: (
                        e.wait_ge(SEM["sou0"], a), e.wait_ge(SEM["sou1"], b)))

        def dump(tiles):
            for ap_, nm, off in tiles:
                emit("gpsimd", lambda e, ap_=ap_, off=off: e.dma_start(
                    out=dbg[0:ap_.shape[0], off:off + ap_.shape[-1]], in_=ap_),
                    wait=[("mm", cnt["mm"]), ("ss", cnt["ss"]),
                          ("cc", cnt["cc"]), ("vv", cnt["vv"])], inc="gd",
                    rd=[(nm, 0, 10 ** 9)], wr=[("dbg", off, off + ap_.shape[-1])])

        def debug_tail():
            with nc.sbuf_tensor("dbg_sb", [128, D], F32) as dbg_sb:
                emit("vector", lambda e: e.memset(dbg_sb[:, :], 1.0),
                     wait=[("mm", cnt["mm"]), ("ss", cnt["ss"]),
                           ("cc", cnt["cc"]), ("vv", cnt["vv"])],
                     inc="vv", wr=[("dbg_sb", 0, D)])
                vd = cnt["vv"]
                for qt in range(NQT):
                    emit("sync", lambda e, qt=qt: e.dma_start(
                        out=out[qt * 128:qt * 128 + 128, :], in_=dbg_sb[:, :]),
                        wait=[("vv", vd)], inc="sou0",
                        rd=[("dbg_sb", 0, D)], wr=[("out", qt * 128, qt * 128 + 128)])
                fs = cnt["sou0"]
                plan["sync"].append(lambda e, v=fs: e.wait_ge(SEM["sou0"], v))

        with (
            nc.sbuf_tensor("qT_bf", [128, HPC * S], BF16) as qT_bf,
            nc.sbuf_tensor("kT_bf", [128, HPC * S], BF16) as kT_bf,
            nc.sbuf_tensor("vtok_bf", [128, HPC * S], BF16) as vtok_bf,
        ):
            with (
                nc.sbuf_tensor("r_flat", [1, S], F32) as r_flat,
                nc.sbuf_tensor("negm_flat", [1, S], BF16) as negm_flat,
                nc.sbuf_tensor("rsum3", [1, 3 * FPC], BF16) as rsum3,
            ):
                with nc.sbuf_tensor("wqk_bf", [128, 2 * IT * FPC], BF16) as wqk_bf:
                    guards["wqk_bf"] = wqk_bf
                    wv_ = wqk_bf.rearrange("p (k i f) -> p k i f", k=2, i=IT)
                    emit("gpsimd", lambda e: e.dma_start(out=wv_[:, 0], in_=wqT_t),
                         wait=[consts_gp], inc="gw",
                         rd=[("wqT", 0, 1)], wr=[("wqk", 0, IT * FPC)])
                    emit("gpsimd", lambda e: e.dma_start(out=wv_[:, 1], in_=wkT_t), inc="gw",
                         rd=[("wkT", 0, 1)], wr=[("wqk", IT * FPC, 2 * IT * FPC)])
                    guards["wqk_gp"] = ("gw", cnt["gw"])
                    phase1_merged(qT_bf, kT_bf, r_flat, negm_flat, rsum3)
                phase1_v(vtok_bf, r_flat, negm_flat, rsum3)
            if upto == "p1p":
                dump([(qT_bf[:, :], "qT", 0), (kT_bf[:, :], "kT", HPC * S),
                      (vtok_bf[:, :], "vtok", 2 * HPC * S)])
            if upto not in ("p1s", "p1p"):
                with nc.sbuf_tensor("l_pack", [HPC, S], F32) as l_pack:
                    phase2(qT_bf, kT_bf, vtok_bf, l_pack)
                    if upto == "p2":
                        dump([(l_pack[0:HPC, :], "l_pack", 0)])
                    if upto != "p2":
                        phase3a(l_pack)
                        if upto == "p3s":
                            for c in range(NOC):
                                emit("gpsimd", lambda e, c=c: e.dma_start(
                                    out=dbg2[:, c * 512:(c + 1) * 512],
                                    in_=sm[c].ap()[:, :]),
                                    wait=[("cc", guards["rs_cc"][c])], inc="gd",
                                    rd=[(f"sm{c}", 0, TPC)], wr=[("dbg2", c, c + 1)])
        if upto is None or upto == "p3g":
            guards["dump_g"] = (upto == "p3g")
            phase3bc()
        if upto is not None and upto != "p3g":
            debug_tail()

        if check:
            reports = check_races()
            if reports:
                lines = [f"RACE {a}->{b} on {nm}: {ra} vs {rb}"
                         for (a, b, nm, ra, rb) in reports]
                raise AssertionError("race check failed:\n" + "\n".join(lines))

        with nc.Block() as block:
            @block.tensor
            def _(eng):
                for f in plan["tensor"]:
                    f(eng)

            @block.vector
            def _(eng):
                for f in plan["vector"]:
                    f(eng)

            @block.scalar
            def _(eng):
                for f in plan["scalar"]:
                    f(eng)

            @block.sync
            def _(eng):
                for f in plan["sync"]:
                    f(eng)

            @block.gpsimd
            def _(eng):
                for f in plan["gpsimd"]:
                    f(eng)

        _semstack.close()

    return nc



def make_inputs(x, q_proj, k_proj, v_proj, o_proj, ffn_w, ffn_scale, ffn_bias, HPC):
    B, S, D = x.shape
    FPC = HPC * 128
    TPC = S // 4
    ones128 = np.ones((1, 128), np.float32)
    onesc = np.ones((1, 1), np.float32)
    onesb = np.ones((128, 1), np.float32)
    expb_a = np.full((128, 1), EXP_SHIFT, np.float32)
    qsel = np.zeros((128, 64), np.float32)
    for qb in range(8):
        qsel[:, qb * 8 + qb] = 1.0
    lsel = np.zeros((128, HPC * 4), np.float32)
    for h in range(HPC):
        lsel[:, h * 4 + h] = 1.0
    maskc = np.zeros((4 * 128, 512), np.float32)
    for t in range(4):
        for p in range(128):
            maskc[t * 128 + p, :t * 128 + p] = NEG_BIG
    wqT = np.ascontiguousarray(q_proj.T)
    wkT = np.ascontiguousarray(k_proj.T)
    wvT = np.ascontiguousarray(v_proj.T)
    woT = np.ascontiguousarray(o_proj.T)
    wfT = np.ascontiguousarray(ffn_w.T)
    gam = np.ascontiguousarray(ffn_scale.reshape(1, D).astype(np.float32))
    bet = np.ascontiguousarray(ffn_bias.reshape(1, D).astype(np.float32))
    maps = []
    for core in range(8):
        b, g = core // 4, core % 4
        xb = x[b]
        xTb = np.ascontiguousarray(xb.T)
        maps.append({
            "xT": xTb,
            "xTmy": np.ascontiguousarray(xTb[:, g * TPC:(g + 1) * TPC]),
            "x_tok": np.ascontiguousarray(xb[g * TPC:(g + 1) * TPC]),
            "wqT": np.ascontiguousarray(wqT[:, g * FPC:(g + 1) * FPC]),
            "wkT": np.ascontiguousarray(wkT[:, g * FPC:(g + 1) * FPC]),
            "wvT": np.ascontiguousarray(wvT[:, g * FPC:(g + 1) * FPC]),
            "woTmy": np.ascontiguousarray(woT[g * FPC:(g + 1) * FPC, :]),
            "wfT": wfT,
            "gam_in": gam,
            "bet_in": bet,
            "ones128": ones128,
            "onesc": onesc,
            "onesb": onesb,
            "expb": expb_a,
            "qsel": qsel,
            "lsel": lsel,
            "maskc": maskc,
        })
    return maps


def assemble(results, B, S, D):
    TPC = S // 4
    full = np.empty((B, S, D), np.float32)
    for core in range(8):
        b, g = core // 4, core % 4
        full[b, g * TPC:(g + 1) * TPC] = results[core]["out"]
    return full



def kernel(L0_attn_in_features, L0_attn_q_proj, L0_attn_k_proj, L0_attn_v_proj,
           L0_attn_o_proj, L0_ffn_weight, L0_ffn_scale_weight, L0_ffn_bias):
    x = np.asarray(L0_attn_in_features)
    B, S, D = x.shape
    HPC = (D // 128) // 4
    nc = build_nc(D, S, HPC)
    maps = make_inputs(x, np.asarray(L0_attn_q_proj), np.asarray(L0_attn_k_proj),
                       np.asarray(L0_attn_v_proj), np.asarray(L0_attn_o_proj),
                       np.asarray(L0_ffn_weight), np.asarray(L0_ffn_scale_weight),
                       np.asarray(L0_ffn_bias), HPC)
    res = run_bass_kernel_spmd(nc, maps, core_ids=list(range(8)))
    return assemble(res.results, B, S, D)


# version 13
# speedup vs baseline: 1.0152x; 1.0152x over previous
import numpy as np
import concourse.bass as bass
import concourse.mybir as mybir
from concourse.bass_utils import run_bass_kernel_spmd

F32 = mybir.dt.float32
BF16 = mybir.dt.bfloat16
AF = mybir.ActivationFunctionType
ALU = mybir.AluOpType

NEG_BIG = -1.0e30
EXP_SHIFT = -12.0
LN_EPS = 1e-5
RG4 = [[0, 1, 2, 3], [4, 5, 6, 7]]
QNAMES = ("tensor", "vector", "scalar", "sync", "gpsimd")


def build_nc(D, S, HPC, upto=None, check=True):
    IT = D // 128
    NQB = S // 512
    TPC = S // 4
    NQT = TPC // 128
    NKT = S // 128
    FPC = HPC * 128
    NDT = FPC // 128
    NCH = (FPC + 511) // 512
    NOC = D // 512
    MYQB = TPC // 512
    SCALE = float(1.0 / np.sqrt(128.0))
    assert NQB <= 8 and 1 <= MYQB <= 8

    nc = bass.Bass("TRN2", target_bir_lowering=False, detect_race_conditions=False)

    xT = nc.dram_tensor("xT", [D, S], F32, kind="ExternalInput").ap()
    xTmy = nc.dram_tensor("xTmy", [D, TPC], F32, kind="ExternalInput").ap()
    x_tok = nc.dram_tensor("x_tok", [TPC, D], F32, kind="ExternalInput").ap()
    wqT = nc.dram_tensor("wqT", [D, FPC], F32, kind="ExternalInput").ap()
    wkT = nc.dram_tensor("wkT", [D, FPC], F32, kind="ExternalInput").ap()
    wvT = nc.dram_tensor("wvT", [D, FPC], F32, kind="ExternalInput").ap()
    woTmy = nc.dram_tensor("woTmy", [FPC, D], F32, kind="ExternalInput").ap()
    wfT = nc.dram_tensor("wfT", [D, D], F32, kind="ExternalInput").ap()
    gam_in = nc.dram_tensor("gam_in", [1, D], F32, kind="ExternalInput").ap()
    bet_in = nc.dram_tensor("bet_in", [1, D], F32, kind="ExternalInput").ap()
    ones128 = nc.dram_tensor("ones128", [1, 128], F32, kind="ExternalInput").ap()
    onesc = nc.dram_tensor("onesc", [1, 1], F32, kind="ExternalInput").ap()
    onesb = nc.dram_tensor("onesb", [128, 1], F32, kind="ExternalInput").ap()
    expb = nc.dram_tensor("expb", [128, 1], F32, kind="ExternalInput").ap()
    qsel = nc.dram_tensor("qsel", [128, 64], F32, kind="ExternalInput").ap()
    lsel = nc.dram_tensor("lsel", [128, HPC * 4], F32, kind="ExternalInput").ap()
    maskc = nc.dram_tensor("maskc", [4 * 128, 512], F32, kind="ExternalInput").ap()
    out = nc.dram_tensor("out", [TPC, D], F32, kind="ExternalOutput").ap()

    dbg = (nc.dram_tensor("dbg", [128, max(3 * HPC * S, NQT * D)], F32,
                          kind="ExternalOutput").ap() if upto else None)
    dbg2 = (nc.dram_tensor("dbg2", [TPC, D], F32, kind="ExternalOutput").ap()
            if upto == "p3s" else None)
    o_dram = nc.dram_tensor("o_dram", [FPC, S], BF16)
    sp = [nc.dram_tensor(f"s_part{c}", [S, 512], BF16) for c in range(NOC)]
    sm = [nc.dram_tensor(f"s_my{c}", [TPC, 512], BF16) for c in range(NOC)]

    xT_t = xT.rearrange("(i p) s -> p i s", p=128)
    xTmy_t = xTmy.rearrange("(i p) s -> p i s", p=128)
    wqT_t = wqT.rearrange("(i p) f -> p i f", p=128)
    wkT_t = wkT.rearrange("(i p) f -> p i f", p=128)
    wvT_t = wvT.rearrange("(i p) f -> p i f", p=128)
    woTmy_t = woTmy.rearrange("(i p) f -> p i f", p=128)
    wfT_t = wfT.rearrange("(i p) f -> p i f", p=128)
    maskc_t = maskc.rearrange("(t p) j -> p t j", p=128)
    o_dram_t = o_dram.ap().rearrange("(h p) s -> p h s", p=128)

    plan = {n: [] for n in QNAMES}
    COMPUTE_SEMS = ("mm", "vv", "ss", "cc")
    DMA_STREAMS = ("sc", "sgb", "so0", "so1", "sosh", "spe0", "spe1", "spe2", "spe3",
                   "sst0", "sst1", "sxt0", "sxt1", "sou0", "sou1",
                   "gc", "gx0", "gx1", "gw", "gwc0", "gwc1", "gwc2", "gwc3", "gf",
                   "gil", "gm", "gq", "gd")
    cnt = {k: 0 for k in COMPUTE_SEMS + DMA_STREAMS}
    SEM = {}
    guards = {}
    nodes = []
    sem_hist = {k: [] for k in cnt}

    def emit(q, fn, wait=None, inc=None, rd=(), wr=()):
        w = tuple((s, v) for (s, v) in (wait or ()) if v > 0)
        amt = 0
        if inc is not None:
            amt = 1 if inc in COMPUTE_SEMS else 16
            cnt[inc] += amt
        nid = len(nodes)
        nodes.append((nid, q, w, tuple(rd), tuple(wr)))
        if inc is not None:
            sem_hist[inc].append((cnt[inc], nid))

        def go(eng, fn=fn, w=w, inc=inc, amt=amt):
            for sname, val in w:
                eng.wait_ge(SEM[sname], val)
            r = fn(eng)
            if inc is not None:
                r.then_inc(SEM[inc], amt)
        plan[q].append(go)
        return cnt[inc] if inc is not None else None

    def check_races(max_report=40):
        qi = {n: i for i, n in enumerate(QNAMES)}

        def wait_node(s, v):
            lo, hi = 0, len(sem_hist[s])
            h = sem_hist[s]
            while lo < hi:
                mid = (lo + hi) // 2
                if h[mid][0] >= v:
                    hi = mid
                else:
                    lo = mid + 1
            return h[lo][1] if lo < len(h) else None

        pos = {}
        counters = {n: 0 for n in QNAMES}
        for nid, q, w, rd_, wr_ in nodes:
            pos[nid] = counters[q]
            counters[q] += 1
        NQ = len(QNAMES)
        vcs = [None] * len(nodes)
        last = {n: None for n in QNAMES}
        for nid, q, w, rd_, wr_ in nodes:
            vc = list(vcs[last[q]]) if last[q] is not None else [-1] * NQ
            for (s, v) in w:
                wn = wait_node(s, v)
                if wn is not None:
                    wv = vcs[wn]
                    for i in range(NQ):
                        if wv[i] > vc[i]:
                            vc[i] = wv[i]
                    wq = nodes[wn][1]
                    if pos[wn] > vc[qi[wq]]:
                        vc[qi[wq]] = pos[wn]
            vc[qi[q]] = pos[nid]
            vcs[nid] = vc
            last[q] = nid

        by_name = {}
        for nid, q, w, rd_, wr_ in nodes:
            for kind, accs in (("r", rd_), ("w", wr_)):
                for acc in accs:
                    by_name.setdefault(acc[0], []).append((nid, kind, acc[1], acc[2]))
        reports = []
        for name, lst in by_name.items():
            lst.sort()
            for i in range(len(lst)):
                ai, ak, a0, a1 = lst[i]
                for j in range(i + 1, len(lst)):
                    bi, bk, b0, b1 = lst[j]
                    if ak == "r" and bk == "r":
                        continue
                    if a1 <= b0 or b1 <= a0:
                        continue
                    qa, qb_ = nodes[ai][1], nodes[bi][1]
                    if qa == qb_:
                        continue
                    if vcs[bi][qi[qa]] >= pos[ai]:
                        continue
                    reports.append((ai, bi, name, (a0, a1, ak, qa), (b0, b1, bk, qb_)))
                    if len(reports) >= max_report:
                        return reports
        return reports

    with (
        nc.sbuf_tensor("r_tok", [128, NKT], F32) as r_tok,
        nc.sbuf_tensor("ones128_sb", [1, 128], F32) as ones128_sb,
        nc.sbuf_tensor("onesc_sb", [1, 1], F32) as onesc_sb,
        nc.sbuf_tensor("onesb_bf", [128, 1], BF16) as onesb_bf,
        nc.sbuf_tensor("expb_sb", [128, 1], F32) as expb_sb,
        nc.sbuf_tensor("qsel_bf", [128, 64], BF16) as qsel_bf,
        nc.sbuf_tensor("lsel_bf", [128, HPC * 4], BF16) as lsel_bf,
        nc.psum_tensor("ps", [128, 4096], F32) as ps,
    ):
        import contextlib
        _semstack = contextlib.ExitStack()
        for _k in COMPUTE_SEMS + DMA_STREAMS:
            SEM[_k] = _semstack.enter_context(nc.semaphore("sem_" + _k))

        def bank(i, w=1):
            return ps[:, i * 512:(i + w) * 512]

        def PS(i, w=1):
            return ("ps", i, i + w)

        spk_box = [None]

        def spr(k, rows=8):
            return spk_box[0][0:rows, k * 512:(k + 1) * 512]

        def SPR(k):
            return ("spk", k * 512, (k + 1) * 512)

        def emit_stats(loader, xq_sl, xq_res, sq_dst, sq_src, sq_sl, sq_res, nqb,
                       extra_wait_first):
            sq_guard = 0
            prevs = [(0, 0), (0, 0)]
            for qb in range(nqb):
                gp_q = loader(qb, prevs[qb % 2][0], prevs[qb % 2][1])
                for i in range(IT):
                    w0 = None
                    if i == 0:
                        w0 = [gp_q] + (extra_wait_first if qb == 0 else [])
                    emit("tensor", lambda e, i=i, q=qb: e.matmul(
                        ps[0:8, 2 * 512:3 * 512], qsel_bf[:, q * 8:q * 8 + 8],
                        xq_sl(q, i),
                        start=(q == 0 and i == 0), stop=(q == nqb - 1 and i == IT - 1),
                        skip_group_check=True),
                        wait=w0, inc="mm" if i == IT - 1 else None,
                        rd=[("qsel", 0, 64), xq_res(qb, i)], wr=[PS(2)])
                for c in range(IT // 4):
                    emit("scalar", lambda e, q=qb, c=c: e.activation(
                        sq_dst(q, c), sq_src(q, c), AF.Square),
                        wait=[gp_q, ("mm", sq_guard)], inc="ss",
                        rd=[xq_res(qb, 4 * c), xq_res(qb, 4 * c + 1),
                            xq_res(qb, 4 * c + 2), xq_res(qb, 4 * c + 3)],
                        wr=[sq_res(0, 4)])
                    ss_here = cnt["ss"]
                    for j in range(4):
                        ig = c * 4 + j
                        emit("tensor", lambda e, j=j, q=qb, ig=ig: e.matmul(
                            ps[0:8, 3 * 512:4 * 512], qsel_bf[:, q * 8:q * 8 + 8],
                            sq_sl(j),
                            start=(q == 0 and ig == 0), stop=(q == nqb - 1 and ig == IT - 1),
                            skip_group_check=True),
                            wait=[("ss", ss_here)] if j == 0 else None,
                            inc="mm" if j == 3 else None,
                            rd=[("qsel", 0, 64), sq_res(j, j + 1)], wr=[PS(3)])
                    sq_guard = cnt["mm"]
                prevs[qb % 2] = (cnt["mm"], cnt["ss"])
            stats_mm = cnt["mm"]
            emit("vector", lambda e: e.tensor_copy(spr(0, nqb), ps[0:nqb, 2 * 512:3 * 512]),
                 wait=[("mm", stats_mm)], rd=[PS(2)], wr=[SPR(0)])
            emit("vector", lambda e: e.tensor_copy(spr(1, nqb), ps[0:nqb, 3 * 512:4 * 512]),
                 inc="vv", rd=[PS(3)], wr=[SPR(1)])
            ps_free = cnt["vv"]
            emit("vector", lambda e: e.tensor_scalar(spr(2, nqb), spr(0, nqb), 1.0 / D, None, ALU.mult),
                 rd=[SPR(0)], wr=[SPR(2)])
            emit("vector", lambda e: e.tensor_tensor(spr(3, nqb), spr(2, nqb), spr(2, nqb), ALU.mult),
                 rd=[SPR(2)], wr=[SPR(3)])
            emit("vector", lambda e: e.tensor_scalar(spr(4, nqb), spr(1, nqb), 1.0 / D, None, ALU.mult),
                 rd=[SPR(1)], wr=[SPR(4)])
            emit("vector", lambda e: e.tensor_tensor(spr(4, nqb), spr(4, nqb), spr(3, nqb), ALU.subtract),
                 rd=[SPR(4), SPR(3)], wr=[SPR(4)])
            emit("vector", lambda e: e.tensor_scalar(spr(4, nqb), spr(4, nqb), LN_EPS, None, ALU.add),
                 rd=[SPR(4)], wr=[SPR(4)])
            emit("vector", lambda e: e.reciprocal(spr(3, nqb), spr(4, nqb)), inc="vv",
                 rd=[SPR(4)], wr=[SPR(3)])
            v_re = cnt["vv"]
            emit("scalar", lambda e: e.activation(spr(4, nqb), spr(3, nqb), AF.Sqrt),
                 wait=[("vv", v_re)], inc="ss", rd=[SPR(3)], wr=[SPR(4)])
            ss_r = cnt["ss"]
            emit("vector", lambda e: e.tensor_scalar(spr(3, nqb), spr(2, nqb), -1.0, None, ALU.mult),
                 wait=[("ss", ss_r)], inc="vv", rd=[SPR(2)], wr=[SPR(3)])
            v_nm = cnt["vv"]
            return stats_mm, ps_free, ss_r, v_nm

        emit("sync", lambda e: e.dma_start(out=ones128_sb[:, :], in_=ones128[:, :]),
             inc="sc", wr=[("ones128_sb", 0, 128)])
        emit("sync", lambda e: e.dma_start(out=onesc_sb[:, :], in_=onesc[:, :]),
             inc="sc", wr=[("onesc_sb", 0, 1)])
        emit("sync", lambda e: e.dma_start(out=expb_sb[:, :], in_=expb[:, :]),
             inc="sc", wr=[("expb_sb", 0, 1)])
        consts_st = ("sc", cnt["sc"])
        emit("gpsimd", lambda e: e.dma_start(out=onesb_bf[:, :], in_=onesb[:, :]),
             inc="gc", wr=[("onesb_bf", 0, 1)])
        emit("gpsimd", lambda e: e.dma_start(out=qsel_bf[:, :], in_=qsel[:, :]),
             inc="gc", wr=[("qsel", 0, 64)])
        emit("gpsimd", lambda e: e.dma_start(out=lsel_bf[:, :], in_=lsel[:, :]),
             inc="gc", wr=[("lsel", 0, HPC * 4)])
        consts_gp = ("gc", cnt["gc"])

        def phase1_merged(qT_bf, kT_bf, r_flat, negm_flat, rsum3):
            wqk_bf = guards["wqk_bf"]
            wqk_gp = guards["wqk_gp"]
            with (
                nc.sbuf_tensor("xq2_bf", [128, 2 * IT * 512], BF16) as xq2_bf,
                nc.sbuf_tensor("sq_bf", [128, 1024], BF16) as sq_bf,
                nc.sbuf_tensor("rb_sb", [128, 512], F32) as rb_sb,
                nc.sbuf_tensor("t_sum", [1, 2, 512], F32) as t_sum,
                nc.sbuf_tensor("t_sq", [1, 2, 512], F32) as t_sq,
                nc.sbuf_tensor("r_row", [1, 2, 512], F32) as r_row,
            ):
                def wsl(k, i):
                    o = (k * IT + i) * FPC
                    return wqk_bf[:, o:o + FPC]

                def WSL(k, i):
                    o = (k * IT + i) * FPC
                    return ("wqk", o, o + FPC)

                def xq2(b, i):
                    return xq2_bf[:, (b * IT + i) * 512:(b * IT + i) * 512 + 512]

                def XQ2(b, i):
                    o = (b * IT + i) * 512
                    return ("xq2", o, o + 512)

                rs_free = 0
                for k in range(2):
                    for i in range(IT):
                        w0 = None
                        if i == 0:
                            w0 = [wqk_gp, ("vv", rs_free)]
                        emit("tensor", lambda e, k=k, i=i: e.matmul(
                            ps[0:1, 7 * 512:7 * 512 + FPC], onesb_bf[:, :], wsl(k, i),
                            start=(i == 0), stop=(i == IT - 1), skip_group_check=True),
                            wait=w0, inc="mm" if i == IT - 1 else None,
                            rd=[("onesb_bf", 0, 1), WSL(k, i)], wr=[PS(7)])
                    mm_rs = cnt["mm"]
                    emit("vector", lambda e, k=k: e.tensor_copy(
                        rsum3[0:1, k * FPC:(k + 1) * FPC], ps[0:1, 7 * 512:7 * 512 + FPC]),
                        wait=[("mm", mm_rs)], inc="vv",
                        rd=[PS(7)], wr=[("rsum3", k * FPC, (k + 1) * FPC)])
                    rs_free = cnt["vv"]
                bank7_free = rs_free

                QKB = [0, 1, 4, 5, 6]
                bank_free = {b: 0 for b in QKB}
                stats_free = 0
                xbuf = [(0, 0), (0, 0)]
                rtok_vv = 0
                bank7_box = [bank7_free]
                qb_math = {}

                def emit_load_stats(qb):
                    nonlocal stats_free
                    b = qb % 2
                    emit("gpsimd", lambda e, b=b, q=qb: e.dma_start(
                        out=xq2_bf.rearrange("p (b i f) -> p b i f", b=2, i=IT)[:, b],
                        in_=xT_t[:, :, q * 512:q * 512 + 512]),
                        wait=[("mm", xbuf[b][0]), ("ss", xbuf[b][1]), consts_gp],
                        inc="gx%d" % b,
                        rd=[("xT", 0, 1)], wr=[("xq2", b * IT * 512, (b + 1) * IT * 512)])
                    gp_here = ("gx%d" % b, cnt["gx%d" % b])
                    for i in range(IT):
                        w0 = None
                        if i == 0:
                            w0 = [gp_here, ("vv", stats_free), consts_gp]
                        emit("tensor", lambda e, i=i, b=b: e.matmul(
                            ps[0:1, 2 * 512:3 * 512], onesb_bf[:, :], xq2(b, i),
                            start=(i == 0), stop=(i == IT - 1), skip_group_check=True),
                            wait=w0, inc="mm" if i == IT - 1 else None,
                            rd=[("onesb_bf", 0, 1), XQ2(b, i)], wr=[PS(2)])
                    mm_sum = cnt["mm"]
                    sq_guard_local = 0
                    for c in range(IT // 2):
                        emit("scalar", lambda e, b=b, c=c: e.activation(
                            sq_bf[:, :],
                            xq2_bf[:, (b * IT) * 512 + c * 1024:(b * IT) * 512 + (c + 1) * 1024],
                            AF.Square),
                            wait=[gp_here, ("mm", sq_guard_local)], inc="ss",
                            rd=[XQ2(b, 2 * c), XQ2(b, 2 * c + 1)],
                            wr=[("sq1", 0, 1024)])
                        ss_here = cnt["ss"]
                        for j in range(2):
                            emit("tensor", lambda e, j=j, c=c: e.matmul(
                                ps[0:1, 3 * 512:4 * 512], onesb_bf[:, :],
                                sq_bf[:, j * 512:(j + 1) * 512],
                                start=(c == 0 and j == 0), stop=(c == IT // 2 - 1 and j == 1),
                                skip_group_check=True),
                                wait=[("ss", ss_here)] if j == 0 else None,
                                inc="mm" if j == 1 else None,
                                rd=[("onesb_bf", 0, 1), ("sq1", j * 512, (j + 1) * 512)],
                                wr=[PS(3)])
                        sq_guard_local = cnt["mm"]
                    mm_sq = cnt["mm"]
                    emit("vector", lambda e, q=qb: e.tensor_copy(
                        t_sum[0:1, q % 2, :], ps[0:1, 2 * 512:3 * 512]),
                        wait=[("mm", mm_sum)], rd=[PS(2)], wr=[("t_sum", (qb % 2) * 512, (qb % 2) * 512 + 512)])
                    emit("vector", lambda e, q=qb: e.tensor_copy(
                        t_sq[0:1, q % 2, :], ps[0:1, 3 * 512:4 * 512]),
                        wait=[("mm", mm_sq)], inc="vv",
                        rd=[PS(3)], wr=[("t_sq", (qb % 2) * 512, (qb % 2) * 512 + 512)])
                    stats_free = cnt["vv"]
                    p = qb % 2
                    TSUM = ("t_sum", p * 512, p * 512 + 512)
                    TSQ = ("t_sq", p * 512, p * 512 + 512)
                    RROW = ("r_row", p * 512, p * 512 + 512)
                    emit("vector", lambda e, p=p: e.tensor_scalar(
                        t_sum[0:1, p, :], t_sum[0:1, p, :], 1.0 / D, None, ALU.mult),
                        rd=[TSUM], wr=[TSUM])
                    emit("vector", lambda e, p=p: e.scalar_tensor_tensor(
                        r_row[0:1, p, :], t_sum[0:1, p, :], -1.0, t_sum[0:1, p, :],
                        ALU.mult, ALU.mult),
                        rd=[TSUM], wr=[RROW])
                    emit("vector", lambda e, p=p: e.tensor_scalar(
                        t_sq[0:1, p, :], t_sq[0:1, p, :], 1.0 / D, None, ALU.mult),
                        rd=[TSQ], wr=[TSQ])
                    emit("vector", lambda e, p=p: e.tensor_tensor(
                        t_sq[0:1, p, :], t_sq[0:1, p, :], r_row[0:1, p, :], ALU.add),
                        rd=[TSQ, RROW], wr=[TSQ])
                    emit("vector", lambda e, p=p: e.tensor_scalar(
                        t_sq[0:1, p, :], t_sq[0:1, p, :], LN_EPS, None, ALU.add),
                        rd=[TSQ], wr=[TSQ])
                    emit("vector", lambda e, p=p: e.reciprocal(
                        t_sq[0:1, p, :], t_sq[0:1, p, :]), inc="vv",
                        rd=[TSQ], wr=[TSQ])
                    v_iv = cnt["vv"]
                    emit("scalar", lambda e, p=p: e.activation(
                        r_row[0:1, p, :], t_sq[0:1, p, :], AF.Sqrt),
                        wait=[("vv", v_iv)], inc="ss",
                        rd=[TSQ], wr=[RROW])
                    ss_r = cnt["ss"]
                    emit("vector", lambda e, q=qb, p=p: e.tensor_scalar(
                        negm_flat[0:1, q * 512:q * 512 + 512], t_sum[0:1, p, :], -1.0, None, ALU.mult),
                        inc="vv",
                        rd=[TSUM],
                        wr=[("negm_flat", qb * 512, qb * 512 + 512)])
                    qb_math[qb] = (ss_r, cnt["vv"])

                def emit_fills(qb):
                    nonlocal rtok_vv
                    b = qb % 2
                    p = qb % 2
                    gp_here = ("gx%d" % b, cnt["gx%d" % b])
                    ss_r, v_negm = qb_math[qb]
                    RROW = ("r_row", p * 512, p * 512 + 512)
                    emit("tensor", lambda e, p=p: e.matmul(
                        bank(7), ones128_sb[:, :], r_row[0:1, p, :],
                        start=True, stop=True, skip_group_check=True),
                        wait=[("ss", ss_r), ("vv", bank7_box[0]), consts_st], inc="mm",
                        rd=[("ones128_sb", 0, 128), RROW], wr=[PS(7)])
                    mm_rb = cnt["mm"]
                    emit("vector", lambda e: e.tensor_copy(
                        rb_sb[:, :], bank(7)),
                        wait=[("mm", mm_rb)], inc="vv",
                        rd=[PS(7)], wr=[("rb_sb", 0, 512)])
                    v_rb = cnt["vv"]
                    fills = [(k, dt) for k in range(2) for dt in range(NDT)]
                    DEPTH = 4

                    def emit_raw(fi):
                        k, dt = fills[fi]
                        bk = QKB[fi % 5]
                        for i in range(IT):
                            w0 = None
                            if i == 0:
                                w0 = [gp_here, ("vv", bank_free[bk])]
                            emit("tensor", lambda e, k=k, i=i, dt=dt, b=b, bk=bk: e.matmul(
                                bank(bk), wsl(k, i)[:, dt * 128:dt * 128 + 128], xq2(b, i),
                                start=(i == 0), stop=False, skip_group_check=True),
                                wait=w0, rd=[WSL(k, i), XQ2(b, i)], wr=[PS(bk)])

                    def emit_r1_evac(fi):
                        k, dt = fills[fi]
                        bk = QKB[fi % 5]
                        dstT, dstN = (qT_bf, "qT") if k == 0 else (kT_bf, "kT")
                        emit("tensor", lambda e, k=k, dt=dt, q=qb, bk=bk: e.matmul(
                            bank(bk), rsum3[0:1, k * FPC + dt * 128:k * FPC + dt * 128 + 128],
                            negm_flat[0:1, q * 512:q * 512 + 512],
                            start=False, stop=True, skip_group_check=True),
                            wait=[("vv", v_negm)], inc="mm",
                            rd=[("rsum3", k * FPC + dt * 128, k * FPC + dt * 128 + 128),
                                ("negm_flat", qb * 512, qb * 512 + 512)],
                            wr=[PS(bk)])
                        mm_fill = cnt["mm"]
                        emit("vector", lambda e, dt=dt, q=qb, bk=bk, dstT=dstT: e.tensor_tensor(
                            dstT[:, dt * S + q * 512:dt * S + q * 512 + 512],
                            bank(bk), rb_sb[:, :], ALU.mult),
                            wait=[("mm", mm_fill), ("vv", v_rb)], inc="vv",
                            rd=[PS(bk), ("rb_sb", 0, 512)],
                            wr=[(dstN, dt * S + qb * 512, dt * S + qb * 512 + 512)])
                        bank_free[bk] = cnt["vv"]

                    for fi in range(len(fills)):
                        emit_raw(fi)
                        if fi >= DEPTH:
                            emit_r1_evac(fi - DEPTH)
                        if fi == 1 and qb + 1 < NQB:
                            emit_load_stats(qb + 1)
                    for fi in range(len(fills) - DEPTH, len(fills)):
                        emit_r1_evac(fi)
                    for tt in range(4):
                        gtt = qb * 4 + tt
                        emit("tensor", lambda e, tt=tt, p=p: e.matmul(
                            ps[:, 7 * 512 + tt:7 * 512 + tt + 1],
                            r_row[0:1, p, tt * 128:tt * 128 + 128], onesc_sb[:, :],
                            start=True, stop=True, skip_group_check=True),
                            wait=[("vv", v_rb)] if tt == 0 else None,
                            inc="mm" if tt == 3 else None,
                            rd=[("r_row", p * 512 + tt * 128, p * 512 + tt * 128 + 128),
                                ("onesc_sb", 0, 1)],
                            wr=[PS(7)])
                    mm_rt = cnt["mm"]
                    emit("vector", lambda e, q=qb: e.tensor_copy(
                        r_tok[:, q * 4:q * 4 + 4], ps[:, 7 * 512:7 * 512 + 4]),
                        wait=[("mm", mm_rt)], inc="vv",
                        rd=[PS(7)], wr=[("r_tok", qb * 4, qb * 4 + 4)])
                    bank7_box[0] = cnt["vv"]
                    rtok_vv = cnt["vv"]
                    xbuf[b] = (cnt["mm"], cnt["ss"])

                emit_load_stats(0)
                for qb in range(NQB):
                    emit_fills(qb)
                guards["v_rtok"] = rtok_vv
                guards["qk_mm"] = cnt["mm"]
                guards["qk_vv"] = cnt["vv"]


        def phase1_v(vtok_bf, r_flat, negm_flat, rsum3):
            v_rtok = guards["v_rtok"]
            if True:
                with (
                    nc.sbuf_tensor("wv_bf", [128, IT * FPC], BF16) as wv_bf,
                    nc.sbuf_tensor("xq3_bf", [128, 2 * IT * 512], BF16) as xq3_bf,
                ):
                    def wvs(i):
                        return wv_bf[:, i * FPC:(i + 1) * FPC]

                    def WVS(i):
                        return ("wv", i * FPC, (i + 1) * FPC)

                    def xq3(b, i):
                        return xq3_bf[:, (b * IT + i) * 512:(b * IT + i) * 512 + 512]

                    def XQ3(b, i):
                        o = (b * IT + i) * 512
                        return ("xq3", o, o + 512)

                    emit("gpsimd", lambda e: e.dma_start(
                        out=wv_bf.rearrange("p (i f) -> p i f", i=IT), in_=wvT_t),
                        wait=[("mm", guards["qk_mm"]), ("vv", guards["qk_vv"])], inc="gw",
                        rd=[("wvT", 0, 1)], wr=[("wv", 0, IT * FPC)])
                    wv_gp = ("gw", cnt["gw"])
                    rs_free = 0
                    for ch in range(NCH):
                        w = min(512, FPC - ch * 512)
                        for i in range(IT):
                            w0 = None
                            if i == 0:
                                w0 = [wv_gp, ("vv", rs_free)]
                            emit("tensor", lambda e, i=i, ch=ch, w=w: e.matmul(
                                ps[0:1, 5 * 512:5 * 512 + w], onesb_bf[:, :],
                                wvs(i)[:, ch * 512:ch * 512 + w],
                                start=(i == 0), stop=(i == IT - 1), skip_group_check=True),
                                wait=w0, inc="mm" if i == IT - 1 else None,
                                rd=[("onesb_bf", 0, 1), WVS(i)], wr=[PS(5)])
                        mm_rs = cnt["mm"]
                        emit("vector", lambda e, ch=ch, w=w: e.tensor_copy(
                            rsum3[0:1, 2 * FPC + ch * 512:2 * FPC + ch * 512 + w],
                            ps[0:1, 5 * 512:5 * 512 + w]),
                            wait=[("mm", mm_rs)], inc="vv",
                            rd=[PS(5)], wr=[("rsum3", 2 * FPC, 3 * FPC)])
                        rs_free = cnt["vv"]
                    v_rsum_v = cnt["vv"]
                    bank_free = {2: 0, 3: 0}
                    xbuf_free = [guards["qk_mm"], guards["qk_mm"]]
                    for qb in range(NQB):
                        b = qb % 2
                        emit("gpsimd", lambda e, b=b, q=qb: e.dma_start(
                            out=xq3_bf.rearrange("p (b i f) -> p b i f", b=2, i=IT)[:, b],
                            in_=xT_t[:, :, q * 512:q * 512 + 512]),
                            wait=[("mm", xbuf_free[b]), ("vv", guards["qk_vv"])], inc="gx%d" % b,
                            rd=[("xT", 0, 1)], wr=[("xq3", b * IT * 512, (b + 1) * IT * 512)])
                        gp_here = ("gx%d" % b, cnt["gx%d" % b])
                        for tt in range(4):
                            gtt = qb * 4 + tt
                            bk = 2 + (tt % 2)
                            for ch in range(NCH):
                                w = min(512, FPC - ch * 512)
                                for i in range(IT):
                                    w0 = None
                                    if i == 0 and ch == 0:
                                        w0 = [gp_here,
                                              ("vv", max(bank_free[bk], v_rsum_v))]
                                    emit("tensor", lambda e, i=i, tt=tt, b=b, bk=bk, ch=ch, w=w: e.matmul(
                                        bank(bk)[:, ch * 512:ch * 512 + w],
                                        xq3(b, i)[:, tt * 128:tt * 128 + 128],
                                        wvs(i)[:, ch * 512:ch * 512 + w],
                                        start=(i == 0), stop=False, skip_group_check=True),
                                        wait=w0, rd=[XQ3(b, i), WVS(i)], wr=[PS(bk)])
                                emit("tensor", lambda e, gtt=gtt, bk=bk, ch=ch, w=w: e.matmul(
                                    bank(bk)[:, ch * 512:ch * 512 + w],
                                    negm_flat[0:1, gtt * 128:gtt * 128 + 128],
                                    rsum3[0:1, 2 * FPC + ch * 512:2 * FPC + ch * 512 + w],
                                    start=False, stop=True, skip_group_check=True),
                                    inc="mm" if ch == NCH - 1 else None,
                                    rd=[("negm_flat", gtt * 128, gtt * 128 + 128),
                                        ("rsum3", 2 * FPC, 3 * FPC)],
                                    wr=[PS(bk)])
                            mm_fill = cnt["mm"]
                            emit("vector", lambda e, gtt=gtt, bk=bk: e.tensor_scalar(
                                vtok_bf.rearrange("p (h s) -> p h s", h=HPC)[:, :, gtt * 128:gtt * 128 + 128],
                                bank(bk)[:, 0:FPC].rearrange("p (h d) -> p h d", h=HPC),
                                r_tok[:, gtt:gtt + 1], None, ALU.mult),
                                wait=[("mm", mm_fill), ("vv", v_rtok)], inc="vv",
                                rd=[PS(bk), ("r_tok", gtt, gtt + 1)],
                                wr=[("vtok", h * S + gtt * 128, h * S + gtt * 128 + 128)
                                    for h in range(HPC)])
                            bank_free[bk] = cnt["vv"]
                        xbuf_free[b] = cnt["mm"]
                    guards["proj_mm"] = cnt["mm"]
                    guards["proj_vv"] = cnt["vv"]

        def phase2(qT_bf, kT_bf, vtok_bf, l_pack):
            with (
                nc.sbuf_tensor("p_bf", [128, NKT * 512], BF16) as p_bf,
                nc.sbuf_tensor("mask_bf", [128, 4 * 512], BF16) as mask_bf,
                nc.sbuf_tensor("ostg2", [128, 2 * 512], BF16) as ostg2,
            ):
                emit("gpsimd", lambda e: e.dma_start(
                    out=mask_bf.rearrange("p (t j) -> p t j", t=4), in_=maskc_t),
                    wait=[("mm", guards["proj_mm"]), ("vv", guards["proj_vv"])], inc="gm",
                    rd=[("maskc", 0, 1)], wr=[("mask", 0, 4 * 512)])
                gp_mask = ("gm", cnt["gm"])
                o_free = [0]
                l_free = [0]
                stg_free = [0, 0]
                first = [True]
                groups = []
                for qb in range(NQB):
                    ng = (4 * qb + 4) // 2
                    for h in range(HPC):
                        for j in range(ng):
                            groups.append((qb, h, j, ng))
                gstate = {}

                SPAIRS = [0, 2, 6]

                def emit_qk_mask_exp(gi):
                    qb, h, j, ng = groups[gi]
                    pair = SPAIRS[gi % 3]
                    pb = 2048 if (qb == 0 and h % 2 == 1) else 0
                    for t2 in range(2):
                        kt = 2 * j + t2
                        w0 = None
                        if first[0]:
                            w0 = [("vv", guards["proj_vv"]), gp_mask, consts_st]
                            first[0] = False
                        emit("tensor", lambda e, h=h, qb=qb, kt=kt, pair=pair, t2=t2: e.matmul(
                            bank(pair + t2),
                            kT_bf[:, h * S + kt * 128:h * S + kt * 128 + 128],
                            qT_bf[:, h * S + qb * 512:h * S + qb * 512 + 512],
                            start=True, stop=True, skip_group_check=True),
                            wait=w0, inc="mm" if t2 == 1 else None,
                            rd=[("kT", h * S + kt * 128, h * S + kt * 128 + 128),
                                ("qT", h * S + qb * 512, h * S + qb * 512 + 512)],
                            wr=[PS(pair + t2)])
                    mm_qk = cnt["mm"]
                    if 2 * j >= 4 * qb:
                        for t2 in range(2):
                            tloc = 2 * j + t2 - 4 * qb
                            emit("vector", lambda e, pair=pair, t2=t2, tloc=tloc: e.tensor_tensor(
                                bank(pair + t2), bank(pair + t2),
                                mask_bf[:, tloc * 512:tloc * 512 + 512], ALU.add),
                                wait=[("mm", mm_qk)] if t2 == 0 else None,
                                inc="vv" if t2 == 1 else None,
                                rd=[PS(pair + t2), ("mask", tloc * 512, tloc * 512 + 512)],
                                wr=[PS(pair + t2)])
                        wexp = [("vv", cnt["vv"])]
                    else:
                        wexp = [("mm", mm_qk)]
                    emit("scalar", lambda e, pair=pair, j=j, pb=pb: e.activation(
                        p_bf[:, pb + (2 * j) * 512:pb + (2 * j) * 512 + 1024],
                        bank(pair, 2), AF.Exp, bias=expb_sb[:, :], scale=SCALE),
                        wait=wexp, inc="ss",
                        rd=[PS(pair, 2), ("expb_sb", 0, 1)],
                        wr=[("p_bf", pb + 2 * j * 512, pb + 2 * j * 512 + 1024)])
                    gstate[gi] = cnt["ss"]

                def emit_lpv(gi):
                    qb, h, j, ng = groups[gi]
                    pb = 2048 if (qb == 0 and h % 2 == 1) else 0
                    ss_exp = gstate.pop(gi)
                    for t2 in range(2):
                        kt = 2 * j + t2
                        w0 = [("ss", ss_exp)] if t2 == 0 else []
                        if t2 == 0 and j == 0:
                            w0.append(("vv", o_free[0]))
                            if h == 0:
                                w0.append(("vv", l_free[0]))
                        emit("tensor", lambda e, h=h, kt=kt, nkt=2 * ng, pb=pb: e.matmul(
                            ps[0:4, 5 * 512:6 * 512], lsel_bf[:, h * 4:h * 4 + 4],
                            p_bf[:, pb + kt * 512:pb + kt * 512 + 512],
                            start=(h == 0 and kt == 0),
                            stop=(h == HPC - 1 and kt == nkt - 1),
                            skip_group_check=True), wait=w0 or None,
                            rd=[("lsel", h * 4, h * 4 + 4),
                                ("p_bf", pb + kt * 512, pb + kt * 512 + 512)],
                            wr=[PS(5)])
                        emit("tensor", lambda e, h=h, kt=kt, nkt=2 * ng, pb=pb: e.matmul(
                            bank(4), vtok_bf[:, h * S + kt * 128:h * S + kt * 128 + 128],
                            p_bf[:, pb + kt * 512:pb + kt * 512 + 512],
                            start=(kt == 0), stop=(kt == nkt - 1),
                            skip_group_check=True),
                            inc="mm" if kt == 2 * ng - 1 else None,
                            rd=[("vtok", h * S + kt * 128, h * S + kt * 128 + 128),
                                ("p_bf", pb + kt * 512, pb + kt * 512 + 512)],
                            wr=[PS(4)])
                    if j == ng - 1:
                        mm_pv = cnt["mm"]
                        sb = (qb * HPC + h) % 2
                        emit("vector", lambda e, sb=sb: e.tensor_copy(
                            ostg2[:, sb * 512:sb * 512 + 512], bank(4)),
                            wait=[("mm", mm_pv), ("so%d" % sb, stg_free[sb])], inc="vv",
                            rd=[PS(4)], wr=[("ostg2", sb * 512, sb * 512 + 512)])
                        o_free[0] = cnt["vv"]
                        v_stg = cnt["vv"]
                        emit("sync", lambda e, h=h, qb=qb, sb=sb: e.dma_start(
                            out=o_dram.ap()[h * 128:h * 128 + 128, qb * 512:qb * 512 + 512],
                            in_=ostg2[:, sb * 512:sb * 512 + 512]),
                            wait=[("vv", v_stg)], inc="so%d" % sb,
                            rd=[("ostg2", sb * 512, sb * 512 + 512)],
                            wr=[("o_dram", h * NQB + qb, h * NQB + qb + 1)])
                        stg_free[sb] = cnt["so%d" % sb]
                        if h == HPC - 1:
                            emit("vector", lambda e, qb=qb: e.tensor_copy(
                                l_pack[0:HPC, qb * 512:qb * 512 + 512],
                                ps[0:HPC, 5 * 512:6 * 512]),
                                wait=[("mm", cnt["mm"])], inc="vv",
                                rd=[PS(5)], wr=[("l_pack", qb * 512, qb * 512 + 512)])
                            l_free[0] = cnt["vv"]

                LOOK = 2
                for gi in range(len(groups)):
                    emit_qk_mask_exp(gi)
                    if gi >= LOOK:
                        emit_lpv(gi - LOOK)
                for gi in range(len(groups) - LOOK, len(groups)):
                    emit_lpv(gi)
                emit("vector", lambda e: e.reciprocal(l_pack[0:HPC, :], l_pack[0:HPC, :]),
                     inc="vv", rd=[("l_pack", 0, S)], wr=[("l_pack", 0, S)])
                guards["v_il"] = cnt["vv"]
                guards["attn_mm"] = cnt["mm"]
                guards["attn_st0"] = cnt["so0"]
                guards["attn_st1"] = cnt["so1"]
                guards["attn_ss"] = cnt["ss"]

        def phase3a(l_pack):
            with (
                nc.sbuf_tensor("osh_l", [128, HPC * S], BF16) as osh_l,
                nc.sbuf_tensor("wo_bf", [128, HPC * D], BF16) as wo_bf,
                nc.sbuf_tensor("il_h", [1, S], F32) as il_h,
                nc.sbuf_tensor("ilb_sb", [128, 2 * 512], F32) as ilb_sb,
                nc.sbuf_tensor("ostg", [128, 2 * 512], BF16) as ostg,
            ):
                emit("gpsimd", lambda e: e.dma_start(
                    out=wo_bf.rearrange("p (i f) -> p i f", i=NDT), in_=woTmy_t),
                    wait=[("mm", guards["attn_mm"]), ("ss", guards["attn_ss"]),
                          ("vv", guards["v_il"])], inc="gw",
                    rd=[("woTmy", 0, 1)], wr=[("wo_bf", 0, HPC * D)])
                wo_gp = ("gw", cnt["gw"])
                emit("sync", lambda e: e.dma_start(
                    out=osh_l.rearrange("p (h s) -> p h s", h=HPC), in_=o_dram_t),
                    wait=[("so0", guards["attn_st0"]), ("so1", guards["attn_st1"]),
                          ("mm", guards["attn_mm"]),
                          ("ss", guards["attn_ss"]), ("vv", guards["v_il"])], inc="sosh",
                    rd=[("o_dram", 0, HPC * NQB)], wr=[("osh", 0, HPC * S)])
                st_osh = ("sosh", cnt["sosh"])
                ilb_free = [0, 0]
                ilh_mm = 0
                for h in range(HPC):
                    gp_il = ("gil", emit("gpsimd", lambda e, h=h: e.dma_start(
                        out=il_h[0:1, :], in_=l_pack[h:h + 1, :]),
                        wait=[("vv", guards["v_il"]), ("mm", ilh_mm)], inc="gil",
                        rd=[("l_pack", 0, S)], wr=[("il_h", 0, S)]))
                    for qb in range(NQB):
                        sb = (h * NQB + qb) % 2
                        bk = 4 + sb
                        emit("tensor", lambda e, h=h, qb=qb, bk=bk: e.matmul(
                            bank(bk), ones128_sb[:, :],
                            il_h[0:1, qb * 512:qb * 512 + 512],
                            start=True, stop=True, skip_group_check=True),
                            wait=[gp_il, ("vv", ilb_free[sb])], inc="mm",
                            rd=[("ones128_sb", 0, 128), ("il_h", qb * 512, qb * 512 + 512)],
                            wr=[PS(bk)])
                        mm_b = cnt["mm"]
                        emit("vector", lambda e, sb=sb, bk=bk: e.tensor_copy(
                            ilb_sb[:, sb * 512:sb * 512 + 512], bank(bk)),
                            wait=[("mm", mm_b)], inc="vv",
                            rd=[PS(bk)], wr=[("ilb", sb * 512, sb * 512 + 512)])
                        v_cp = cnt["vv"]
                        emit("vector", lambda e, h=h, qb=qb, sb=sb: e.tensor_tensor(
                            osh_l[:, h * S + qb * 512:h * S + qb * 512 + 512],
                            osh_l[:, h * S + qb * 512:h * S + qb * 512 + 512],
                            ilb_sb[:, sb * 512:sb * 512 + 512], ALU.mult),
                            wait=([st_osh] if (h == 0 and qb == 0) else None),
                            inc="vv",
                            rd=[("osh", h * S + qb * 512, h * S + qb * 512 + 512),
                                ("ilb", sb * 512, sb * 512 + 512)],
                            wr=[("osh", h * S + qb * 512, h * S + qb * 512 + 512)])
                        ilb_free[sb] = cnt["vv"]
                    ilh_mm = cnt["mm"]
                onorm_vv = cnt["vv"]

                rs_cc = []
                bank_free2 = {0: 0, 1: 0, 2: 0, 3: 0}
                stg_free = [("spe0", 0), ("spe0", 0)]
                for ec in range(NOC):
                    for qt in range(NKT):
                        bk = qt % 4
                        for h in range(NDT):
                            w0 = None
                            if h == 0:
                                w0 = [("vv", bank_free2[bk])]
                                if ec == 0 and qt == 0:
                                    w0 += [wo_gp, ("vv", onorm_vv)]
                            emit("tensor", lambda e, h=h, qt=qt, ec=ec, bk=bk: e.matmul(
                                bank(bk), osh_l[:, h * S + qt * 128:h * S + qt * 128 + 128],
                                wo_bf[:, h * D + ec * 512:h * D + ec * 512 + 512],
                                start=(h == 0), stop=(h == NDT - 1), skip_group_check=True),
                                wait=w0, inc="mm" if h == NDT - 1 else None,
                                rd=[("osh", h * S + qt * 128, h * S + qt * 128 + 128),
                                    ("wo_bf", h * D + ec * 512, h * D + ec * 512 + 512)],
                                wr=[PS(bk)])
                        mm_fill = cnt["mm"]
                        sb = qt % 2
                        emit("vector", lambda e, sb=sb, bk=bk: e.tensor_copy(
                            ostg[:, sb * 512:sb * 512 + 512], bank(bk)),
                            wait=[("mm", mm_fill), stg_free[sb]], inc="vv",
                            rd=[PS(bk)], wr=[("ostg", sb * 512, sb * 512 + 512)])
                        bank_free2[bk] = cnt["vv"]
                        v_stg = cnt["vv"]
                        strm = "spe%d" % ec
                        emit("sync", lambda e, ec=ec, qt=qt, sb=sb: e.dma_start(
                            out=sp[ec].ap()[qt * 128:qt * 128 + 128, :],
                            in_=ostg[:, sb * 512:sb * 512 + 512]),
                            wait=[("vv", v_stg)], inc=strm,
                            rd=[("ostg", sb * 512, sb * 512 + 512)],
                            wr=[(f"sp{ec}", qt * 128, qt * 128 + 128)])
                        stg_free[sb] = (strm, cnt[strm])
                guards["spe_totals"] = [("spe%d" % e2, cnt["spe%d" % e2])
                                        for e2 in range(NOC)]
                guards["rs_cc"] = rs_cc
                guards["oproj_mm"] = cnt["mm"]
                guards["oproj_vv"] = cnt["vv"]

        def phase3bc():
            with nc.sbuf_tensor("g_sb", [128, NQT * D], BF16) as g_sb:
                with (
                    nc.sbuf_tensor("spk", [8, 6 * 512], F32) as spk,
                    nc.sbuf_tensor("xqmy_bf", [128, IT * TPC], BF16) as xqmy_bf,
                    nc.sbuf_tensor("sqm_bf", [128, 2048], BF16) as sqm_bf,
                    nc.sbuf_tensor("wch_bf", [128, 4 * IT * 512], BF16) as wch_bf,
                    nc.sbuf_tensor("gb_sb", [128, D], F32) as gb_sb,
                    nc.sbuf_tensor("bb_sb", [128, D], F32) as bb_sb,
                    nc.sbuf_tensor("gb_row", [1, 2 * D], F32) as gb_row,
                    nc.sbuf_tensor("rsf_bf", [1, D], BF16) as rsf_bf,
                    nc.sbuf_tensor("rmy_flat", [1, TPC], F32) as rmy_flat,
                    nc.sbuf_tensor("nmy_flat", [1, TPC], BF16) as nmy_flat,
                    nc.sbuf_tensor("rtmy", [128, NQT], F32) as rtmy,
                    nc.sbuf_tensor("z_sb", [128, 2 * 512], F32) as z_sb,
                ):
                    spk_box[0] = spk
                    og = guards["oproj_mm"]
                    ogv = guards["oproj_vv"]
                    emit("gpsimd", lambda e: e.dma_start(
                        out=xqmy_bf.rearrange("p (i f) -> p i f", i=IT), in_=xTmy_t),
                        wait=[("mm", og), ("vv", ogv)], inc="gq",
                        rd=[("xTmy", 0, 1)], wr=[("xqmy", 0, IT * TPC)])
                    gp_xmy = ("gq", cnt["gq"])
                    emit("sync", lambda e: e.dma_start(out=gb_row[0:1, 0:D], in_=gam_in[:, :]),
                         wait=[("mm", og)], inc="sgb",
                         rd=[("gam_in", 0, 1)], wr=[("gb_row", 0, D)])
                    emit("sync", lambda e: e.dma_start(out=gb_row[0:1, D:2 * D], in_=bet_in[:, :]),
                         inc="sgb", rd=[("bet_in", 0, 1)], wr=[("gb_row", D, 2 * D)])
                    st_gb = ("sgb", cnt["sgb"])
                    wch_loaded = []
                    for oc in range(NOC):
                        strm = "gwc%d" % oc
                        emit("gpsimd", lambda e, oc=oc: e.dma_start(
                            out=wch_bf.rearrange("p (b i f) -> p b i f", b=4, i=IT)[:, oc],
                            in_=wfT_t[:, :, oc * 512:oc * 512 + 512]),
                            wait=[("mm", og), ("vv", ogv)] if oc == 0 else None,
                            inc=strm,
                            rd=[("wfT", 0, 1)],
                            wr=[("wch", oc * IT * 512, (oc + 1) * IT * 512)])
                        wch_loaded.append((strm, cnt[strm]))

                    def xqm(i):
                        return xqmy_bf[:, i * TPC:(i + 1) * TPC]

                    def wch(i, oc):
                        return wch_bf[:, (oc * IT + i) * 512:(oc * IT + i) * 512 + 512]

                    def WCH(i, oc):
                        o = (oc * IT + i) * 512
                        return ("wch", o, o + 512)

                    xqmy_v = xqmy_bf.rearrange("p (i f) -> p i f", i=IT)

                    r = emit_stats(
                        lambda qb, pm, pss: gp_xmy,
                        lambda q, i: xqm(i)[:, q * 512:q * 512 + 512],
                        lambda q, i: ("xqmy", i * TPC + q * 512, i * TPC + q * 512 + 512),
                        lambda q, c: sqm_bf.rearrange("p (i f) -> p i f", i=4),
                        lambda q, c: xqmy_v[:, 4 * c:4 * c + 4, q * 512:q * 512 + 512],
                        lambda j: sqm_bf[:, j * 512:(j + 1) * 512],
                        lambda j0, j1: ("sqm", j0 * 512, j1 * 512),
                        MYQB, [("vv", guards["qk_vv"]), ("mm", og)])
                    _, _, ss_r2, v_nm2 = r
                    emit("gpsimd", lambda e: e.dma_start(
                        out=rmy_flat[0:1, :].rearrange("a (q j) -> a q j", q=MYQB),
                        in_=spr(4, MYQB)),
                        wait=[("ss", ss_r2)], inc="gf",
                        rd=[SPR(4)], wr=[("rmy", 0, TPC)])
                    emit("gpsimd", lambda e: e.dma_start(
                        out=nmy_flat[0:1, :].rearrange("a (q j) -> a q j", q=MYQB),
                        in_=spr(3, MYQB)),
                        wait=[("vv", v_nm2)], inc="gf",
                        rd=[SPR(3)], wr=[("nmy", 0, TPC)])
                    gp_f2 = ("gf", cnt["gf"])

                    rs_cc = guards["rs_cc"]
                    if upto != "p3p":
                        for ec in range(NOC):
                            w0 = None
                            if ec == 0:
                                w0 = (guards["spe_totals"] +
                                      [gp_xmy, st_gb, gp_f2] + wch_loaded)
                            emit("gpsimd", lambda e, ec=ec: e.collective_compute(
                                "ReduceScatter", ALU.add, replica_groups=RG4,
                                ins=[sp[ec].ap().opt()], outs=[sm[ec].ap().opt()]),
                                wait=w0, inc="cc",
                                rd=[(f"sp{ec}", 0, S)], wr=[(f"sm{ec}", 0, TPC)])
                            rs_cc.append(cnt["cc"])
                    else:
                        rs_cc.extend([0] * NOC)
                    cc_all = rs_cc[-1]

                    for tt in range(NQT):
                        emit("tensor", lambda e, tt=tt: e.matmul(
                            ps[:, 4 * 512 + tt:4 * 512 + tt + 1],
                            rmy_flat[0:1, tt * 128:tt * 128 + 128], onesc_sb[:, :],
                            start=True, stop=True, skip_group_check=True),
                            wait=[gp_f2] if tt == 0 else None,
                            inc="mm" if tt == NQT - 1 else None,
                            rd=[("rmy", tt * 128, tt * 128 + 128), ("onesc_sb", 0, 1)],
                            wr=[PS(4)])
                    mm_rt2 = cnt["mm"]
                    emit("vector", lambda e: e.tensor_copy(rtmy[:, 0:NQT], ps[:, 4 * 512:4 * 512 + NQT]),
                         wait=[("mm", mm_rt2)], inc="vv",
                         rd=[PS(4)], wr=[("rtmy", 0, NQT)])
                    v_rt2 = cnt["vv"]

                    gbfree = [0, 0]
                    for kk, dst, dstN in ((0, gb_sb, "gb"), (1, bb_sb, "bb")):
                        for ocq in range(NOC):
                            sb6 = (kk * NOC + ocq) % 2
                            emit("tensor", lambda e, kk=kk, ocq=ocq, sb6=sb6: e.matmul(
                                bank(6 + sb6), ones128_sb[:, :],
                                gb_row[0:1, kk * D + ocq * 512:kk * D + ocq * 512 + 512],
                                start=True, stop=True, skip_group_check=True),
                                wait=[st_gb, ("vv", max(gbfree[sb6], ogv))], inc="mm",
                                rd=[("ones128_sb", 0, 128),
                                    ("gb_row", kk * D + ocq * 512, kk * D + ocq * 512 + 512)],
                                wr=[PS(6 + sb6)])
                            mm_g = cnt["mm"]
                            emit("vector", lambda e, ocq=ocq, sb6=sb6, dst=dst: e.tensor_copy(
                                dst[:, ocq * 512:ocq * 512 + 512], bank(6 + sb6)),
                                wait=[("mm", mm_g)], inc="vv",
                                rd=[PS(6 + sb6)],
                                wr=[(dstN, ocq * 512, ocq * 512 + 512)])
                            gbfree[sb6] = cnt["vv"]
                    v_gb = cnt["vv"]

                    bank_free3 = {0: 0, 1: 0}
                    z_free = [0, 0]
                    rsf_free = 0
                    for oc in range(NOC):
                        for i in range(IT):
                            w0 = None
                            if i == 0:
                                w0 = [wch_loaded[oc], ("vv", rsf_free)]
                            emit("tensor", lambda e, i=i, oc=oc: e.matmul(
                                ps[0:1, 5 * 512:6 * 512], onesb_bf[:, :], wch(i, oc),
                                start=(i == 0), stop=(i == IT - 1), skip_group_check=True),
                                wait=w0, inc="mm" if i == IT - 1 else None,
                                rd=[("onesb_bf", 0, 1), WCH(i, oc)], wr=[PS(5)])
                        mm_rsf = cnt["mm"]
                        emit("vector", lambda e, oc=oc: e.tensor_copy(
                            rsf_bf[0:1, oc * 512:oc * 512 + 512], ps[0:1, 5 * 512:6 * 512]),
                            wait=[("mm", mm_rsf)], inc="vv",
                            rd=[PS(5)], wr=[("rsf", oc * 512, oc * 512 + 512)])
                        v_rsf = cnt["vv"]
                        rsf_free = v_rsf
                        for qt in range(NQT):
                            bk = qt % 2
                            for i in range(IT):
                                w0 = [("vv", max(bank_free3[bk], v_rsf))] if i == 0 else None
                                emit("tensor", lambda e, i=i, qt=qt, oc=oc, bk=bk: e.matmul(
                                    bank(bk), xqm(i)[:, qt * 128:qt * 128 + 128], wch(i, oc),
                                    start=(i == 0), stop=False, skip_group_check=True),
                                    wait=w0,
                                    rd=[("xqmy", i * TPC + qt * 128, i * TPC + qt * 128 + 128),
                                        WCH(i, oc)],
                                    wr=[PS(bk)])
                            emit("tensor", lambda e, qt=qt, oc=oc, bk=bk: e.matmul(
                                bank(bk), nmy_flat[0:1, qt * 128:qt * 128 + 128],
                                rsf_bf[0:1, oc * 512:oc * 512 + 512],
                                start=False, stop=True, skip_group_check=True), inc="mm",
                                rd=[("nmy", qt * 128, qt * 128 + 128),
                                    ("rsf", oc * 512, oc * 512 + 512)],
                                wr=[PS(bk)])
                            mm_fill = cnt["mm"]
                            sbz = qt % 2
                            emit("vector", lambda e, qt=qt, oc=oc, bk=bk, sbz=sbz: e.scalar_tensor_tensor(
                                z_sb[:, sbz * 512:sbz * 512 + 512], bank(bk), rtmy[:, qt:qt + 1],
                                gb_sb[:, oc * 512:oc * 512 + 512], ALU.mult, ALU.mult),
                                wait=[("mm", mm_fill), ("vv", max(v_gb, v_rt2)),
                                      ("ss", z_free[sbz])], inc="vv",
                                rd=[PS(bk), ("rtmy", qt, qt + 1), ("gb", oc * 512, oc * 512 + 512)],
                                wr=[("z", sbz * 512, sbz * 512 + 512)])
                            bank_free3[bk] = cnt["vv"]
                            emit("vector", lambda e, oc=oc, sbz=sbz: e.tensor_tensor(
                                z_sb[:, sbz * 512:sbz * 512 + 512], z_sb[:, sbz * 512:sbz * 512 + 512],
                                bb_sb[:, oc * 512:oc * 512 + 512], ALU.add), inc="vv",
                                rd=[("z", sbz * 512, sbz * 512 + 512), ("bb", oc * 512, oc * 512 + 512)],
                                wr=[("z", sbz * 512, sbz * 512 + 512)])
                            v_z = cnt["vv"]
                            emit("scalar", lambda e, qt=qt, oc=oc, sbz=sbz: e.activation(
                                g_sb[:, qt * D + oc * 512:qt * D + oc * 512 + 512],
                                z_sb[:, sbz * 512:sbz * 512 + 512], AF.Silu),
                                wait=[("vv", v_z)], inc="ss",
                                rd=[("z", sbz * 512, sbz * 512 + 512)],
                                wr=[("g", qt * D + oc * 512, qt * D + oc * 512 + 512)])
                            z_free[sbz] = cnt["ss"]
                    guards["ffn_mm"] = cnt["mm"]
                    guards["ffn_ss"] = cnt["ss"]
                    guards["ffn_vv"] = cnt["vv"]
                    guards["ffn_gwc0"] = cnt["gwc0"]
                    guards["ffn_gwc1"] = cnt["gwc1"]
                    if guards.get("dump_g"):
                        dump([(g_sb[:, :], "g", 0)])

                with (
                    nc.sbuf_tensor("stile", [128, 2 * 512], BF16) as stile,
                    nc.sbuf_tensor("xtok_sb", [128, 2 * D], F32) as xtok_sb,
                    nc.sbuf_tensor("osb", [128, 2 * D], F32) as osb,
                ):
                    rs_cc = guards["rs_cc"]
                    s_free = [0, 0]
                    x_free = [0, 0]
                    o_store = [0, 0]
                    fg = guards["ffn_mm"]
                    for qt in range(NQT):
                        xb = qt % 2
                        w0 = [("vv", x_free[xb])]
                        if qt <= 1:
                            w0 += [("mm", fg), ("ss", guards["ffn_ss"]),
                                   ("vv", guards["ffn_vv"]),
                                   ("gwc0", guards["ffn_gwc0"]),
                                   ("gwc1", guards["ffn_gwc1"])]
                        emit("sync", lambda e, qt=qt, xb=xb: e.dma_start(
                            out=xtok_sb[:, xb * D:(xb + 1) * D],
                            in_=x_tok[qt * 128:qt * 128 + 128, :]),
                            wait=w0, inc="sxt%d" % xb,
                            rd=[("x_tok", 0, 1)], wr=[("xtok", xb * D, (xb + 1) * D)])
                        st_x = ("sxt%d" % xb, cnt["sxt%d" % xb])
                        for ec in range(NOC):
                            sb = (qt * NOC + ec) % 2
                            emit("sync", lambda e, qt=qt, ec=ec, sb=sb: e.dma_start(
                                out=stile[:, sb * 512:sb * 512 + 512],
                                in_=sm[ec].ap()[qt * 128:qt * 128 + 128, :]),
                                wait=[("cc", rs_cc[ec]), ("vv", s_free[sb])], inc="sst%d" % sb,
                                rd=[(f"sm{ec}", qt * 128, qt * 128 + 128)],
                                wr=[("stile", sb * 512, sb * 512 + 512)])
                            st_s = ("sst%d" % sb, cnt["sst%d" % sb])
                            emit("vector", lambda e, qt=qt, ec=ec, sb=sb, xb=xb: e.tensor_tensor(
                                osb[:, xb * D + ec * 512:xb * D + ec * 512 + 512],
                                stile[:, sb * 512:sb * 512 + 512],
                                g_sb[:, qt * D + ec * 512:qt * D + ec * 512 + 512], ALU.mult),
                                wait=[st_s,
                                      ("ss", guards["ffn_ss"] if (qt == 0 and ec == 0) else 0),
                                      ("sou%d" % xb, o_store[xb] if ec == 0 else 0)],
                                inc="vv",
                                rd=[("stile", sb * 512, sb * 512 + 512),
                                    ("g", qt * D + ec * 512, qt * D + ec * 512 + 512)],
                                wr=[("osb", xb * D + ec * 512, xb * D + ec * 512 + 512)])
                            s_free[sb] = cnt["vv"]
                            emit("vector", lambda e, qt=qt, ec=ec, xb=xb: e.tensor_tensor(
                                osb[:, xb * D + ec * 512:xb * D + ec * 512 + 512],
                                osb[:, xb * D + ec * 512:xb * D + ec * 512 + 512],
                                xtok_sb[:, xb * D + ec * 512:xb * D + ec * 512 + 512], ALU.add),
                                wait=[st_x] if ec == 0 else None, inc="vv",
                                rd=[("osb", xb * D + ec * 512, xb * D + ec * 512 + 512),
                                    ("xtok", xb * D + ec * 512, xb * D + ec * 512 + 512)],
                                wr=[("osb", xb * D + ec * 512, xb * D + ec * 512 + 512)])
                        v_row = cnt["vv"]
                        x_free[xb] = v_row
                        emit("sync", lambda e, qt=qt, xb=xb: e.dma_start(
                            out=out[qt * 128:qt * 128 + 128, :], in_=osb[:, xb * D:(xb + 1) * D]),
                            wait=[("vv", v_row)], inc="sou%d" % xb,
                            rd=[("osb", xb * D, (xb + 1) * D)],
                            wr=[("out", qt * 128, qt * 128 + 128)])
                        o_store[xb] = cnt["sou%d" % xb]
                    f0, f1 = cnt["sou0"], cnt["sou1"]
                    plan["sync"].append(lambda e, a=f0, b=f1: (
                        e.wait_ge(SEM["sou0"], a), e.wait_ge(SEM["sou1"], b)))

        def dump(tiles):
            for ap_, nm, off in tiles:
                emit("gpsimd", lambda e, ap_=ap_, off=off: e.dma_start(
                    out=dbg[0:ap_.shape[0], off:off + ap_.shape[-1]], in_=ap_),
                    wait=[("mm", cnt["mm"]), ("ss", cnt["ss"]),
                          ("cc", cnt["cc"]), ("vv", cnt["vv"])], inc="gd",
                    rd=[(nm, 0, 10 ** 9)], wr=[("dbg", off, off + ap_.shape[-1])])

        def debug_tail():
            with nc.sbuf_tensor("dbg_sb", [128, D], F32) as dbg_sb:
                emit("vector", lambda e: e.memset(dbg_sb[:, :], 1.0),
                     wait=[("mm", cnt["mm"]), ("ss", cnt["ss"]),
                           ("cc", cnt["cc"]), ("vv", cnt["vv"])],
                     inc="vv", wr=[("dbg_sb", 0, D)])
                vd = cnt["vv"]
                for qt in range(NQT):
                    emit("sync", lambda e, qt=qt: e.dma_start(
                        out=out[qt * 128:qt * 128 + 128, :], in_=dbg_sb[:, :]),
                        wait=[("vv", vd)], inc="sou0",
                        rd=[("dbg_sb", 0, D)], wr=[("out", qt * 128, qt * 128 + 128)])
                fs = cnt["sou0"]
                plan["sync"].append(lambda e, v=fs: e.wait_ge(SEM["sou0"], v))

        with (
            nc.sbuf_tensor("qT_bf", [128, HPC * S], BF16) as qT_bf,
            nc.sbuf_tensor("kT_bf", [128, HPC * S], BF16) as kT_bf,
            nc.sbuf_tensor("vtok_bf", [128, HPC * S], BF16) as vtok_bf,
        ):
            with (
                nc.sbuf_tensor("r_flat", [1, S], F32) as r_flat,
                nc.sbuf_tensor("negm_flat", [1, S], BF16) as negm_flat,
                nc.sbuf_tensor("rsum3", [1, 3 * FPC], BF16) as rsum3,
            ):
                with nc.sbuf_tensor("wqk_bf", [128, 2 * IT * FPC], BF16) as wqk_bf:
                    guards["wqk_bf"] = wqk_bf
                    wv_ = wqk_bf.rearrange("p (k i f) -> p k i f", k=2, i=IT)
                    emit("gpsimd", lambda e: e.dma_start(out=wv_[:, 0], in_=wqT_t),
                         wait=[consts_gp], inc="gw",
                         rd=[("wqT", 0, 1)], wr=[("wqk", 0, IT * FPC)])
                    emit("gpsimd", lambda e: e.dma_start(out=wv_[:, 1], in_=wkT_t), inc="gw",
                         rd=[("wkT", 0, 1)], wr=[("wqk", IT * FPC, 2 * IT * FPC)])
                    guards["wqk_gp"] = ("gw", cnt["gw"])
                    phase1_merged(qT_bf, kT_bf, r_flat, negm_flat, rsum3)
                phase1_v(vtok_bf, r_flat, negm_flat, rsum3)
            if upto == "p1p":
                dump([(qT_bf[:, :], "qT", 0), (kT_bf[:, :], "kT", HPC * S),
                      (vtok_bf[:, :], "vtok", 2 * HPC * S)])
            if upto not in ("p1s", "p1p"):
                with nc.sbuf_tensor("l_pack", [HPC, S], F32) as l_pack:
                    phase2(qT_bf, kT_bf, vtok_bf, l_pack)
                    if upto == "p2":
                        dump([(l_pack[0:HPC, :], "l_pack", 0)])
                    if upto != "p2":
                        phase3a(l_pack)
                        if upto == "p3s":
                            for c in range(NOC):
                                emit("gpsimd", lambda e, c=c: e.dma_start(
                                    out=dbg2[:, c * 512:(c + 1) * 512],
                                    in_=sm[c].ap()[:, :]),
                                    wait=[("cc", guards["rs_cc"][c])], inc="gd",
                                    rd=[(f"sm{c}", 0, TPC)], wr=[("dbg2", c, c + 1)])
        if upto is None or upto == "p3g":
            guards["dump_g"] = (upto == "p3g")
            phase3bc()
        if upto is not None and upto != "p3g":
            debug_tail()

        if check:
            reports = check_races()
            if reports:
                lines = [f"RACE {a}->{b} on {nm}: {ra} vs {rb}"
                         for (a, b, nm, ra, rb) in reports]
                raise AssertionError("race check failed:\n" + "\n".join(lines))

        with nc.Block() as block:
            @block.tensor
            def _(eng):
                for f in plan["tensor"]:
                    f(eng)

            @block.vector
            def _(eng):
                for f in plan["vector"]:
                    f(eng)

            @block.scalar
            def _(eng):
                for f in plan["scalar"]:
                    f(eng)

            @block.sync
            def _(eng):
                for f in plan["sync"]:
                    f(eng)

            @block.gpsimd
            def _(eng):
                for f in plan["gpsimd"]:
                    f(eng)

        _semstack.close()

    return nc



def make_inputs(x, q_proj, k_proj, v_proj, o_proj, ffn_w, ffn_scale, ffn_bias, HPC):
    B, S, D = x.shape
    FPC = HPC * 128
    TPC = S // 4
    ones128 = np.ones((1, 128), np.float32)
    onesc = np.ones((1, 1), np.float32)
    onesb = np.ones((128, 1), np.float32)
    expb_a = np.full((128, 1), EXP_SHIFT, np.float32)
    qsel = np.zeros((128, 64), np.float32)
    for qb in range(8):
        qsel[:, qb * 8 + qb] = 1.0
    lsel = np.zeros((128, HPC * 4), np.float32)
    for h in range(HPC):
        lsel[:, h * 4 + h] = 1.0
    maskc = np.zeros((4 * 128, 512), np.float32)
    for t in range(4):
        for p in range(128):
            maskc[t * 128 + p, :t * 128 + p] = NEG_BIG
    wqT = np.ascontiguousarray(q_proj.T)
    wkT = np.ascontiguousarray(k_proj.T)
    wvT = np.ascontiguousarray(v_proj.T)
    woT = np.ascontiguousarray(o_proj.T)
    wfT = np.ascontiguousarray(ffn_w.T)
    gam = np.ascontiguousarray(ffn_scale.reshape(1, D).astype(np.float32))
    bet = np.ascontiguousarray(ffn_bias.reshape(1, D).astype(np.float32))
    maps = []
    for core in range(8):
        b, g = core // 4, core % 4
        xb = x[b]
        xTb = np.ascontiguousarray(xb.T)
        maps.append({
            "xT": xTb,
            "xTmy": np.ascontiguousarray(xTb[:, g * TPC:(g + 1) * TPC]),
            "x_tok": np.ascontiguousarray(xb[g * TPC:(g + 1) * TPC]),
            "wqT": np.ascontiguousarray(wqT[:, g * FPC:(g + 1) * FPC]),
            "wkT": np.ascontiguousarray(wkT[:, g * FPC:(g + 1) * FPC]),
            "wvT": np.ascontiguousarray(wvT[:, g * FPC:(g + 1) * FPC]),
            "woTmy": np.ascontiguousarray(woT[g * FPC:(g + 1) * FPC, :]),
            "wfT": wfT,
            "gam_in": gam,
            "bet_in": bet,
            "ones128": ones128,
            "onesc": onesc,
            "onesb": onesb,
            "expb": expb_a,
            "qsel": qsel,
            "lsel": lsel,
            "maskc": maskc,
        })
    return maps


def assemble(results, B, S, D):
    TPC = S // 4
    full = np.empty((B, S, D), np.float32)
    for core in range(8):
        b, g = core // 4, core % 4
        full[b, g * TPC:(g + 1) * TPC] = results[core]["out"]
    return full



def kernel(L0_attn_in_features, L0_attn_q_proj, L0_attn_k_proj, L0_attn_v_proj,
           L0_attn_o_proj, L0_ffn_weight, L0_ffn_scale_weight, L0_ffn_bias):
    x = np.asarray(L0_attn_in_features)
    B, S, D = x.shape
    HPC = (D // 128) // 4
    nc = build_nc(D, S, HPC)
    maps = make_inputs(x, np.asarray(L0_attn_q_proj), np.asarray(L0_attn_k_proj),
                       np.asarray(L0_attn_v_proj), np.asarray(L0_attn_o_proj),
                       np.asarray(L0_ffn_weight), np.asarray(L0_ffn_scale_weight),
                       np.asarray(L0_ffn_bias), HPC)
    res = run_bass_kernel_spmd(nc, maps, core_ids=list(range(8)))
    return assemble(res.results, B, S, D)


# version 14
# speedup vs baseline: 1.0586x; 1.0427x over previous
import numpy as np
import concourse.bass as bass
import concourse.mybir as mybir
from concourse.bass_utils import run_bass_kernel_spmd

F32 = mybir.dt.float32
BF16 = mybir.dt.bfloat16
AF = mybir.ActivationFunctionType
ALU = mybir.AluOpType

NEG_BIG = -1.0e30
EXP_SHIFT = -12.0
LN_EPS = 1e-5
RG4 = [[0, 1, 2, 3], [4, 5, 6, 7]]
QNAMES = ("tensor", "vector", "scalar", "sync", "gpsimd")


def build_nc(D, S, HPC, upto=None, check=True):
    IT = D // 128
    NQB = S // 512
    TPC = S // 4
    NQT = TPC // 128
    NKT = S // 128
    FPC = HPC * 128
    NDT = FPC // 128
    NCH = (FPC + 511) // 512
    NOC = D // 512
    MYQB = TPC // 512
    SCALE = float(1.0 / np.sqrt(128.0))
    assert NQB <= 8 and 1 <= MYQB <= 8

    nc = bass.Bass("TRN2", target_bir_lowering=False, detect_race_conditions=False)

    xT = nc.dram_tensor("xT", [D, S], F32, kind="ExternalInput").ap()
    xTmy = nc.dram_tensor("xTmy", [D, TPC], F32, kind="ExternalInput").ap()
    x_tok = nc.dram_tensor("x_tok", [TPC, D], F32, kind="ExternalInput").ap()
    wqT = nc.dram_tensor("wqT", [D, FPC], F32, kind="ExternalInput").ap()
    wkT = nc.dram_tensor("wkT", [D, FPC], F32, kind="ExternalInput").ap()
    wvT = nc.dram_tensor("wvT", [D, FPC], F32, kind="ExternalInput").ap()
    woTmy = nc.dram_tensor("woTmy", [FPC, D], F32, kind="ExternalInput").ap()
    wfT = nc.dram_tensor("wfT", [D, D], F32, kind="ExternalInput").ap()
    gam_in = nc.dram_tensor("gam_in", [1, D], F32, kind="ExternalInput").ap()
    bet_in = nc.dram_tensor("bet_in", [1, D], F32, kind="ExternalInput").ap()
    ones128 = nc.dram_tensor("ones128", [1, 128], F32, kind="ExternalInput").ap()
    onesc = nc.dram_tensor("onesc", [1, 1], F32, kind="ExternalInput").ap()
    onesb = nc.dram_tensor("onesb", [128, 1], F32, kind="ExternalInput").ap()
    expb = nc.dram_tensor("expb", [128, 1], F32, kind="ExternalInput").ap()
    qsel = nc.dram_tensor("qsel", [128, 64], F32, kind="ExternalInput").ap()
    lsel = nc.dram_tensor("lsel", [128, HPC * 4], F32, kind="ExternalInput").ap()
    maskc = nc.dram_tensor("maskc", [4 * 128, 512], F32, kind="ExternalInput").ap()
    out = nc.dram_tensor("out", [TPC, D], F32, kind="ExternalOutput").ap()

    dbg = (nc.dram_tensor("dbg", [128, max(3 * HPC * S, NQT * D)], F32,
                          kind="ExternalOutput").ap() if upto else None)
    dbg2 = (nc.dram_tensor("dbg2", [TPC, D], F32, kind="ExternalOutput").ap()
            if upto == "p3s" else None)
    o_dram = nc.dram_tensor("o_dram", [FPC, S], BF16)
    sp = [nc.dram_tensor(f"s_part{c}", [S, 512], BF16) for c in range(NOC)]
    sm = [nc.dram_tensor(f"s_my{c}", [TPC, 512], BF16) for c in range(NOC)]

    xT_t = xT.rearrange("(i p) s -> p i s", p=128)
    xTmy_t = xTmy.rearrange("(i p) s -> p i s", p=128)
    wqT_t = wqT.rearrange("(i p) f -> p i f", p=128)
    wkT_t = wkT.rearrange("(i p) f -> p i f", p=128)
    wvT_t = wvT.rearrange("(i p) f -> p i f", p=128)
    woTmy_t = woTmy.rearrange("(i p) f -> p i f", p=128)
    wfT_t = wfT.rearrange("(i p) f -> p i f", p=128)
    maskc_t = maskc.rearrange("(t p) j -> p t j", p=128)
    o_dram_t = o_dram.ap().rearrange("(h p) s -> p h s", p=128)

    plan = {n: [] for n in QNAMES}
    COMPUTE_SEMS = ("mm", "vv", "ss", "cc")
    DMA_STREAMS = ("sc", "sgb", "so0", "so1", "sosh", "spe0", "spe1", "spe2", "spe3",
                   "sst0", "sst1", "sxt0", "sxt1", "sou0", "sou1",
                   "gc", "gx0", "gx1", "gw", "gwc0", "gwc1", "gwc2", "gwc3", "gf",
                   "gil", "gm", "gq", "gd")
    cnt = {k: 0 for k in COMPUTE_SEMS + DMA_STREAMS}
    SEM = {}
    guards = {}
    nodes = []
    sem_hist = {k: [] for k in cnt}

    def emit(q, fn, wait=None, inc=None, rd=(), wr=()):
        w = tuple((s, v) for (s, v) in (wait or ()) if v > 0)
        amt = 0
        if inc is not None:
            amt = 1 if inc in COMPUTE_SEMS else 16
            cnt[inc] += amt
        nid = len(nodes)
        nodes.append((nid, q, w, tuple(rd), tuple(wr)))
        if inc is not None:
            sem_hist[inc].append((cnt[inc], nid))

        def go(eng, fn=fn, w=w, inc=inc, amt=amt):
            for sname, val in w:
                eng.wait_ge(SEM[sname], val)
            r = fn(eng)
            if inc is not None:
                r.then_inc(SEM[inc], amt)
        plan[q].append(go)
        return cnt[inc] if inc is not None else None

    def check_races(max_report=40):
        qi = {n: i for i, n in enumerate(QNAMES)}

        def wait_node(s, v):
            lo, hi = 0, len(sem_hist[s])
            h = sem_hist[s]
            while lo < hi:
                mid = (lo + hi) // 2
                if h[mid][0] >= v:
                    hi = mid
                else:
                    lo = mid + 1
            return h[lo][1] if lo < len(h) else None

        pos = {}
        counters = {n: 0 for n in QNAMES}
        for nid, q, w, rd_, wr_ in nodes:
            pos[nid] = counters[q]
            counters[q] += 1
        NQ = len(QNAMES)
        vcs = [None] * len(nodes)
        last = {n: None for n in QNAMES}
        for nid, q, w, rd_, wr_ in nodes:
            vc = list(vcs[last[q]]) if last[q] is not None else [-1] * NQ
            for (s, v) in w:
                wn = wait_node(s, v)
                if wn is not None:
                    wv = vcs[wn]
                    for i in range(NQ):
                        if wv[i] > vc[i]:
                            vc[i] = wv[i]
                    wq = nodes[wn][1]
                    if pos[wn] > vc[qi[wq]]:
                        vc[qi[wq]] = pos[wn]
            vc[qi[q]] = pos[nid]
            vcs[nid] = vc
            last[q] = nid

        by_name = {}
        for nid, q, w, rd_, wr_ in nodes:
            for kind, accs in (("r", rd_), ("w", wr_)):
                for acc in accs:
                    by_name.setdefault(acc[0], []).append((nid, kind, acc[1], acc[2]))
        reports = []
        for name, lst in by_name.items():
            lst.sort()
            for i in range(len(lst)):
                ai, ak, a0, a1 = lst[i]
                for j in range(i + 1, len(lst)):
                    bi, bk, b0, b1 = lst[j]
                    if ak == "r" and bk == "r":
                        continue
                    if a1 <= b0 or b1 <= a0:
                        continue
                    qa, qb_ = nodes[ai][1], nodes[bi][1]
                    if qa == qb_:
                        continue
                    if vcs[bi][qi[qa]] >= pos[ai]:
                        continue
                    reports.append((ai, bi, name, (a0, a1, ak, qa), (b0, b1, bk, qb_)))
                    if len(reports) >= max_report:
                        return reports
        return reports

    with (
        nc.sbuf_tensor("r_tok", [128, NKT], F32) as r_tok,
        nc.sbuf_tensor("ones128_sb", [1, 128], F32) as ones128_sb,
        nc.sbuf_tensor("onesc_sb", [1, 1], F32) as onesc_sb,
        nc.sbuf_tensor("onesb_bf", [128, 1], BF16) as onesb_bf,
        nc.sbuf_tensor("expb_sb", [128, 1], F32) as expb_sb,
        nc.sbuf_tensor("qsel_bf", [128, 64], BF16) as qsel_bf,
        nc.sbuf_tensor("lsel_bf", [128, HPC * 4], BF16) as lsel_bf,
        nc.psum_tensor("ps", [128, 4096], F32) as ps,
    ):
        import contextlib
        _semstack = contextlib.ExitStack()
        for _k in COMPUTE_SEMS + DMA_STREAMS:
            SEM[_k] = _semstack.enter_context(nc.semaphore("sem_" + _k))

        def bank(i, w=1):
            return ps[:, i * 512:(i + w) * 512]

        def PS(i, w=1):
            return ("ps", i, i + w)

        spk_box = [None]

        def spr(k, rows=8):
            return spk_box[0][0:rows, k * 512:(k + 1) * 512]

        def SPR(k):
            return ("spk", k * 512, (k + 1) * 512)

        def emit_stats(loader, xq_sl, xq_res, sq_dst, sq_src, sq_sl, sq_res, nqb,
                       extra_wait_first):
            sq_guard = 0
            prevs = [(0, 0), (0, 0)]
            for qb in range(nqb):
                gp_q = loader(qb, prevs[qb % 2][0], prevs[qb % 2][1])
                for i in range(IT):
                    w0 = None
                    if i == 0:
                        w0 = [gp_q] + (extra_wait_first if qb == 0 else [])
                    emit("tensor", lambda e, i=i, q=qb: e.matmul(
                        ps[0:8, 2 * 512:3 * 512], qsel_bf[:, q * 8:q * 8 + 8],
                        xq_sl(q, i),
                        start=(q == 0 and i == 0), stop=(q == nqb - 1 and i == IT - 1),
                        skip_group_check=True),
                        wait=w0, inc="mm" if i == IT - 1 else None,
                        rd=[("qsel", 0, 64), xq_res(qb, i)], wr=[PS(2)])
                for c in range(IT // 4):
                    emit("scalar", lambda e, q=qb, c=c: e.activation(
                        sq_dst(q, c), sq_src(q, c), AF.Square),
                        wait=[gp_q, ("mm", sq_guard)], inc="ss",
                        rd=[xq_res(qb, 4 * c), xq_res(qb, 4 * c + 1),
                            xq_res(qb, 4 * c + 2), xq_res(qb, 4 * c + 3)],
                        wr=[sq_res(0, 4)])
                    ss_here = cnt["ss"]
                    for j in range(4):
                        ig = c * 4 + j
                        emit("tensor", lambda e, j=j, q=qb, ig=ig: e.matmul(
                            ps[0:8, 3 * 512:4 * 512], qsel_bf[:, q * 8:q * 8 + 8],
                            sq_sl(j),
                            start=(q == 0 and ig == 0), stop=(q == nqb - 1 and ig == IT - 1),
                            skip_group_check=True),
                            wait=[("ss", ss_here)] if j == 0 else None,
                            inc="mm" if j == 3 else None,
                            rd=[("qsel", 0, 64), sq_res(j, j + 1)], wr=[PS(3)])
                    sq_guard = cnt["mm"]
                prevs[qb % 2] = (cnt["mm"], cnt["ss"])
            stats_mm = cnt["mm"]
            emit("vector", lambda e: e.tensor_copy(spr(0, nqb), ps[0:nqb, 2 * 512:3 * 512]),
                 wait=[("mm", stats_mm)], rd=[PS(2)], wr=[SPR(0)])
            emit("vector", lambda e: e.tensor_copy(spr(1, nqb), ps[0:nqb, 3 * 512:4 * 512]),
                 inc="vv", rd=[PS(3)], wr=[SPR(1)])
            ps_free = cnt["vv"]
            emit("vector", lambda e: e.tensor_scalar(spr(2, nqb), spr(0, nqb), 1.0 / D, None, ALU.mult),
                 rd=[SPR(0)], wr=[SPR(2)])
            emit("vector", lambda e: e.tensor_tensor(spr(3, nqb), spr(2, nqb), spr(2, nqb), ALU.mult),
                 rd=[SPR(2)], wr=[SPR(3)])
            emit("vector", lambda e: e.tensor_scalar(spr(4, nqb), spr(1, nqb), 1.0 / D, None, ALU.mult),
                 rd=[SPR(1)], wr=[SPR(4)])
            emit("vector", lambda e: e.tensor_tensor(spr(4, nqb), spr(4, nqb), spr(3, nqb), ALU.subtract),
                 rd=[SPR(4), SPR(3)], wr=[SPR(4)])
            emit("vector", lambda e: e.tensor_scalar(spr(4, nqb), spr(4, nqb), LN_EPS, None, ALU.add),
                 rd=[SPR(4)], wr=[SPR(4)])
            emit("vector", lambda e: e.reciprocal(spr(3, nqb), spr(4, nqb)), inc="vv",
                 rd=[SPR(4)], wr=[SPR(3)])
            v_re = cnt["vv"]
            emit("scalar", lambda e: e.activation(spr(4, nqb), spr(3, nqb), AF.Sqrt),
                 wait=[("vv", v_re)], inc="ss", rd=[SPR(3)], wr=[SPR(4)])
            ss_r = cnt["ss"]
            emit("vector", lambda e: e.tensor_scalar(spr(3, nqb), spr(2, nqb), -1.0, None, ALU.mult),
                 wait=[("ss", ss_r)], inc="vv", rd=[SPR(2)], wr=[SPR(3)])
            v_nm = cnt["vv"]
            return stats_mm, ps_free, ss_r, v_nm

        emit("sync", lambda e: e.dma_start(out=ones128_sb[:, :], in_=ones128[:, :]),
             inc="sc", wr=[("ones128_sb", 0, 128)])
        emit("sync", lambda e: e.dma_start(out=onesc_sb[:, :], in_=onesc[:, :]),
             inc="sc", wr=[("onesc_sb", 0, 1)])
        emit("sync", lambda e: e.dma_start(out=expb_sb[:, :], in_=expb[:, :]),
             inc="sc", wr=[("expb_sb", 0, 1)])
        consts_st = ("sc", cnt["sc"])
        emit("gpsimd", lambda e: e.dma_start(out=onesb_bf[:, :], in_=onesb[:, :]),
             inc="gc", wr=[("onesb_bf", 0, 1)])
        emit("gpsimd", lambda e: e.dma_start(out=qsel_bf[:, :], in_=qsel[:, :]),
             inc="gc", wr=[("qsel", 0, 64)])
        emit("gpsimd", lambda e: e.dma_start(out=lsel_bf[:, :], in_=lsel[:, :]),
             inc="gc", wr=[("lsel", 0, HPC * 4)])
        consts_gp = ("gc", cnt["gc"])

        def phase1_merged(qT_bf, kT_bf, r_flat, negm_flat, rsum3):
            wqk_bf = guards["wqk_bf"]
            wqk_gp = guards["wqk_gp"]
            with (
                nc.sbuf_tensor("xq2_bf", [128, 2 * IT * 512], BF16) as xq2_bf,
                nc.sbuf_tensor("sq_bf", [128, 1024], BF16) as sq_bf,
                nc.sbuf_tensor("rb_sb", [128, 512], F32) as rb_sb,
                nc.sbuf_tensor("t_sum", [1, 2, 512], F32) as t_sum,
                nc.sbuf_tensor("t_sq", [1, 2, 512], F32) as t_sq,
                nc.sbuf_tensor("r_row", [1, 2, 512], F32) as r_row,
            ):
                def wsl(k, i):
                    o = (k * IT + i) * FPC
                    return wqk_bf[:, o:o + FPC]

                def WSL(k, i):
                    o = (k * IT + i) * FPC
                    return ("wqk", o, o + FPC)

                def xq2(b, i):
                    return xq2_bf[:, (b * IT + i) * 512:(b * IT + i) * 512 + 512]

                def XQ2(b, i):
                    o = (b * IT + i) * 512
                    return ("xq2", o, o + 512)

                rs_free = 0
                for k in range(2):
                    for i in range(IT):
                        w0 = None
                        if i == 0:
                            w0 = [wqk_gp, ("vv", rs_free)]
                        emit("tensor", lambda e, k=k, i=i: e.matmul(
                            ps[0:1, 7 * 512:7 * 512 + FPC], onesb_bf[:, :], wsl(k, i),
                            start=(i == 0), stop=(i == IT - 1), skip_group_check=True),
                            wait=w0, inc="mm" if i == IT - 1 else None,
                            rd=[("onesb_bf", 0, 1), WSL(k, i)], wr=[PS(7)])
                    mm_rs = cnt["mm"]
                    emit("vector", lambda e, k=k: e.tensor_copy(
                        rsum3[0:1, k * FPC:(k + 1) * FPC], ps[0:1, 7 * 512:7 * 512 + FPC]),
                        wait=[("mm", mm_rs)], inc="vv",
                        rd=[PS(7)], wr=[("rsum3", k * FPC, (k + 1) * FPC)])
                    rs_free = cnt["vv"]
                bank7_free = rs_free

                QKB = [0, 1, 4, 5, 6]
                bank_free = {b: 0 for b in QKB}
                stats_free = 0
                xbuf = [(0, 0), (0, 0)]
                rtok_vv = 0
                bank7_box = [bank7_free]
                qb_math = {}

                def emit_load_stats(qb):
                    nonlocal stats_free
                    b = qb % 2
                    emit("gpsimd", lambda e, b=b, q=qb: e.dma_start(
                        out=xq2_bf.rearrange("p (b i f) -> p b i f", b=2, i=IT)[:, b],
                        in_=xT_t[:, :, q * 512:q * 512 + 512]),
                        wait=[("mm", xbuf[b][0]), ("ss", xbuf[b][1]), consts_gp],
                        inc="gx%d" % b,
                        rd=[("xT", 0, 1)], wr=[("xq2", b * IT * 512, (b + 1) * IT * 512)])
                    gp_here = ("gx%d" % b, cnt["gx%d" % b])
                    for i in range(IT):
                        w0 = None
                        if i == 0:
                            w0 = [gp_here, ("vv", stats_free), consts_gp]
                        emit("tensor", lambda e, i=i, b=b: e.matmul(
                            ps[0:1, 2 * 512:3 * 512], onesb_bf[:, :], xq2(b, i),
                            start=(i == 0), stop=(i == IT - 1), skip_group_check=True),
                            wait=w0, inc="mm" if i == IT - 1 else None,
                            rd=[("onesb_bf", 0, 1), XQ2(b, i)], wr=[PS(2)])
                    mm_sum = cnt["mm"]
                    sq_guard_local = 0
                    for c in range(IT // 2):
                        emit("scalar", lambda e, b=b, c=c: e.activation(
                            sq_bf[:, :],
                            xq2_bf[:, (b * IT) * 512 + c * 1024:(b * IT) * 512 + (c + 1) * 1024],
                            AF.Square),
                            wait=[gp_here, ("mm", sq_guard_local)], inc="ss",
                            rd=[XQ2(b, 2 * c), XQ2(b, 2 * c + 1)],
                            wr=[("sq1", 0, 1024)])
                        ss_here = cnt["ss"]
                        for j in range(2):
                            emit("tensor", lambda e, j=j, c=c: e.matmul(
                                ps[0:1, 3 * 512:4 * 512], onesb_bf[:, :],
                                sq_bf[:, j * 512:(j + 1) * 512],
                                start=(c == 0 and j == 0), stop=(c == IT // 2 - 1 and j == 1),
                                skip_group_check=True),
                                wait=[("ss", ss_here)] if j == 0 else None,
                                inc="mm" if j == 1 else None,
                                rd=[("onesb_bf", 0, 1), ("sq1", j * 512, (j + 1) * 512)],
                                wr=[PS(3)])
                        sq_guard_local = cnt["mm"]
                    mm_sq = cnt["mm"]
                    emit("vector", lambda e, q=qb: e.tensor_copy(
                        t_sum[0:1, q % 2, :], ps[0:1, 2 * 512:3 * 512]),
                        wait=[("mm", mm_sum)], rd=[PS(2)], wr=[("t_sum", (qb % 2) * 512, (qb % 2) * 512 + 512)])
                    emit("vector", lambda e, q=qb: e.tensor_copy(
                        t_sq[0:1, q % 2, :], ps[0:1, 3 * 512:4 * 512]),
                        wait=[("mm", mm_sq)], inc="vv",
                        rd=[PS(3)], wr=[("t_sq", (qb % 2) * 512, (qb % 2) * 512 + 512)])
                    stats_free = cnt["vv"]
                    p = qb % 2
                    TSUM = ("t_sum", p * 512, p * 512 + 512)
                    TSQ = ("t_sq", p * 512, p * 512 + 512)
                    RROW = ("r_row", p * 512, p * 512 + 512)
                    emit("vector", lambda e, p=p: e.tensor_scalar(
                        t_sum[0:1, p, :], t_sum[0:1, p, :], 1.0 / D, None, ALU.mult),
                        rd=[TSUM], wr=[TSUM])
                    emit("vector", lambda e, p=p: e.scalar_tensor_tensor(
                        r_row[0:1, p, :], t_sum[0:1, p, :], -1.0, t_sum[0:1, p, :],
                        ALU.mult, ALU.mult),
                        rd=[TSUM], wr=[RROW])
                    emit("vector", lambda e, p=p: e.tensor_scalar(
                        t_sq[0:1, p, :], t_sq[0:1, p, :], 1.0 / D, None, ALU.mult),
                        rd=[TSQ], wr=[TSQ])
                    emit("vector", lambda e, p=p: e.tensor_tensor(
                        t_sq[0:1, p, :], t_sq[0:1, p, :], r_row[0:1, p, :], ALU.add),
                        rd=[TSQ, RROW], wr=[TSQ])
                    emit("vector", lambda e, p=p: e.tensor_scalar(
                        t_sq[0:1, p, :], t_sq[0:1, p, :], LN_EPS, None, ALU.add),
                        rd=[TSQ], wr=[TSQ])
                    emit("vector", lambda e, p=p: e.reciprocal(
                        t_sq[0:1, p, :], t_sq[0:1, p, :]), inc="vv",
                        rd=[TSQ], wr=[TSQ])
                    v_iv = cnt["vv"]
                    emit("scalar", lambda e, p=p: e.activation(
                        r_row[0:1, p, :], t_sq[0:1, p, :], AF.Sqrt),
                        wait=[("vv", v_iv)], inc="ss",
                        rd=[TSQ], wr=[RROW])
                    ss_r = cnt["ss"]
                    emit("vector", lambda e, q=qb, p=p: e.tensor_scalar(
                        negm_flat[0:1, q * 512:q * 512 + 512], t_sum[0:1, p, :], -1.0, None, ALU.mult),
                        inc="vv",
                        rd=[TSUM],
                        wr=[("negm_flat", qb * 512, qb * 512 + 512)])
                    qb_math[qb] = (ss_r, cnt["vv"])

                def emit_fills(qb):
                    nonlocal rtok_vv
                    b = qb % 2
                    p = qb % 2
                    gp_here = ("gx%d" % b, cnt["gx%d" % b])
                    ss_r, v_negm = qb_math[qb]
                    RROW = ("r_row", p * 512, p * 512 + 512)
                    emit("tensor", lambda e, p=p: e.matmul(
                        bank(7), ones128_sb[:, :], r_row[0:1, p, :],
                        start=True, stop=True, skip_group_check=True),
                        wait=[("ss", ss_r), ("vv", bank7_box[0]), consts_st], inc="mm",
                        rd=[("ones128_sb", 0, 128), RROW], wr=[PS(7)])
                    mm_rb = cnt["mm"]
                    emit("vector", lambda e: e.tensor_copy(
                        rb_sb[:, :], bank(7)),
                        wait=[("mm", mm_rb)], inc="vv",
                        rd=[PS(7)], wr=[("rb_sb", 0, 512)])
                    v_rb = cnt["vv"]
                    fills = [(k, dt) for k in range(2) for dt in range(NDT)]
                    DEPTH = 4

                    def emit_raw(fi):
                        k, dt = fills[fi]
                        bk = QKB[fi % 5]
                        for i in range(IT):
                            w0 = None
                            if i == 0:
                                w0 = [gp_here, ("vv", bank_free[bk])]
                            emit("tensor", lambda e, k=k, i=i, dt=dt, b=b, bk=bk: e.matmul(
                                bank(bk), wsl(k, i)[:, dt * 128:dt * 128 + 128], xq2(b, i),
                                start=(i == 0), stop=False, skip_group_check=True),
                                wait=w0, rd=[WSL(k, i), XQ2(b, i)], wr=[PS(bk)])

                    def emit_r1_evac(fi):
                        k, dt = fills[fi]
                        bk = QKB[fi % 5]
                        dstT, dstN = (qT_bf, "qT") if k == 0 else (kT_bf, "kT")
                        emit("tensor", lambda e, k=k, dt=dt, q=qb, bk=bk: e.matmul(
                            bank(bk), rsum3[0:1, k * FPC + dt * 128:k * FPC + dt * 128 + 128],
                            negm_flat[0:1, q * 512:q * 512 + 512],
                            start=False, stop=True, skip_group_check=True),
                            wait=[("vv", v_negm)], inc="mm",
                            rd=[("rsum3", k * FPC + dt * 128, k * FPC + dt * 128 + 128),
                                ("negm_flat", qb * 512, qb * 512 + 512)],
                            wr=[PS(bk)])
                        mm_fill = cnt["mm"]
                        emit("vector", lambda e, dt=dt, q=qb, bk=bk, dstT=dstT: e.tensor_tensor(
                            dstT[:, dt * S + q * 512:dt * S + q * 512 + 512],
                            bank(bk), rb_sb[:, :], ALU.mult),
                            wait=[("mm", mm_fill), ("vv", v_rb)], inc="vv",
                            rd=[PS(bk), ("rb_sb", 0, 512)],
                            wr=[(dstN, dt * S + qb * 512, dt * S + qb * 512 + 512)])
                        bank_free[bk] = cnt["vv"]

                    for fi in range(len(fills)):
                        emit_raw(fi)
                        if fi >= DEPTH:
                            emit_r1_evac(fi - DEPTH)
                        if fi == 1 and qb + 1 < NQB:
                            emit_load_stats(qb + 1)
                    for fi in range(len(fills) - DEPTH, len(fills)):
                        emit_r1_evac(fi)
                    for tt in range(4):
                        gtt = qb * 4 + tt
                        emit("tensor", lambda e, tt=tt, p=p: e.matmul(
                            ps[:, 7 * 512 + tt:7 * 512 + tt + 1],
                            r_row[0:1, p, tt * 128:tt * 128 + 128], onesc_sb[:, :],
                            start=True, stop=True, skip_group_check=True),
                            wait=[("vv", v_rb)] if tt == 0 else None,
                            inc="mm" if tt == 3 else None,
                            rd=[("r_row", p * 512 + tt * 128, p * 512 + tt * 128 + 128),
                                ("onesc_sb", 0, 1)],
                            wr=[PS(7)])
                    mm_rt = cnt["mm"]
                    emit("vector", lambda e, q=qb: e.tensor_copy(
                        r_tok[:, q * 4:q * 4 + 4], ps[:, 7 * 512:7 * 512 + 4]),
                        wait=[("mm", mm_rt)], inc="vv",
                        rd=[PS(7)], wr=[("r_tok", qb * 4, qb * 4 + 4)])
                    bank7_box[0] = cnt["vv"]
                    rtok_vv = cnt["vv"]
                    xbuf[b] = (cnt["mm"], cnt["ss"])

                emit_load_stats(0)
                for qb in range(NQB):
                    emit_fills(qb)
                guards["v_rtok"] = rtok_vv
                guards["qk_mm"] = cnt["mm"]
                guards["qk_vv"] = cnt["vv"]


        def phase1_v(vtok_bf, r_flat, negm_flat, rsum3):
            v_rtok = guards["v_rtok"]
            if True:
                with (
                    nc.sbuf_tensor("wv_bf", [128, IT * FPC], BF16) as wv_bf,
                    nc.sbuf_tensor("xq3_bf", [128, 2 * IT * 512], BF16) as xq3_bf,
                ):
                    def wvs(i):
                        return wv_bf[:, i * FPC:(i + 1) * FPC]

                    def WVS(i):
                        return ("wv", i * FPC, (i + 1) * FPC)

                    def xq3(b, i):
                        return xq3_bf[:, (b * IT + i) * 512:(b * IT + i) * 512 + 512]

                    def XQ3(b, i):
                        o = (b * IT + i) * 512
                        return ("xq3", o, o + 512)

                    emit("gpsimd", lambda e: e.dma_start(
                        out=wv_bf.rearrange("p (i f) -> p i f", i=IT), in_=wvT_t),
                        wait=[("mm", guards["qk_mm"]), ("vv", guards["qk_vv"])], inc="gw",
                        rd=[("wvT", 0, 1)], wr=[("wv", 0, IT * FPC)])
                    wv_gp = ("gw", cnt["gw"])
                    rs_free = 0
                    for ch in range(NCH):
                        w = min(512, FPC - ch * 512)
                        for i in range(IT):
                            w0 = None
                            if i == 0:
                                w0 = [wv_gp, ("vv", rs_free)]
                            emit("tensor", lambda e, i=i, ch=ch, w=w: e.matmul(
                                ps[0:1, 5 * 512:5 * 512 + w], onesb_bf[:, :],
                                wvs(i)[:, ch * 512:ch * 512 + w],
                                start=(i == 0), stop=(i == IT - 1), skip_group_check=True),
                                wait=w0, inc="mm" if i == IT - 1 else None,
                                rd=[("onesb_bf", 0, 1), WVS(i)], wr=[PS(5)])
                        mm_rs = cnt["mm"]
                        emit("vector", lambda e, ch=ch, w=w: e.tensor_copy(
                            rsum3[0:1, 2 * FPC + ch * 512:2 * FPC + ch * 512 + w],
                            ps[0:1, 5 * 512:5 * 512 + w]),
                            wait=[("mm", mm_rs)], inc="vv",
                            rd=[PS(5)], wr=[("rsum3", 2 * FPC, 3 * FPC)])
                        rs_free = cnt["vv"]
                    v_rsum_v = cnt["vv"]
                    bank_free = {2: 0, 3: 0}
                    xbuf_free = [guards["qk_mm"], guards["qk_mm"]]
                    for qb in range(NQB):
                        b = qb % 2
                        emit("gpsimd", lambda e, b=b, q=qb: e.dma_start(
                            out=xq3_bf.rearrange("p (b i f) -> p b i f", b=2, i=IT)[:, b],
                            in_=xT_t[:, :, q * 512:q * 512 + 512]),
                            wait=[("mm", xbuf_free[b]), ("vv", guards["qk_vv"])], inc="gx%d" % b,
                            rd=[("xT", 0, 1)], wr=[("xq3", b * IT * 512, (b + 1) * IT * 512)])
                        gp_here = ("gx%d" % b, cnt["gx%d" % b])
                        for tt in range(4):
                            gtt = qb * 4 + tt
                            bk = 2 + (tt % 2)
                            for ch in range(NCH):
                                w = min(512, FPC - ch * 512)
                                for i in range(IT):
                                    w0 = None
                                    if i == 0 and ch == 0:
                                        w0 = [gp_here,
                                              ("vv", max(bank_free[bk], v_rsum_v))]
                                    emit("tensor", lambda e, i=i, tt=tt, b=b, bk=bk, ch=ch, w=w: e.matmul(
                                        bank(bk)[:, ch * 512:ch * 512 + w],
                                        xq3(b, i)[:, tt * 128:tt * 128 + 128],
                                        wvs(i)[:, ch * 512:ch * 512 + w],
                                        start=(i == 0), stop=False, skip_group_check=True),
                                        wait=w0, rd=[XQ3(b, i), WVS(i)], wr=[PS(bk)])
                                emit("tensor", lambda e, gtt=gtt, bk=bk, ch=ch, w=w: e.matmul(
                                    bank(bk)[:, ch * 512:ch * 512 + w],
                                    negm_flat[0:1, gtt * 128:gtt * 128 + 128],
                                    rsum3[0:1, 2 * FPC + ch * 512:2 * FPC + ch * 512 + w],
                                    start=False, stop=True, skip_group_check=True),
                                    inc="mm" if ch == NCH - 1 else None,
                                    rd=[("negm_flat", gtt * 128, gtt * 128 + 128),
                                        ("rsum3", 2 * FPC, 3 * FPC)],
                                    wr=[PS(bk)])
                            mm_fill = cnt["mm"]
                            emit("vector", lambda e, gtt=gtt, bk=bk: e.tensor_scalar(
                                vtok_bf.rearrange("p (h s) -> p h s", h=HPC)[:, :, gtt * 128:gtt * 128 + 128],
                                bank(bk)[:, 0:FPC].rearrange("p (h d) -> p h d", h=HPC),
                                r_tok[:, gtt:gtt + 1], None, ALU.mult),
                                wait=[("mm", mm_fill), ("vv", v_rtok)], inc="vv",
                                rd=[PS(bk), ("r_tok", gtt, gtt + 1)],
                                wr=[("vtok", h * S + gtt * 128, h * S + gtt * 128 + 128)
                                    for h in range(HPC)])
                            bank_free[bk] = cnt["vv"]
                        xbuf_free[b] = cnt["mm"]
                    guards["proj_mm"] = cnt["mm"]
                    guards["proj_vv"] = cnt["vv"]

        def phase2(qT_bf, kT_bf, vtok_bf, l_pack):
            with (
                nc.sbuf_tensor("p_bf", [128, NKT * 512], BF16) as p_bf,
                nc.sbuf_tensor("mask_bf", [128, 4 * 512], BF16) as mask_bf,
                nc.sbuf_tensor("ostg2", [128, 2 * 512], BF16) as ostg2,
            ):
                emit("gpsimd", lambda e: e.dma_start(
                    out=mask_bf.rearrange("p (t j) -> p t j", t=4), in_=maskc_t),
                    wait=[("mm", guards["proj_mm"]), ("vv", guards["proj_vv"])], inc="gm",
                    rd=[("maskc", 0, 1)], wr=[("mask", 0, 4 * 512)])
                gp_mask = ("gm", cnt["gm"])
                o_free = [0]
                l_free = [0]
                stg_free = [0, 0]
                first = [True]
                groups = []
                for qb in range(NQB):
                    ng = (4 * qb + 4) // 2
                    for h in range(HPC):
                        for j in range(ng):
                            groups.append((qb, h, j, ng))
                gstate = {}

                SPAIRS = [0, 2, 6]

                def emit_qk_mask_exp(gi):
                    qb, h, j, ng = groups[gi]
                    pair = SPAIRS[gi % 3]
                    pb = 2048 if (qb == 0 and h % 2 == 1) else 0
                    for t2 in range(2):
                        kt = 2 * j + t2
                        w0 = None
                        if first[0]:
                            w0 = [("vv", guards["proj_vv"]), gp_mask, consts_st]
                            first[0] = False
                        emit("tensor", lambda e, h=h, qb=qb, kt=kt, pair=pair, t2=t2: e.matmul(
                            bank(pair + t2),
                            kT_bf[:, h * S + kt * 128:h * S + kt * 128 + 128],
                            qT_bf[:, h * S + qb * 512:h * S + qb * 512 + 512],
                            start=True, stop=True, skip_group_check=True),
                            wait=w0, inc="mm" if t2 == 1 else None,
                            rd=[("kT", h * S + kt * 128, h * S + kt * 128 + 128),
                                ("qT", h * S + qb * 512, h * S + qb * 512 + 512)],
                            wr=[PS(pair + t2)])
                    mm_qk = cnt["mm"]
                    if 2 * j >= 4 * qb:
                        for t2 in range(2):
                            tloc = 2 * j + t2 - 4 * qb
                            emit("vector", lambda e, pair=pair, t2=t2, tloc=tloc: e.tensor_tensor(
                                bank(pair + t2), bank(pair + t2),
                                mask_bf[:, tloc * 512:tloc * 512 + 512], ALU.add),
                                wait=[("mm", mm_qk)] if t2 == 0 else None,
                                inc="vv" if t2 == 1 else None,
                                rd=[PS(pair + t2), ("mask", tloc * 512, tloc * 512 + 512)],
                                wr=[PS(pair + t2)])
                        wexp = [("vv", cnt["vv"])]
                    else:
                        wexp = [("mm", mm_qk)]
                    emit("scalar", lambda e, pair=pair, j=j, pb=pb: e.activation(
                        p_bf[:, pb + (2 * j) * 512:pb + (2 * j) * 512 + 1024],
                        bank(pair, 2), AF.Exp, bias=expb_sb[:, :], scale=SCALE),
                        wait=wexp, inc="ss",
                        rd=[PS(pair, 2), ("expb_sb", 0, 1)],
                        wr=[("p_bf", pb + 2 * j * 512, pb + 2 * j * 512 + 1024)])
                    gstate[gi] = cnt["ss"]

                def emit_lpv(gi):
                    qb, h, j, ng = groups[gi]
                    pb = 2048 if (qb == 0 and h % 2 == 1) else 0
                    ss_exp = gstate.pop(gi)
                    for t2 in range(2):
                        kt = 2 * j + t2
                        w0 = [("ss", ss_exp)] if t2 == 0 else []
                        if t2 == 0 and j == 0:
                            w0.append(("vv", o_free[0]))
                            if h == 0:
                                w0.append(("vv", l_free[0]))
                        emit("tensor", lambda e, h=h, kt=kt, nkt=2 * ng, pb=pb: e.matmul(
                            ps[0:4, 5 * 512:6 * 512], lsel_bf[:, h * 4:h * 4 + 4],
                            p_bf[:, pb + kt * 512:pb + kt * 512 + 512],
                            start=(h == 0 and kt == 0),
                            stop=(h == HPC - 1 and kt == nkt - 1),
                            skip_group_check=True), wait=w0 or None,
                            rd=[("lsel", h * 4, h * 4 + 4),
                                ("p_bf", pb + kt * 512, pb + kt * 512 + 512)],
                            wr=[PS(5)])
                        emit("tensor", lambda e, h=h, kt=kt, nkt=2 * ng, pb=pb: e.matmul(
                            bank(4), vtok_bf[:, h * S + kt * 128:h * S + kt * 128 + 128],
                            p_bf[:, pb + kt * 512:pb + kt * 512 + 512],
                            start=(kt == 0), stop=(kt == nkt - 1),
                            skip_group_check=True),
                            inc="mm" if kt == 2 * ng - 1 else None,
                            rd=[("vtok", h * S + kt * 128, h * S + kt * 128 + 128),
                                ("p_bf", pb + kt * 512, pb + kt * 512 + 512)],
                            wr=[PS(4)])
                    if j == ng - 1:
                        mm_pv = cnt["mm"]
                        sb = (qb * HPC + h) % 2
                        emit("vector", lambda e, sb=sb: e.tensor_copy(
                            ostg2[:, sb * 512:sb * 512 + 512], bank(4)),
                            wait=[("mm", mm_pv), ("so%d" % sb, stg_free[sb])], inc="vv",
                            rd=[PS(4)], wr=[("ostg2", sb * 512, sb * 512 + 512)])
                        o_free[0] = cnt["vv"]
                        v_stg = cnt["vv"]
                        emit("sync", lambda e, h=h, qb=qb, sb=sb: e.dma_start(
                            out=o_dram.ap()[h * 128:h * 128 + 128, qb * 512:qb * 512 + 512],
                            in_=ostg2[:, sb * 512:sb * 512 + 512]),
                            wait=[("vv", v_stg)], inc="so%d" % sb,
                            rd=[("ostg2", sb * 512, sb * 512 + 512)],
                            wr=[("o_dram", h * NQB + qb, h * NQB + qb + 1)])
                        stg_free[sb] = cnt["so%d" % sb]
                        if h == HPC - 1:
                            emit("vector", lambda e, qb=qb: e.tensor_copy(
                                l_pack[0:HPC, qb * 512:qb * 512 + 512],
                                ps[0:HPC, 5 * 512:6 * 512]),
                                wait=[("mm", cnt["mm"])], inc="vv",
                                rd=[PS(5)], wr=[("l_pack", qb * 512, qb * 512 + 512)])
                            l_free[0] = cnt["vv"]

                LOOK = 2
                for gi in range(len(groups)):
                    emit_qk_mask_exp(gi)
                    if gi >= LOOK:
                        emit_lpv(gi - LOOK)
                for gi in range(len(groups) - LOOK, len(groups)):
                    emit_lpv(gi)
                emit("vector", lambda e: e.reciprocal(l_pack[0:HPC, :], l_pack[0:HPC, :]),
                     inc="vv", rd=[("l_pack", 0, S)], wr=[("l_pack", 0, S)])
                guards["v_il"] = cnt["vv"]
                guards["attn_mm"] = cnt["mm"]
                guards["attn_st0"] = cnt["so0"]
                guards["attn_st1"] = cnt["so1"]
                guards["attn_ss"] = cnt["ss"]

        def phase3a(l_pack):
            with (
                nc.sbuf_tensor("osh_l", [128, HPC * S], BF16) as osh_l,
                nc.sbuf_tensor("wo_bf", [128, HPC * D], BF16) as wo_bf,
                nc.sbuf_tensor("il_h", [1, S], F32) as il_h,
                nc.sbuf_tensor("ilb_sb", [128, 2 * 512], F32) as ilb_sb,
                nc.sbuf_tensor("ostg", [128, 8 * 512], BF16) as ostg,
            ):
                emit("gpsimd", lambda e: e.dma_start(
                    out=wo_bf.rearrange("p (i f) -> p i f", i=NDT), in_=woTmy_t),
                    wait=[("mm", guards["attn_mm"]), ("ss", guards["attn_ss"]),
                          ("vv", guards["v_il"])], inc="gw",
                    rd=[("woTmy", 0, 1)], wr=[("wo_bf", 0, HPC * D)])
                wo_gp = ("gw", cnt["gw"])
                emit("sync", lambda e: e.dma_start(
                    out=osh_l.rearrange("p (h s) -> p h s", h=HPC), in_=o_dram_t),
                    wait=[("so0", guards["attn_st0"]), ("so1", guards["attn_st1"]),
                          ("mm", guards["attn_mm"]),
                          ("ss", guards["attn_ss"]), ("vv", guards["v_il"])], inc="sosh",
                    rd=[("o_dram", 0, HPC * NQB)], wr=[("osh", 0, HPC * S)])
                st_osh = ("sosh", cnt["sosh"])
                ilb_free = [0, 0]
                ilh_mm = 0
                for h in range(HPC):
                    gp_il = ("gil", emit("gpsimd", lambda e, h=h: e.dma_start(
                        out=il_h[0:1, :], in_=l_pack[h:h + 1, :]),
                        wait=[("vv", guards["v_il"]), ("mm", ilh_mm)], inc="gil",
                        rd=[("l_pack", 0, S)], wr=[("il_h", 0, S)]))
                    for qb in range(NQB):
                        sb = (h * NQB + qb) % 2
                        bk = 4 + sb
                        emit("tensor", lambda e, h=h, qb=qb, bk=bk: e.matmul(
                            bank(bk), ones128_sb[:, :],
                            il_h[0:1, qb * 512:qb * 512 + 512],
                            start=True, stop=True, skip_group_check=True),
                            wait=[gp_il, ("vv", ilb_free[sb])], inc="mm",
                            rd=[("ones128_sb", 0, 128), ("il_h", qb * 512, qb * 512 + 512)],
                            wr=[PS(bk)])
                        mm_b = cnt["mm"]
                        emit("vector", lambda e, sb=sb, bk=bk: e.tensor_copy(
                            ilb_sb[:, sb * 512:sb * 512 + 512], bank(bk)),
                            wait=[("mm", mm_b)], inc="vv",
                            rd=[PS(bk)], wr=[("ilb", sb * 512, sb * 512 + 512)])
                        v_cp = cnt["vv"]
                        emit("vector", lambda e, h=h, qb=qb, sb=sb: e.tensor_tensor(
                            osh_l[:, h * S + qb * 512:h * S + qb * 512 + 512],
                            osh_l[:, h * S + qb * 512:h * S + qb * 512 + 512],
                            ilb_sb[:, sb * 512:sb * 512 + 512], ALU.mult),
                            wait=([st_osh] if (h == 0 and qb == 0) else None),
                            inc="vv",
                            rd=[("osh", h * S + qb * 512, h * S + qb * 512 + 512),
                                ("ilb", sb * 512, sb * 512 + 512)],
                            wr=[("osh", h * S + qb * 512, h * S + qb * 512 + 512)])
                        ilb_free[sb] = cnt["vv"]
                    ilh_mm = cnt["mm"]
                onorm_vv = cnt["vv"]

                rs_cc = []
                bank_free2 = {b: 0 for b in range(8)}
                stg_free = {b: ("spe0", 0) for b in range(8)}
                for qt in range(NKT):
                    qp = qt % 2
                    for h in range(NDT):
                        for ec in range(NOC):
                            bk = qp * 4 + ec
                            w0 = None
                            if h == 0 and ec == 0:
                                w0 = [("vv", max(bank_free2[qp * 4 + e2] for e2 in range(NOC)))]
                                if qt == 0:
                                    w0 += [wo_gp, ("vv", onorm_vv)]
                            emit("tensor", lambda e, h=h, qt=qt, ec=ec, bk=bk: e.matmul(
                                bank(bk), osh_l[:, h * S + qt * 128:h * S + qt * 128 + 128],
                                wo_bf[:, h * D + ec * 512:h * D + ec * 512 + 512],
                                start=(h == 0), stop=(h == NDT - 1), skip_group_check=True),
                                wait=w0,
                                inc="mm" if (h == NDT - 1 and ec == NOC - 1) else None,
                                rd=[("osh", h * S + qt * 128, h * S + qt * 128 + 128),
                                    ("wo_bf", h * D + ec * 512, h * D + ec * 512 + 512)],
                                wr=[PS(bk)])
                    mm_fill = cnt["mm"]
                    for ec in range(NOC):
                        bk = qp * 4 + ec
                        emit("vector", lambda e, bk=bk: e.tensor_copy(
                            ostg[:, bk * 512:bk * 512 + 512], bank(bk)),
                            wait=[("mm", mm_fill), stg_free[bk]], inc="vv",
                            rd=[PS(bk)], wr=[("ostg", bk * 512, bk * 512 + 512)])
                        bank_free2[bk] = cnt["vv"]
                        v_stg = cnt["vv"]
                        strm = "spe%d" % ec
                        emit("sync", lambda e, ec=ec, qt=qt, bk=bk: e.dma_start(
                            out=sp[ec].ap()[qt * 128:qt * 128 + 128, :],
                            in_=ostg[:, bk * 512:bk * 512 + 512]),
                            wait=[("vv", v_stg)], inc=strm,
                            rd=[("ostg", bk * 512, bk * 512 + 512)],
                            wr=[(f"sp{ec}", qt * 128, qt * 128 + 128)])
                        stg_free[bk] = (strm, cnt[strm])
                guards["spe_totals"] = [("spe%d" % e2, cnt["spe%d" % e2])
                                        for e2 in range(NOC)]
                guards["rs_cc"] = rs_cc
                guards["oproj_mm"] = cnt["mm"]
                guards["oproj_vv"] = cnt["vv"]

        def phase3bc():
            with nc.sbuf_tensor("g_sb", [128, NQT * D], BF16) as g_sb:
                with (
                    nc.sbuf_tensor("spk", [8, 6 * 512], F32) as spk,
                    nc.sbuf_tensor("xqmy_bf", [128, IT * TPC], BF16) as xqmy_bf,
                    nc.sbuf_tensor("sqm_bf", [128, 2048], BF16) as sqm_bf,
                    nc.sbuf_tensor("wch_bf", [128, 4 * IT * 512], BF16) as wch_bf,
                    nc.sbuf_tensor("gb_sb", [128, D], F32) as gb_sb,
                    nc.sbuf_tensor("bb_sb", [128, D], F32) as bb_sb,
                    nc.sbuf_tensor("gb_row", [1, 2 * D], F32) as gb_row,
                    nc.sbuf_tensor("rsf_bf", [1, D], BF16) as rsf_bf,
                    nc.sbuf_tensor("rmy_flat", [1, TPC], F32) as rmy_flat,
                    nc.sbuf_tensor("nmy_flat", [1, TPC], BF16) as nmy_flat,
                    nc.sbuf_tensor("rtmy", [128, NQT], F32) as rtmy,
                    nc.sbuf_tensor("z_sb", [128, 2 * 512], F32) as z_sb,
                ):
                    spk_box[0] = spk
                    og = guards["oproj_mm"]
                    ogv = guards["oproj_vv"]
                    emit("gpsimd", lambda e: e.dma_start(
                        out=xqmy_bf.rearrange("p (i f) -> p i f", i=IT), in_=xTmy_t),
                        wait=[("mm", og), ("vv", ogv)], inc="gq",
                        rd=[("xTmy", 0, 1)], wr=[("xqmy", 0, IT * TPC)])
                    gp_xmy = ("gq", cnt["gq"])
                    emit("sync", lambda e: e.dma_start(out=gb_row[0:1, 0:D], in_=gam_in[:, :]),
                         wait=[("mm", og)], inc="sgb",
                         rd=[("gam_in", 0, 1)], wr=[("gb_row", 0, D)])
                    emit("sync", lambda e: e.dma_start(out=gb_row[0:1, D:2 * D], in_=bet_in[:, :]),
                         inc="sgb", rd=[("bet_in", 0, 1)], wr=[("gb_row", D, 2 * D)])
                    st_gb = ("sgb", cnt["sgb"])
                    wch_loaded = []
                    for oc in range(NOC):
                        strm = "gwc%d" % oc
                        emit("gpsimd", lambda e, oc=oc: e.dma_start(
                            out=wch_bf.rearrange("p (b i f) -> p b i f", b=4, i=IT)[:, oc],
                            in_=wfT_t[:, :, oc * 512:oc * 512 + 512]),
                            wait=[("mm", og), ("vv", ogv)] if oc == 0 else None,
                            inc=strm,
                            rd=[("wfT", 0, 1)],
                            wr=[("wch", oc * IT * 512, (oc + 1) * IT * 512)])
                        wch_loaded.append((strm, cnt[strm]))

                    def xqm(i):
                        return xqmy_bf[:, i * TPC:(i + 1) * TPC]

                    def wch(i, oc):
                        return wch_bf[:, (oc * IT + i) * 512:(oc * IT + i) * 512 + 512]

                    def WCH(i, oc):
                        o = (oc * IT + i) * 512
                        return ("wch", o, o + 512)

                    xqmy_v = xqmy_bf.rearrange("p (i f) -> p i f", i=IT)

                    r = emit_stats(
                        lambda qb, pm, pss: gp_xmy,
                        lambda q, i: xqm(i)[:, q * 512:q * 512 + 512],
                        lambda q, i: ("xqmy", i * TPC + q * 512, i * TPC + q * 512 + 512),
                        lambda q, c: sqm_bf.rearrange("p (i f) -> p i f", i=4),
                        lambda q, c: xqmy_v[:, 4 * c:4 * c + 4, q * 512:q * 512 + 512],
                        lambda j: sqm_bf[:, j * 512:(j + 1) * 512],
                        lambda j0, j1: ("sqm", j0 * 512, j1 * 512),
                        MYQB, [("vv", guards["qk_vv"]), ("mm", og)])
                    _, _, ss_r2, v_nm2 = r
                    emit("gpsimd", lambda e: e.dma_start(
                        out=rmy_flat[0:1, :].rearrange("a (q j) -> a q j", q=MYQB),
                        in_=spr(4, MYQB)),
                        wait=[("ss", ss_r2)], inc="gf",
                        rd=[SPR(4)], wr=[("rmy", 0, TPC)])
                    emit("gpsimd", lambda e: e.dma_start(
                        out=nmy_flat[0:1, :].rearrange("a (q j) -> a q j", q=MYQB),
                        in_=spr(3, MYQB)),
                        wait=[("vv", v_nm2)], inc="gf",
                        rd=[SPR(3)], wr=[("nmy", 0, TPC)])
                    gp_f2 = ("gf", cnt["gf"])

                    rs_cc = guards["rs_cc"]
                    if upto != "p3p":
                        for ec in range(NOC):
                            w0 = None
                            if ec == 0:
                                w0 = (guards["spe_totals"] +
                                      [gp_xmy, st_gb, gp_f2] + wch_loaded)
                            emit("gpsimd", lambda e, ec=ec: e.collective_compute(
                                "ReduceScatter", ALU.add, replica_groups=RG4,
                                ins=[sp[ec].ap().opt()], outs=[sm[ec].ap().opt()]),
                                wait=w0, inc="cc",
                                rd=[(f"sp{ec}", 0, S)], wr=[(f"sm{ec}", 0, TPC)])
                            rs_cc.append(cnt["cc"])
                    else:
                        rs_cc.extend([0] * NOC)
                    cc_all = rs_cc[-1]

                    for tt in range(NQT):
                        emit("tensor", lambda e, tt=tt: e.matmul(
                            ps[:, 4 * 512 + tt:4 * 512 + tt + 1],
                            rmy_flat[0:1, tt * 128:tt * 128 + 128], onesc_sb[:, :],
                            start=True, stop=True, skip_group_check=True),
                            wait=[gp_f2] if tt == 0 else None,
                            inc="mm" if tt == NQT - 1 else None,
                            rd=[("rmy", tt * 128, tt * 128 + 128), ("onesc_sb", 0, 1)],
                            wr=[PS(4)])
                    mm_rt2 = cnt["mm"]
                    emit("vector", lambda e: e.tensor_copy(rtmy[:, 0:NQT], ps[:, 4 * 512:4 * 512 + NQT]),
                         wait=[("mm", mm_rt2)], inc="vv",
                         rd=[PS(4)], wr=[("rtmy", 0, NQT)])
                    v_rt2 = cnt["vv"]

                    gbfree = [0, 0]
                    for kk, dst, dstN in ((0, gb_sb, "gb"), (1, bb_sb, "bb")):
                        for ocq in range(NOC):
                            sb6 = (kk * NOC + ocq) % 2
                            emit("tensor", lambda e, kk=kk, ocq=ocq, sb6=sb6: e.matmul(
                                bank(6 + sb6), ones128_sb[:, :],
                                gb_row[0:1, kk * D + ocq * 512:kk * D + ocq * 512 + 512],
                                start=True, stop=True, skip_group_check=True),
                                wait=[st_gb, ("vv", max(gbfree[sb6], ogv))], inc="mm",
                                rd=[("ones128_sb", 0, 128),
                                    ("gb_row", kk * D + ocq * 512, kk * D + ocq * 512 + 512)],
                                wr=[PS(6 + sb6)])
                            mm_g = cnt["mm"]
                            emit("vector", lambda e, ocq=ocq, sb6=sb6, dst=dst: e.tensor_copy(
                                dst[:, ocq * 512:ocq * 512 + 512], bank(6 + sb6)),
                                wait=[("mm", mm_g)], inc="vv",
                                rd=[PS(6 + sb6)],
                                wr=[(dstN, ocq * 512, ocq * 512 + 512)])
                            gbfree[sb6] = cnt["vv"]
                    v_gb = cnt["vv"]

                    bank_free3 = {0: 0, 1: 0}
                    z_free = [0, 0]
                    rsf_free = 0
                    for oc in range(NOC):
                        for i in range(IT):
                            w0 = None
                            if i == 0:
                                w0 = [wch_loaded[oc], ("vv", rsf_free)]
                            emit("tensor", lambda e, i=i, oc=oc: e.matmul(
                                ps[0:1, 5 * 512:6 * 512], onesb_bf[:, :], wch(i, oc),
                                start=(i == 0), stop=(i == IT - 1), skip_group_check=True),
                                wait=w0, inc="mm" if i == IT - 1 else None,
                                rd=[("onesb_bf", 0, 1), WCH(i, oc)], wr=[PS(5)])
                        mm_rsf = cnt["mm"]
                        emit("vector", lambda e, oc=oc: e.tensor_copy(
                            rsf_bf[0:1, oc * 512:oc * 512 + 512], ps[0:1, 5 * 512:6 * 512]),
                            wait=[("mm", mm_rsf)], inc="vv",
                            rd=[PS(5)], wr=[("rsf", oc * 512, oc * 512 + 512)])
                        v_rsf = cnt["vv"]
                        rsf_free = v_rsf
                        for qt in range(NQT):
                            bk = qt % 2
                            for i in range(IT):
                                w0 = [("vv", max(bank_free3[bk], v_rsf))] if i == 0 else None
                                emit("tensor", lambda e, i=i, qt=qt, oc=oc, bk=bk: e.matmul(
                                    bank(bk), xqm(i)[:, qt * 128:qt * 128 + 128], wch(i, oc),
                                    start=(i == 0), stop=False, skip_group_check=True),
                                    wait=w0,
                                    rd=[("xqmy", i * TPC + qt * 128, i * TPC + qt * 128 + 128),
                                        WCH(i, oc)],
                                    wr=[PS(bk)])
                            emit("tensor", lambda e, qt=qt, oc=oc, bk=bk: e.matmul(
                                bank(bk), nmy_flat[0:1, qt * 128:qt * 128 + 128],
                                rsf_bf[0:1, oc * 512:oc * 512 + 512],
                                start=False, stop=True, skip_group_check=True), inc="mm",
                                rd=[("nmy", qt * 128, qt * 128 + 128),
                                    ("rsf", oc * 512, oc * 512 + 512)],
                                wr=[PS(bk)])
                            mm_fill = cnt["mm"]
                            sbz = qt % 2
                            emit("vector", lambda e, qt=qt, oc=oc, bk=bk, sbz=sbz: e.scalar_tensor_tensor(
                                z_sb[:, sbz * 512:sbz * 512 + 512], bank(bk), rtmy[:, qt:qt + 1],
                                gb_sb[:, oc * 512:oc * 512 + 512], ALU.mult, ALU.mult),
                                wait=[("mm", mm_fill), ("vv", max(v_gb, v_rt2)),
                                      ("ss", z_free[sbz])], inc="vv",
                                rd=[PS(bk), ("rtmy", qt, qt + 1), ("gb", oc * 512, oc * 512 + 512)],
                                wr=[("z", sbz * 512, sbz * 512 + 512)])
                            bank_free3[bk] = cnt["vv"]
                            emit("vector", lambda e, oc=oc, sbz=sbz: e.tensor_tensor(
                                z_sb[:, sbz * 512:sbz * 512 + 512], z_sb[:, sbz * 512:sbz * 512 + 512],
                                bb_sb[:, oc * 512:oc * 512 + 512], ALU.add), inc="vv",
                                rd=[("z", sbz * 512, sbz * 512 + 512), ("bb", oc * 512, oc * 512 + 512)],
                                wr=[("z", sbz * 512, sbz * 512 + 512)])
                            v_z = cnt["vv"]
                            emit("scalar", lambda e, qt=qt, oc=oc, sbz=sbz: e.activation(
                                g_sb[:, qt * D + oc * 512:qt * D + oc * 512 + 512],
                                z_sb[:, sbz * 512:sbz * 512 + 512], AF.Silu),
                                wait=[("vv", v_z)], inc="ss",
                                rd=[("z", sbz * 512, sbz * 512 + 512)],
                                wr=[("g", qt * D + oc * 512, qt * D + oc * 512 + 512)])
                            z_free[sbz] = cnt["ss"]
                    guards["ffn_mm"] = cnt["mm"]
                    guards["ffn_ss"] = cnt["ss"]
                    guards["ffn_vv"] = cnt["vv"]
                    guards["ffn_gwc0"] = cnt["gwc0"]
                    guards["ffn_gwc1"] = cnt["gwc1"]
                    if guards.get("dump_g"):
                        dump([(g_sb[:, :], "g", 0)])

                with (
                    nc.sbuf_tensor("stile", [128, 2 * 512], BF16) as stile,
                    nc.sbuf_tensor("xtok_sb", [128, 2 * D], F32) as xtok_sb,
                    nc.sbuf_tensor("osb", [128, 2 * D], F32) as osb,
                ):
                    rs_cc = guards["rs_cc"]
                    s_free = [0, 0]
                    x_free = [0, 0]
                    o_store = [0, 0]
                    fg = guards["ffn_mm"]
                    for qt in range(NQT):
                        xb = qt % 2
                        w0 = [("vv", x_free[xb])]
                        if qt <= 1:
                            w0 += [("mm", fg), ("ss", guards["ffn_ss"]),
                                   ("vv", guards["ffn_vv"]),
                                   ("gwc0", guards["ffn_gwc0"]),
                                   ("gwc1", guards["ffn_gwc1"])]
                        emit("sync", lambda e, qt=qt, xb=xb: e.dma_start(
                            out=xtok_sb[:, xb * D:(xb + 1) * D],
                            in_=x_tok[qt * 128:qt * 128 + 128, :]),
                            wait=w0, inc="sxt%d" % xb,
                            rd=[("x_tok", 0, 1)], wr=[("xtok", xb * D, (xb + 1) * D)])
                        st_x = ("sxt%d" % xb, cnt["sxt%d" % xb])
                        for ec in range(NOC):
                            sb = (qt * NOC + ec) % 2
                            emit("sync", lambda e, qt=qt, ec=ec, sb=sb: e.dma_start(
                                out=stile[:, sb * 512:sb * 512 + 512],
                                in_=sm[ec].ap()[qt * 128:qt * 128 + 128, :]),
                                wait=[("cc", rs_cc[ec]), ("vv", s_free[sb])], inc="sst%d" % sb,
                                rd=[(f"sm{ec}", qt * 128, qt * 128 + 128)],
                                wr=[("stile", sb * 512, sb * 512 + 512)])
                            st_s = ("sst%d" % sb, cnt["sst%d" % sb])
                            emit("vector", lambda e, qt=qt, ec=ec, sb=sb, xb=xb: e.tensor_tensor(
                                osb[:, xb * D + ec * 512:xb * D + ec * 512 + 512],
                                stile[:, sb * 512:sb * 512 + 512],
                                g_sb[:, qt * D + ec * 512:qt * D + ec * 512 + 512], ALU.mult),
                                wait=[st_s,
                                      ("ss", guards["ffn_ss"] if (qt == 0 and ec == 0) else 0),
                                      ("sou%d" % xb, o_store[xb] if ec == 0 else 0)],
                                inc="vv",
                                rd=[("stile", sb * 512, sb * 512 + 512),
                                    ("g", qt * D + ec * 512, qt * D + ec * 512 + 512)],
                                wr=[("osb", xb * D + ec * 512, xb * D + ec * 512 + 512)])
                            s_free[sb] = cnt["vv"]
                            emit("vector", lambda e, qt=qt, ec=ec, xb=xb: e.tensor_tensor(
                                osb[:, xb * D + ec * 512:xb * D + ec * 512 + 512],
                                osb[:, xb * D + ec * 512:xb * D + ec * 512 + 512],
                                xtok_sb[:, xb * D + ec * 512:xb * D + ec * 512 + 512], ALU.add),
                                wait=[st_x] if ec == 0 else None, inc="vv",
                                rd=[("osb", xb * D + ec * 512, xb * D + ec * 512 + 512),
                                    ("xtok", xb * D + ec * 512, xb * D + ec * 512 + 512)],
                                wr=[("osb", xb * D + ec * 512, xb * D + ec * 512 + 512)])
                        v_row = cnt["vv"]
                        x_free[xb] = v_row
                        emit("sync", lambda e, qt=qt, xb=xb: e.dma_start(
                            out=out[qt * 128:qt * 128 + 128, :], in_=osb[:, xb * D:(xb + 1) * D]),
                            wait=[("vv", v_row)], inc="sou%d" % xb,
                            rd=[("osb", xb * D, (xb + 1) * D)],
                            wr=[("out", qt * 128, qt * 128 + 128)])
                        o_store[xb] = cnt["sou%d" % xb]
                    f0, f1 = cnt["sou0"], cnt["sou1"]
                    plan["sync"].append(lambda e, a=f0, b=f1: (
                        e.wait_ge(SEM["sou0"], a), e.wait_ge(SEM["sou1"], b)))

        def dump(tiles):
            for ap_, nm, off in tiles:
                emit("gpsimd", lambda e, ap_=ap_, off=off: e.dma_start(
                    out=dbg[0:ap_.shape[0], off:off + ap_.shape[-1]], in_=ap_),
                    wait=[("mm", cnt["mm"]), ("ss", cnt["ss"]),
                          ("cc", cnt["cc"]), ("vv", cnt["vv"])], inc="gd",
                    rd=[(nm, 0, 10 ** 9)], wr=[("dbg", off, off + ap_.shape[-1])])

        def debug_tail():
            with nc.sbuf_tensor("dbg_sb", [128, D], F32) as dbg_sb:
                emit("vector", lambda e: e.memset(dbg_sb[:, :], 1.0),
                     wait=[("mm", cnt["mm"]), ("ss", cnt["ss"]),
                           ("cc", cnt["cc"]), ("vv", cnt["vv"])],
                     inc="vv", wr=[("dbg_sb", 0, D)])
                vd = cnt["vv"]
                for qt in range(NQT):
                    emit("sync", lambda e, qt=qt: e.dma_start(
                        out=out[qt * 128:qt * 128 + 128, :], in_=dbg_sb[:, :]),
                        wait=[("vv", vd)], inc="sou0",
                        rd=[("dbg_sb", 0, D)], wr=[("out", qt * 128, qt * 128 + 128)])
                fs = cnt["sou0"]
                plan["sync"].append(lambda e, v=fs: e.wait_ge(SEM["sou0"], v))

        with (
            nc.sbuf_tensor("qT_bf", [128, HPC * S], BF16) as qT_bf,
            nc.sbuf_tensor("kT_bf", [128, HPC * S], BF16) as kT_bf,
            nc.sbuf_tensor("vtok_bf", [128, HPC * S], BF16) as vtok_bf,
        ):
            with (
                nc.sbuf_tensor("r_flat", [1, S], F32) as r_flat,
                nc.sbuf_tensor("negm_flat", [1, S], BF16) as negm_flat,
                nc.sbuf_tensor("rsum3", [1, 3 * FPC], BF16) as rsum3,
            ):
                with nc.sbuf_tensor("wqk_bf", [128, 2 * IT * FPC], BF16) as wqk_bf:
                    guards["wqk_bf"] = wqk_bf
                    wv_ = wqk_bf.rearrange("p (k i f) -> p k i f", k=2, i=IT)
                    emit("gpsimd", lambda e: e.dma_start(out=wv_[:, 0], in_=wqT_t),
                         wait=[consts_gp], inc="gw",
                         rd=[("wqT", 0, 1)], wr=[("wqk", 0, IT * FPC)])
                    emit("gpsimd", lambda e: e.dma_start(out=wv_[:, 1], in_=wkT_t), inc="gw",
                         rd=[("wkT", 0, 1)], wr=[("wqk", IT * FPC, 2 * IT * FPC)])
                    guards["wqk_gp"] = ("gw", cnt["gw"])
                    phase1_merged(qT_bf, kT_bf, r_flat, negm_flat, rsum3)
                phase1_v(vtok_bf, r_flat, negm_flat, rsum3)
            if upto == "p1p":
                dump([(qT_bf[:, :], "qT", 0), (kT_bf[:, :], "kT", HPC * S),
                      (vtok_bf[:, :], "vtok", 2 * HPC * S)])
            if upto not in ("p1s", "p1p"):
                with nc.sbuf_tensor("l_pack", [HPC, S], F32) as l_pack:
                    phase2(qT_bf, kT_bf, vtok_bf, l_pack)
                    if upto == "p2":
                        dump([(l_pack[0:HPC, :], "l_pack", 0)])
                    if upto != "p2":
                        phase3a(l_pack)
                        if upto == "p3s":
                            for c in range(NOC):
                                emit("gpsimd", lambda e, c=c: e.dma_start(
                                    out=dbg2[:, c * 512:(c + 1) * 512],
                                    in_=sm[c].ap()[:, :]),
                                    wait=[("cc", guards["rs_cc"][c])], inc="gd",
                                    rd=[(f"sm{c}", 0, TPC)], wr=[("dbg2", c, c + 1)])
        if upto is None or upto == "p3g":
            guards["dump_g"] = (upto == "p3g")
            phase3bc()
        if upto is not None and upto != "p3g":
            debug_tail()

        if check:
            reports = check_races()
            if reports:
                lines = [f"RACE {a}->{b} on {nm}: {ra} vs {rb}"
                         for (a, b, nm, ra, rb) in reports]
                raise AssertionError("race check failed:\n" + "\n".join(lines))

        with nc.Block() as block:
            @block.tensor
            def _(eng):
                for f in plan["tensor"]:
                    f(eng)

            @block.vector
            def _(eng):
                for f in plan["vector"]:
                    f(eng)

            @block.scalar
            def _(eng):
                for f in plan["scalar"]:
                    f(eng)

            @block.sync
            def _(eng):
                for f in plan["sync"]:
                    f(eng)

            @block.gpsimd
            def _(eng):
                for f in plan["gpsimd"]:
                    f(eng)

        _semstack.close()

    return nc



def make_inputs(x, q_proj, k_proj, v_proj, o_proj, ffn_w, ffn_scale, ffn_bias, HPC):
    B, S, D = x.shape
    FPC = HPC * 128
    TPC = S // 4
    ones128 = np.ones((1, 128), np.float32)
    onesc = np.ones((1, 1), np.float32)
    onesb = np.ones((128, 1), np.float32)
    expb_a = np.full((128, 1), EXP_SHIFT, np.float32)
    qsel = np.zeros((128, 64), np.float32)
    for qb in range(8):
        qsel[:, qb * 8 + qb] = 1.0
    lsel = np.zeros((128, HPC * 4), np.float32)
    for h in range(HPC):
        lsel[:, h * 4 + h] = 1.0
    maskc = np.zeros((4 * 128, 512), np.float32)
    for t in range(4):
        for p in range(128):
            maskc[t * 128 + p, :t * 128 + p] = NEG_BIG
    wqT = np.ascontiguousarray(q_proj.T)
    wkT = np.ascontiguousarray(k_proj.T)
    wvT = np.ascontiguousarray(v_proj.T)
    woT = np.ascontiguousarray(o_proj.T)
    wfT = np.ascontiguousarray(ffn_w.T)
    gam = np.ascontiguousarray(ffn_scale.reshape(1, D).astype(np.float32))
    bet = np.ascontiguousarray(ffn_bias.reshape(1, D).astype(np.float32))
    maps = []
    for core in range(8):
        b, g = core // 4, core % 4
        xb = x[b]
        xTb = np.ascontiguousarray(xb.T)
        maps.append({
            "xT": xTb,
            "xTmy": np.ascontiguousarray(xTb[:, g * TPC:(g + 1) * TPC]),
            "x_tok": np.ascontiguousarray(xb[g * TPC:(g + 1) * TPC]),
            "wqT": np.ascontiguousarray(wqT[:, g * FPC:(g + 1) * FPC]),
            "wkT": np.ascontiguousarray(wkT[:, g * FPC:(g + 1) * FPC]),
            "wvT": np.ascontiguousarray(wvT[:, g * FPC:(g + 1) * FPC]),
            "woTmy": np.ascontiguousarray(woT[g * FPC:(g + 1) * FPC, :]),
            "wfT": wfT,
            "gam_in": gam,
            "bet_in": bet,
            "ones128": ones128,
            "onesc": onesc,
            "onesb": onesb,
            "expb": expb_a,
            "qsel": qsel,
            "lsel": lsel,
            "maskc": maskc,
        })
    return maps


def assemble(results, B, S, D):
    TPC = S // 4
    full = np.empty((B, S, D), np.float32)
    for core in range(8):
        b, g = core // 4, core % 4
        full[b, g * TPC:(g + 1) * TPC] = results[core]["out"]
    return full



def kernel(L0_attn_in_features, L0_attn_q_proj, L0_attn_k_proj, L0_attn_v_proj,
           L0_attn_o_proj, L0_ffn_weight, L0_ffn_scale_weight, L0_ffn_bias):
    x = np.asarray(L0_attn_in_features)
    B, S, D = x.shape
    HPC = (D // 128) // 4
    nc = build_nc(D, S, HPC)
    maps = make_inputs(x, np.asarray(L0_attn_q_proj), np.asarray(L0_attn_k_proj),
                       np.asarray(L0_attn_v_proj), np.asarray(L0_attn_o_proj),
                       np.asarray(L0_ffn_weight), np.asarray(L0_ffn_scale_weight),
                       np.asarray(L0_ffn_bias), HPC)
    res = run_bass_kernel_spmd(nc, maps, core_ids=list(range(8)))
    return assemble(res.results, B, S, D)
